# version 03
# baseline (speedup 1.0000x reference)
import numpy as np
import ml_dtypes
from contextlib import ExitStack

import concourse.bass as bass
import concourse.mybir as mybir
from concourse.bass_utils import run_bass_kernel_spmd
from concourse.bass_types import AP

F32 = mybir.dt.float32
BF16 = mybir.dt.bfloat16
I32 = mybir.dt.int32
AF = mybir.ActivationFunctionType
ALU = mybir.AluOpType
AX = mybir.AxisListType
BF = ml_dtypes.bfloat16

NC = 8
D = 4096
T = 1024
NS = 1024
VB = 256
BOT = 64
RD = 48
HID4 = 1024
KS = D // NC

class _Chain:

    def __init__(self, eng, sem):
        self.eng = eng
        self.sem = sem
        self.n = 0

    def step(self, instr):
        instr.then_inc(self.sem, 1)
        self.n += 1
        self.eng.wait_ge(self.sem, self.n)
        return instr


RSQRT_RD = 1.0 / np.sqrt(48.0)
NEGBIG = -1e30


def _consts():
    idt = np.eye(128, dtype=BF)
    e4c = np.zeros((128, 4), dtype=BF)
    for q in range(4):
        e4c[q * 32:(q + 1) * 32, q] = 1
    place = np.zeros((4, 4096), dtype=np.float32)
    for t in range(32):
        for q in range(4):
            place[q, 128 * t + 4 * t + q] = 1
    ks16 = np.zeros((16, 512), dtype=BF)
    for j in range(4):
        for i in range(128):
            ks16[4 * j + i // 32, 128 * j + i] = 1
    return idt, e4c, place, ks16


def build_l0():
    nc = bass.Bass("TRN2", target_bir_lowering=False, debug=False)
    m_cols = nc.dram_tensor("m_cols", [128, 32, 512], I32, kind="ExternalInput").ap()
    w_cols = nc.dram_tensor("w_cols", [128, 32, 512], F32, kind="ExternalInput").ap()
    v_rows = nc.dram_tensor("v_rows", [128, 4096], F32, kind="ExternalInput").ap()
    wvd = nc.dram_tensor("wvd", [128, 2, 4096], F32, kind="ExternalInput").ap()
    wu1_cols = nc.dram_tensor("wu1_cols", [128, 8, 512], F32, kind="ExternalInput").ap()
    wvu_rows = nc.dram_tensor("wvu_rows", [128, 4, 256], F32, kind="ExternalInput").ap()
    wq_cols = nc.dram_tensor("wq_cols", [64, 512], F32, kind="ExternalInput").ap()
    keys_rows = nc.dram_tensor("keys_rows", [128, 64], F32, kind="ExternalInput").ap()
    wr = nc.dram_tensor("wr", [48, 64], F32, kind="ExternalInput").ap()
    idt = nc.dram_tensor("idt", [128, 128], BF16, kind="ExternalInput").ap()
    e4c = nc.dram_tensor("e4c", [128, 4], BF16, kind="ExternalInput").ap()
    place = nc.dram_tensor("place", [4, 4096], F32, kind="ExternalInput").ap()
    ks16 = nc.dram_tensor("ks16", [16, 512], BF16, kind="ExternalInput").ap()
    o_wtm = nc.dram_tensor("o_wtm", [512, 4096], BF16, kind="ExternalOutput").ap()
    o_vd = nc.dram_tensor("o_vd", [128, 256], BF16, kind="ExternalOutput").ap()
    o_wu1t = nc.dram_tensor("o_wu1t", [512, 1024], BF16, kind="ExternalOutput").ap()
    o_wvut = nc.dram_tensor("o_wvut", [256, 512], BF16, kind="ExternalOutput").ap()
    o_wqt = nc.dram_tensor("o_wqt", [512, 64], BF16, kind="ExternalOutput").ap()
    o_keyst = nc.dram_tensor("o_keyst", [64, 128], BF16, kind="ExternalOutput").ap()
    o_rk = nc.dram_tensor("o_rk", [48, 128], BF16, kind="ExternalOutput").ap()
    o_wrt = nc.dram_tensor("o_wrt", [64, 48], BF16, kind="ExternalOutput").ap()

    with ExitStack() as ctx:
        e = ctx.enter_context
        s_mask = e(nc.sbuf_tensor("s_mask", [128, 32 * 512], BF16))
        s_w = e(nc.sbuf_tensor("s_w", [128, 32 * 512], BF16))
        s_v = e(nc.sbuf_tensor("s_v", [128, 4096], BF16))
        s_wvd = e(nc.sbuf_tensor("s_wvd", [128, 2 * 4096], BF16))
        s_wu1 = e(nc.sbuf_tensor("s_wu1", [128, 8 * 512], BF16))
        s_wvu = e(nc.sbuf_tensor("s_wvu", [128, 4 * 256], BF16))
        s_wq = e(nc.sbuf_tensor("s_wq", [64, 512], BF16))
        s_keys = e(nc.sbuf_tensor("s_keys", [128, 64], BF16))
        s_wr = e(nc.sbuf_tensor("s_wr", [48, 64], BF16))
        s_idt = e(nc.sbuf_tensor("s_idt", [128, 128], BF16))
        s_e4c = e(nc.sbuf_tensor("s_e4c", [128, 4], BF16))
        s_place = e(nc.sbuf_tensor("s_place", [4, 4096], F32))
        s_ks16 = e(nc.sbuf_tensor("s_ks16", [16, 512], BF16))

        s2 = e(nc.sbuf_tensor("s2", [4, 2 * 16], F32))
        bm01 = e(nc.sbuf_tensor("bm01", [128, 16], BF16))
        bmt = e(nc.sbuf_tensor("bmt", [16, 128], BF16))
        bmte = e(nc.sbuf_tensor("bmte", [128, 512], BF16))
        wtm = e(nc.sbuf_tensor("wtm", [128, 4 * 4096], BF16))
        vt = e(nc.sbuf_tensor("vt", [128, 32 * 128], BF16))
        wvdt = e(nc.sbuf_tensor("wvdt", [128, 32 * 256], BF16))
        s_vd = e(nc.sbuf_tensor("s_vd", [128, 256], BF16))
        wu1t = e(nc.sbuf_tensor("wu1t", [128, 4 * 1024], BF16))
        wvut = e(nc.sbuf_tensor("wvut", [128, 2 * 512], BF16))
        wqt = e(nc.sbuf_tensor("wqt", [128, 4 * 64], BF16))
        keyst = e(nc.sbuf_tensor("keyst", [64, 128], BF16))
        keystu = e(nc.sbuf_tensor("keystu", [64, 128], BF16))
        s_rk = e(nc.sbuf_tensor("s_rk", [48, 128], BF16))
        s_wrt = e(nc.sbuf_tensor("s_wrt", [64, 48], BF16))

        P = [e(nc.psum_tensor(f"P{i}", [128, 512], F32)) for i in range(8)]

        ld = e(nc.semaphore("ld"))
        tb = e(nc.semaphore("tb"))
        tp = e(nc.semaphore("tp"))
        vb = e(nc.semaphore("vb"))
        tw = e(nc.semaphore("tw"))
        ve = e(nc.semaphore("ve"))
        sb = e(nc.semaphore("sb"))
        tx = e(nc.semaphore("tx"))
        so = e(nc.semaphore("so"))
        block = e(nc.Block())

        NLD = 13

        @block.gpsimd
        def _(g):
            g.dma_start(out=s_mask[:].rearrange("p (t k) -> p t k", k=512),
                        in_=m_cols).then_inc(ld, 16)
            g.dma_start(out=s_w[:].rearrange("p (t k) -> p t k", k=512),
                        in_=w_cols).then_inc(ld, 16)
            g.dma_start(out=s_v[:], in_=v_rows).then_inc(ld, 16)
            g.dma_start(out=s_wvd[:].rearrange("p (t k) -> p t k", k=4096),
                        in_=wvd).then_inc(ld, 16)
            g.dma_start(out=s_wu1[:].rearrange("p (t k) -> p t k", k=512),
                        in_=wu1_cols).then_inc(ld, 16)
            g.dma_start(out=s_wvu[:].rearrange("p (t k) -> p t k", k=256),
                        in_=wvu_rows).then_inc(ld, 16)
            g.dma_start(out=s_wq[:], in_=wq_cols).then_inc(ld, 16)
            g.dma_start(out=s_keys[:], in_=keys_rows).then_inc(ld, 16)
            g.dma_start(out=s_wr[:], in_=wr).then_inc(ld, 16)
            g.dma_start(out=s_idt[:], in_=idt).then_inc(ld, 16)
            g.dma_start(out=s_e4c[:], in_=e4c).then_inc(ld, 16)
            g.dma_start(out=s_place[:], in_=place).then_inc(ld, 16)
            g.dma_start(out=s_ks16[:], in_=ks16).then_inc(ld, 16)

        @block.tensor
        def _(t):
            t.wait_ge(ld, 16 * NLD)
            for i in range(32):
                if i >= 2:
                    t.wait_ge(vb, i - 1)
                t.matmul(out=P[i % 2][0:4, :], lhsT=s_e4c[:],
                         rhs=s_mask[:, 512 * i:512 * (i + 1)],
                         start=True, stop=True).then_inc(tb, 1)
                if i >= 1:
                    t.wait_ge(vb, i)
                    t.matmul(out=P[2][:, 0:16],
                             lhsT=s_place[:, 128 * (i - 1):128 * i],
                             rhs=s2[0:4, 16 * ((i - 1) % 2):16 * ((i - 1) % 2) + 16],
                             start=(i - 1 == 0), stop=False).then_inc(tp, 1)
            t.wait_ge(vb, 32)
            t.matmul(out=P[2][:, 0:16], lhsT=s_place[:, 128 * 31:128 * 32],
                     rhs=s2[0:4, 16 * (31 % 2):16 * (31 % 2) + 16],
                     start=False, stop=True).then_inc(tp, 1)
            t.wait_ge(vb, 33)
            t.transpose(out=P[3][:].bitcast(BF16)[0:16, 0:128], in_=bm01[:],
                        identity=s_idt[:]).then_inc(tx, 1)
            t.wait_ge(sb, 1)
            for j in range(4):
                mm = t.matmul(out=P[4][:, 128 * j:128 * (j + 1)],
                              lhsT=s_ks16[:, 128 * j:128 * (j + 1)], rhs=bmt[:],
                              start=True, stop=True)
            mm.then_inc(tx, 1)
            t.wait_ge(sb, 2)
            for i in range(32):
                if i >= 2:
                    t.wait_ge(ve, 4 * (i - 1))
                for j in range(4):
                    mm = t.transpose(out=P[i % 2][:].bitcast(BF16)[:, 128 * j:128 * (j + 1)],
                                in_=s_w[:, 512 * i + 128 * j:512 * i + 128 * (j + 1)],
                                identity=s_idt[:])
                mm.then_inc(tw, 1)
            t.wait_ge(ve, 128)
            for m in range(32):
                if m >= 2:
                    t.wait_ge(sb, 2 + (m - 1))
                t.transpose(out=P[m % 2][:].bitcast(BF16)[:, 0:128],
                            in_=s_v[:, 128 * m:128 * (m + 1)], identity=s_idt[:])
                for v in range(2):
                    mm = t.transpose(out=P[m % 2][:].bitcast(BF16)[:, 128 + 128 * v:256 + 128 * v],
                                in_=s_wvd[:, 4096 * v + 128 * m:4096 * v + 128 * (m + 1)],
                                identity=s_idt[:])
                mm.then_inc(tw, 1)
            t.wait_ge(sb, 2 + 32)
            for m in range(32):
                mm = t.matmul(out=P[5][:, 0:256], lhsT=vt[:, 128 * m:128 * (m + 1)],
                         rhs=wvdt[:, 256 * m:256 * (m + 1)],
                         start=(m == 0), stop=(m == 31))
            mm.then_inc(tx, 1)
            cnt = 0
            for j in range(4):
                for tt in range(8):
                    slot = cnt % 8
                    if cnt >= 8 and cnt % 8 == 0:
                        t.wait_ge(sb, 2 + 32 + cnt // 8)
                    mm = t.transpose(out=P[6][:].bitcast(BF16)[:, 128 * slot:128 * (slot + 1)],
                                in_=s_wu1[:, 512 * tt + 128 * j:512 * tt + 128 * (j + 1)],
                                identity=s_idt[:])
                    cnt += 1
                    if cnt % 8 == 0:
                        mm.then_inc(tx, 1)
            for v in range(2):
                for tt in range(4):
                    mm = t.transpose(out=P[7][:].bitcast(BF16)[:, 128 * (4 * v + tt):128 * (4 * v + tt + 1)],
                                in_=s_wvu[:, 256 * tt + 128 * v:256 * tt + 128 * (v + 1)],
                                identity=s_idt[:])
            mm.then_inc(tx, 1)
            for tt in range(4):
                t.transpose(out=P[3][:].bitcast(BF16)[:, 64 * tt:64 * (tt + 1)],
                            in_=s_wq[0:64, 128 * tt:128 * (tt + 1)],
                            identity=s_idt[0:64, 0:64])
            t.transpose(out=P[3][:].bitcast(BF16)[0:64, 256:384], in_=s_keys[:],
                        identity=s_idt[:])
            mm = t.transpose(out=P[3][:].bitcast(BF16)[0:64, 384:432], in_=s_wr[0:48, 0:64],
                        identity=s_idt[0:48, 0:48])
            mm.then_inc(tx, 1)
            t.wait_ge(sb, 2 + 32 + 4 + 1 + 1 + 1)
            t.matmul(out=P[3][0:48, 256:384], lhsT=s_wrt[0:64, 0:48], rhs=keystu[:],
                     start=True, stop=True).then_inc(tx, 1)

        @block.vector
        def _(v):
            for i in range(32):
                v.wait_ge(tb, i + 1)
                if i >= 2:
                    v.wait_ge(tp, i - 1)
                v.tensor_reduce(
                    out=s2[0:4, 16 * (i % 2):16 * (i % 2) + 16],
                    in_=P[i % 2][0:4, :].rearrange("p (a b) -> p a b", b=32),
                    axis=AX.X, op=ALU.add).then_inc(vb, 1)
            v.wait_ge(tp, 32)
            v.tensor_scalar(out=bm01[:], in0=P[2][:, 0:16], scalar1=0.0, scalar2=None,
                            op0=ALU.is_gt).then_inc(vb, 1)
            for i in range(32):
                v.wait_ge(tw, i + 1)
                for j in range(4):
                    mv = bmte[:, 128 * j + 4 * i:128 * j + 4 * i + 4]
                    v.tensor_tensor(
                        out=wtm[:, 4096 * j + 128 * i:4096 * j + 128 * (i + 1)]
                            .rearrange("p (a b) -> p a b", b=32),
                        in0=P[i % 2][:].bitcast(BF16)[:, 128 * j:128 * (j + 1)]
                            .rearrange("p (a b) -> p a b", b=32),
                        in1=mv.unsqueeze(2).broadcast_to([128, 4, 32]),
                        op=ALU.mult).then_inc(ve, 1)

        @block.scalar
        def _(s):
            s.wait_ge(tx, 1)
            s.copy(out=bmt[:], in_=P[3][:].bitcast(BF16)[0:16, 0:128]).then_inc(sb, 1)
            s.wait_ge(tx, 2)
            s.copy(out=bmte[:], in_=P[4][:, 0:512]).then_inc(sb, 1)
            for m in range(32):
                s.wait_ge(tw, 32 + m + 1)
                s.copy(out=vt[:, 128 * m:128 * (m + 1)],
                       in_=P[m % 2][:].bitcast(BF16)[:, 0:128])
                s.copy(out=wvdt[:, 256 * m:256 * (m + 1)],
                       in_=P[m % 2][:].bitcast(BF16)[:, 128:384]).then_inc(sb, 1)
            s.wait_ge(tx, 3)
            s.copy(out=s_vd[:], in_=P[5][:, 0:256])
            for b in range(4):
                s.wait_ge(tx, 4 + b)
                s.copy(out=wu1t[:, 1024 * b:1024 * (b + 1)],
                       in_=P[6][:].bitcast(BF16)[:, 0:1024]).then_inc(sb, 1)
            s.wait_ge(tx, 8)
            s.copy(out=wvut[:], in_=P[7][:].bitcast(BF16)[:, 0:1024]).then_inc(sb, 1)
            s.wait_ge(tx, 9)
            s.copy(out=wqt[:], in_=P[3][:].bitcast(BF16)[:, 0:256])
            s.activation(out=keyst[:], in_=P[3][:].bitcast(BF16)[0:64, 256:384],
                         func=AF.Copy, scale=0.125)
            s.copy(out=keystu[:], in_=P[3][:].bitcast(BF16)[0:64, 256:384])
            s.copy(out=s_wrt[0:64, 0:48],
                   in_=P[3][:].bitcast(BF16)[0:64, 384:432]).then_inc(sb, 2)
            s.wait_ge(tx, 10)
            s.activation(out=s_rk[:], in_=P[3][0:48, 256:384], func=AF.Copy,
                         scale=float(RSQRT_RD)).then_inc(sb, 1)

        @block.sync
        def _(sy):
            sy.wait_ge(ve, 128)
            sy.dma_start(out=o_wtm.rearrange("(j k) n -> k j n", j=4),
                         in_=wtm[:].rearrange("p (j n) -> p j n", n=4096)).then_inc(so, 16)
            sy.wait_ge(sb, 2 + 32 + 4 + 1 + 1 + 1 + 1)
            sy.dma_start(out=o_vd, in_=s_vd[:]).then_inc(so, 16)
            sy.dma_start(out=o_wu1t.rearrange("(j k) u -> k j u", j=4),
                         in_=wu1t[:].rearrange("p (j u) -> p j u", j=4)).then_inc(so, 16)
            sy.dma_start(out=o_wvut.rearrange("(v k) h -> k v h", v=2),
                         in_=wvut[:].rearrange("p (v h) -> p v h", v=2)).then_inc(so, 16)
            sy.dma_start(out=o_wqt.rearrange("(t k) d -> k t d", t=4),
                         in_=wqt[:].rearrange("p (t d) -> p t d", t=4)).then_inc(so, 16)
            sy.dma_start(out=o_keyst, in_=keyst[:]).then_inc(so, 16)
            sy.dma_start(out=o_rk, in_=s_rk[:]).then_inc(so, 16)
            sy.dma_start(out=o_wrt, in_=s_wrt[0:64, 0:48]).then_inc(so, 16)
            sy.wait_ge(so, 16 * 8)
    return nc


def _l0_in_maps(inp, consts):
    idt, e4c, place, ks16 = consts
    mask = np.asarray(inp["L0_attn_mask"])
    w = np.asarray(inp["L0_attn_weight"], dtype=np.float32)
    V = np.asarray(inp["L0_ffn_aux_values"], dtype=np.float32)
    Wvd = np.asarray(inp["L0_ffn_Wvd"], dtype=np.float32)
    Wu1 = np.asarray(inp["L0_ffn_Wu1"], dtype=np.float32)
    Wvu = np.asarray(inp["L0_ffn_Wvu"], dtype=np.float32)
    Wq = np.asarray(inp["L0_ffn_Wq"], dtype=np.float32)
    keys = np.asarray(inp["L0_ffn_aux_keys"], dtype=np.float32)
    Wr = np.asarray(inp["L0_ffn_Wr"], dtype=np.float32)
    maps = []
    for c in range(NC):
        sl = slice(512 * c, 512 * (c + 1))
        maps.append({
            "m_cols": np.ascontiguousarray(
                mask[:, sl].reshape(32, 128, 512).transpose(1, 0, 2)),
            "w_cols": np.ascontiguousarray(
                w[:, sl].reshape(32, 128, 512).transpose(1, 0, 2)),
            "v_rows": np.ascontiguousarray(V[128 * c:128 * (c + 1), :]),
            "wvd": np.ascontiguousarray(Wvd.reshape(2, 128, 4096).transpose(1, 0, 2)),
            "wu1_cols": np.ascontiguousarray(
                Wu1[:, sl].reshape(8, 128, 512).transpose(1, 0, 2)),
            "wvu_rows": np.ascontiguousarray(
                Wvu[sl, :].reshape(4, 128, 256).transpose(1, 0, 2)),
            "wq_cols": np.ascontiguousarray(Wq[:, sl]),
            "keys_rows": np.ascontiguousarray(keys[128 * c:128 * (c + 1), :]),
            "wr": Wr,
            "idt": idt, "e4c": e4c, "place": place, "ks16": ks16,
        })
    return maps


def _l0_assemble(res):
    wT = np.concatenate([np.asarray(res[c]["o_wtm"]) for c in range(NC)], axis=0)
    w_l1 = np.ascontiguousarray(
        wT.reshape(32, 128, NSL, SLW).transpose(2, 1, 0, 3))
    vd = np.concatenate([np.asarray(res[c]["o_vd"]) for c in range(NC)], axis=0)
    vd_l1 = np.ascontiguousarray(vd.reshape(8, 128, 256).transpose(1, 0, 2))
    wu1T = np.concatenate([np.asarray(res[c]["o_wu1t"]) for c in range(NC)], axis=0)
    wu1t_l1 = np.ascontiguousarray(
        wu1T.reshape(32, 128, 8, 128).transpose(2, 1, 0, 3))
    wvuT = np.concatenate([np.asarray(res[c]["o_wvut"]) for c in range(NC)], axis=1)
    wvut_l1 = np.ascontiguousarray(wvuT.reshape(2, 128, 4096).transpose(1, 0, 2))
    wqT = np.concatenate([np.asarray(res[c]["o_wqt"]) for c in range(NC)], axis=0)
    wqt_l1 = np.ascontiguousarray(wqT.reshape(32, 128, 64).transpose(1, 0, 2))
    keysT = np.concatenate([np.asarray(res[c]["o_keyst"]) for c in range(NC)], axis=1)
    rkT = np.concatenate([np.asarray(res[c]["o_rk"]) for c in range(NC)], axis=1)
    wrT = np.asarray(res[0]["o_wrt"])
    return dict(w_l1=w_l1, vd_l1=vd_l1, wu1t_l1=wu1t_l1, wvut_l1=wvut_l1,
                wqt_l1=wqt_l1, keysT=keysT, rkT=rkT, wrT=wrT)


DBG_L1 = 0
NSL = 16
SLW = 256
NFFN = 10


def build_l1():
    nc = bass.Bass("TRN2", target_bir_lowering=False, debug=False)
    x_rows = nc.dram_tensor("x_rows", [8, 128, 4096], F32, kind="ExternalInput").ap()
    x_cols = nc.dram_tensor("x_cols", [NSL, 128, 8, SLW], F32, kind="ExternalInput").ap()
    w_l1 = nc.dram_tensor("w_l1", [NSL, 128, 32, SLW], BF16, kind="ExternalInput").ap()
    wu1t = nc.dram_tensor("wu1t", [8, 128, 32, 128], BF16, kind="ExternalInput").ap()
    i_vd = nc.dram_tensor("i_vd", [128, 8, 256], BF16, kind="ExternalInput").ap()
    i_wvut = nc.dram_tensor("i_wvut", [128, 2, 4096], BF16, kind="ExternalInput").ap()
    i_wqt = nc.dram_tensor("i_wqt", [128, 32, 64], BF16, kind="ExternalInput").ap()
    i_keyst = nc.dram_tensor("i_keyst", [64, 1024], BF16, kind="ExternalInput").ap()
    i_rkt = nc.dram_tensor("i_rkt", [48, 1024], BF16, kind="ExternalInput").ap()
    i_wrt = nc.dram_tensor("i_wrt", [64, 48], BF16, kind="ExternalInput").ap()
    i_logrel = nc.dram_tensor("i_logrel", [128, 1024], F32, kind="ExternalInput").ap()
    bias_r = nc.dram_tensor("bias_r", [1, 4096], F32, kind="ExternalInput").ap()
    i_bu1 = nc.dram_tensor("i_bu1", [128, 8], F32, kind="ExternalInput").ap()
    i_wu2t = nc.dram_tensor("i_wu2t", [128, 8], F32, kind="ExternalInput").ap()
    i_bu2 = nc.dram_tensor("i_bu2", [1, 1], F32, kind="ExternalInput").ap()
    idt = nc.dram_tensor("idt", [128, 128], BF16, kind="ExternalInput").ap()

    o_h = nc.dram_tensor("o_h", [8, 128, 4096], BF16, kind="ExternalOutput").ap()
    o_a2 = nc.dram_tensor("o_a2", [8, 128, 4096], BF16, kind="ExternalOutput").ap()
    o_cols = nc.dram_tensor("o_cols", [8, 128, 8], F32, kind="ExternalOutput").ap()
    o_lsum = nc.dram_tensor("o_lsum", [1, 1], F32, kind="ExternalOutput").ap()
    if DBG_L1 >= 1:
        o_dbg = nc.dram_tensor("o_dbg", [2, 128, 4096], BF16, kind="ExternalOutput").ap()
        o_dbg2 = nc.dram_tensor("o_dbg2", [128, 8192], BF16, kind="ExternalOutput").ap()

    with ExitStack() as ctx:
        e = ctx.enter_context
        y = [e(nc.sbuf_tensor(f"y{t}", [128, 4096], BF16)) for t in range(8)]
        xT = e(nc.sbuf_tensor("xT", [128, 32 * 1024], BF16))
        colb = e(nc.sbuf_tensor("colb", [128, 8 * 8], F32))
        s_idt = e(nc.sbuf_tensor("s_idt", [128, 128], BF16))
        ones1 = e(nc.sbuf_tensor("ones1", [1, 128], BF16))
        onesc = e(nc.sbuf_tensor("onesc", [128, 1], BF16))
        id1f = e(nc.sbuf_tensor("id1f", [1, 1], F32))
        mpart = e(nc.sbuf_tensor("mpart", [128, 8 * NSL], F32))
        sqc = e(nc.sbuf_tensor("sqc", [128, 8], F32))
        vbuf = e(nc.sbuf_tensor("vbuf", [128, 8], F32))
        rbuf = e(nc.sbuf_tensor("rbuf", [128, 8], F32))
        nbuf = e(nc.sbuf_tensor("nbuf", [128, 8], F32))
        mbuf = e(nc.sbuf_tensor("mbuf", [128, 8], F32))
        ccol = e(nc.sbuf_tensor("ccol", [128, 16], F32))
        lv8 = e(nc.sbuf_tensor("lv8", [128, 8], BF16))
        lsum_sb = e(nc.sbuf_tensor("lsum_sb", [1, 8], F32))
        cst = e(nc.sbuf_tensor("cst", [128, 2], F32))
        dmv = e(nc.sbuf_tensor("dmv", [1, 4], F32))
        dms = e(nc.sbuf_tensor("dms", [1, 4], F32))
        P = [e(nc.psum_tensor(f"P{i}", [128, 512], F32)) for i in range(8)]

        sem_names = ["ld", "lx0", "lx1", "l20", "l21", "lw0", "lw1", "lf", "lz0", "lz1", "tt", "xe", "tm", "ae",
                     "sq", "va", "rc", "nm", "ha", "lv", "tls", "vls",
                     "tq", "sq2", "tr", "sr", "tzc", "szb", "vg1", "st", "vg",
                     "tw2", "sl", "trA", "vA", "vtk", "sex", "vR", "tpT", "spT",
                     "ta1", "sa1", "ta1t", "sa1t", "ta2", "va2", "ya2", "vst",
                     "vlc", "vini", "so", "vch", "sch", "tl2", "vw2", "vlr"]
        S = {n: e(nc.semaphore(n)) for n in sem_names}
        block = e(nc.Block())

        with ExitStack() as actx:
            ea = actx.enter_context
            xload = ea(nc.sbuf_tensor("xload", [128, 2 * 4096], BF16))
            biasb = ea(nc.sbuf_tensor("biasb", [1, 4096], BF16))
            x2b = ea(nc.sbuf_tensor("x2b", [128, 2 * 8 * SLW], BF16))
            wbuf = ea(nc.sbuf_tensor("wbuf", [128, 2 * 32 * SLW], BF16))

            @block.gpsimd
            def _(g):
                g.dma_start(out=s_idt[:], in_=idt).then_inc(S["ld"], 16)
                g.dma_start(out=biasb[:], in_=bias_r).then_inc(S["ld"], 16)
                for t in range(8):
                    if t >= 2:
                        g.wait_ge(S["xe"], 8 * (t - 1))
                    g.dma_start(out=xload[:, 4096 * (t % 2):4096 * (t % 2 + 1)],
                                in_=x_rows[t]).then_inc(S[f"lx{t % 2}"], 16)
                for s in range(NSL):
                    if s >= 2:
                        g.wait_ge(S["ae"], 8 * (s - 1))
                    g.dma_start(
                        out=x2b[:, 8 * SLW * (s % 2):8 * SLW * (s % 2 + 1)]
                            .rearrange("p (a b) -> p a b", b=SLW),
                        in_=x_cols[s]).then_inc(S[f"l2{s % 2}"], 16)

            @block.sync
            def _(sy):
                for s in range(NSL):
                    if s >= 2:
                        sy.wait_ge(S["ae"], 8 * (s - 1))
                    sy.dma_start(
                        out=wbuf[:, 32 * SLW * (s % 2):32 * SLW * (s % 2 + 1)]
                            .rearrange("p (a b) -> p a b", b=SLW),
                        in_=w_l1[s]).then_inc(S[f"lw{s % 2}"], 16)

            @block.tensor
            def _(t_):
                t_.wait_ge(S["ld"], 32)
                t_.wait_ge(S["vini"], 1)
                for t in range(8):
                    t_.wait_ge(S[f"lx{t % 2}"], 16 * (t // 2 + 1))
                    for b in range(8):
                        gi = t * 8 + b
                        if gi >= 2:
                            t_.wait_ge(S["xe"], gi - 1)
                        for j in range(4):
                            kc = 4 * b + j
                            mm = t_.transpose(
                                out=P[gi % 2][:].bitcast(BF16)[:, 128 * j:128 * (j + 1)],
                                in_=xload[:, 4096 * (t % 2) + 128 * kc:
                                          4096 * (t % 2) + 128 * (kc + 1)],
                                identity=s_idt[:])
                        mm.then_inc(S["tt"], 1)
                for s in range(NSL):
                    t_.wait_ge(S[f"lw{s % 2}"], 16 * (s // 2 + 1))
                    if s == 0:
                        t_.wait_ge(S["xe"], 64)
                    for t in range(8):
                        if s >= 1:
                            t_.wait_ge(S["ae"], 8 * (s - 1) + t + 1)
                        t_.matmul(out=P[t][:, 0:SLW], lhsT=ones1[:],
                                  rhs=biasb[0:1, SLW * s:SLW * (s + 1)],
                                  start=True, stop=False)
                        for kc in range(32):
                            mm = t_.matmul(
                                out=P[t][:, 0:SLW],
                                lhsT=xT[:, 1024 * kc + 128 * t:1024 * kc + 128 * (t + 1)],
                                rhs=wbuf[:, 32 * SLW * (s % 2) + SLW * kc:
                                         32 * SLW * (s % 2) + SLW * (kc + 1)],
                                start=False, stop=(kc == 31))
                        mm.then_inc(S["tm"], 1)

            @block.vector
            def _(v):
                v.memset(ones1[:], 1.0)
                v.memset(onesc[:], 1.0)
                v.memset(dmv[:], 0.0)
                v.memset(colb[:], 0.0)
                v.memset(cst[:, 0:1], 1e-5)
                v.memset(cst[:, 1:2], 1.0)
                v.memset(id1f[:], 1.0).then_inc(S["vini"], 1)
                for t in range(8):
                    for b in range(8):
                        gi = t * 8 + b
                        v.wait_ge(S["tt"], gi + 1)
                        dst = AP(xT, 4096 * b + 128 * t,
                                 [[32768, 128], [1024, 4], [1, 128]])
                        v.tensor_copy(out=dst,
                                      in_=P[gi % 2][:].bitcast(BF16)[:, 0:512]
                                      .rearrange("p (a c) -> p a c", c=128)
                                      ).then_inc(S["xe"], 1)
                for s in range(NSL):
                    v.wait_ge(S[f"l2{s % 2}"], 16 * (s // 2 + 1))
                    for t in range(8):
                        v.wait_ge(S["tm"], 8 * s + t + 1)
                        v.scalar_tensor_tensor(
                            out=y[t][:, SLW * s:SLW * (s + 1)],
                            in0=P[t][:, 0:SLW], scalar=1.0,
                            in1=x2b[:, 8 * SLW * (s % 2) + SLW * t:
                                    8 * SLW * (s % 2) + SLW * (t + 1)],
                            op0=ALU.mult, op1=ALU.add,
                            accum_out=mpart[:, NSL * t + s:NSL * t + s + 1],
                        ).then_inc(S["ae"], 1)
        w1b = e(nc.sbuf_tensor("w1b", [128, 2 * 4096], BF16))
        wvutb = e(nc.sbuf_tensor("wvutb", [128, 2 * 4096], BF16))
        vdb = e(nc.sbuf_tensor("vdb", [128, 8 * 256], BF16))
        wqtb = e(nc.sbuf_tensor("wqtb", [128, 32 * 64], BF16))
        keystb = e(nc.sbuf_tensor("keystb", [64, 1024], BF16))
        rktb = e(nc.sbuf_tensor("rktb", [48, 1024], BF16))
        wrtb = e(nc.sbuf_tensor("wrtb", [64, 48], BF16))
        logrelb = e(nc.sbuf_tensor("logrelb", [128, 1024], BF16))
        bu1b = e(nc.sbuf_tensor("bu1b", [128, 8], F32))
        wu2tb = e(nc.sbuf_tensor("wu2tb", [128, 8], BF16))
        bu2b = e(nc.sbuf_tensor("bu2b", [1, 1], F32))
        qbuf = e(nc.sbuf_tensor("qbuf", [64, 1024], BF16))
        rqb = e(nc.sbuf_tensor("rqb", [48, 1024], BF16))
        zbt = e(nc.sbuf_tensor("zbt", [128, 1024], BF16))
        gb = e(nc.sbuf_tensor("gb", [128, 2 * 1024], BF16))
        gx1 = e(nc.sbuf_tensor("gx1", [128, 1024], BF16))
        gx2 = e(nc.sbuf_tensor("gx2", [128, 1024], BF16))
        learn8 = e(nc.sbuf_tensor("learn8", [128, 8], F32))
        Rbuf = e(nc.sbuf_tensor("Rbuf", [128, 1024], F32))
        Abuf = e(nc.sbuf_tensor("Abuf", [128, 1024], BF16))
        top8 = e(nc.sbuf_tensor("top8", [128, 8], F32))
        tauc = e(nc.sbuf_tensor("tauc", [128, 1], F32))
        nbig = e(nc.sbuf_tensor("nbig", [128, 1], BF16))
        zsc = e(nc.sbuf_tensor("zsc", [128, 1], F32))
        rzc = e(nc.sbuf_tensor("rzc", [128, 1], F32))
        nmx = e(nc.sbuf_tensor("nmx", [128, 1], F32))
        a1b = e(nc.sbuf_tensor("a1b", [128, 256], BF16))
        a1Tb = e(nc.sbuf_tensor("a1Tb", [128, 256], BF16))
        a2t = e(nc.sbuf_tensor("a2t", [128, 4096], BF16))
        apart = e(nc.sbuf_tensor("apart", [128, 8], F32))
        learnacc = e(nc.sbuf_tensor("learnacc", [1, 1024], BF16))

        @block.gpsimd
        def _(g):
            if DBG_L1 >= 1:
                return
            g.wait_ge(S["ae"], 128)
            g.dma_start(out=vdb[:].rearrange("p (a b) -> p a b", b=256),
                        in_=i_vd).then_inc(S["lf"], 16)
            g.dma_start(out=wvutb[:].rearrange("p (a b) -> p a b", b=4096),
                        in_=i_wvut).then_inc(S["lf"], 16)
            g.dma_start(out=wqtb[:].rearrange("p (a b) -> p a b", b=64),
                        in_=i_wqt).then_inc(S["lf"], 16)
            g.dma_start(out=keystb[:], in_=i_keyst).then_inc(S["lf"], 16)
            g.dma_start(out=rktb[:], in_=i_rkt).then_inc(S["lf"], 16)
            g.dma_start(out=wrtb[:], in_=i_wrt).then_inc(S["lf"], 16)
            g.dma_start(out=logrelb[:], in_=i_logrel).then_inc(S["lf"], 16)
            g.dma_start(out=bu1b[:], in_=i_bu1).then_inc(S["lf"], 16)
            g.dma_start(out=wu2tb[:], in_=i_wu2t).then_inc(S["lf"], 16)
            g.dma_start(out=bu2b[:], in_=i_bu2).then_inc(S["lf"], 16)
            for uc in range(8):
                if uc >= 2:
                    g.wait_ge(S["szb"], uc - 1)
                g.dma_start(out=w1b[:, 4096 * (uc % 2):4096 * (uc % 2 + 1)]
                            .rearrange("p (a b) -> p a b", b=128),
                            in_=wu1t[uc]).then_inc(S[f"lz{uc % 2}"], 16)

        @block.sync
        def _(sy):
            for t in range(8):
                sy.wait_ge(S["ha"], t + 1)
                sy.wait_ge(S["so"], 16 * t)
                sy.dma_start(out=o_h[t], in_=y[t][:]).then_inc(S["so"], 16)
            if DBG_L1 >= 1:
                sy.wait_ge(S["xe"], 64)
                sy.dma_start(out=o_dbg2, in_=xT[:, 0:8192]).then_inc(S["so"], 16)
                sy.wait_ge(S["sq"], 1)
                sy.dma_start(out=o_dbg[0], in_=y[0][:]).then_inc(S["so"], 16)
                sy.wait_ge(S["ha"], 8)
                sy.dma_start(out=o_dbg[1], in_=y[0][:]).then_inc(S["so"], 16)
                sy.wait_ge(S["so"], 16 * 11)
                return
            for t in range(8):
                sy.wait_ge(S["va2"], 8 * (t + 1))
                sy.dma_start(out=o_a2[t], in_=a2t[:]).then_inc(S["ya2"], 16)
            sy.wait_ge(S["vst"], 8)
            sy.wait_ge(S["vlc"], 1)
            sy.wait_ge(S["ha"], 8)
            sy.wait_ge(S["nm"], 8)
            sy.dma_start(out=o_cols.rearrange("t p c -> p t c"),
                         in_=colb[:].rearrange("p (t c) -> p t c", c=8)
                         ).then_inc(S["so"], 16)
            sy.wait_ge(S["vls"], 1)
            sy.dma_start(out=o_lsum, in_=lsum_sb[0:1, 0:1]).then_inc(S["so"], 16)
            sy.wait_ge(S["so"], 16 * 10)
            sy.wait_ge(S["ya2"], 16 * 8)

        @block.tensor
        def _(t_):
            if DBG_L1 >= 1:
                return
            for t in range(8):
                t_.wait_ge(S["ha"], t + 1)
                for b in range(8):
                    gi = t * 8 + b
                    if gi >= 2:
                        t_.wait_ge(S["xe"], 64 + gi - 1)
                    for j in range(4):
                        kc = 4 * b + j
                        mm = t_.transpose(
                            out=P[2 + gi % 2][:].bitcast(BF16)[:, 128 * j:128 * (j + 1)],
                            in_=y[t][:, 128 * kc:128 * (kc + 1)],
                            identity=s_idt[:])
                    mm.then_inc(S["tt"], 1)
            t_.wait_ge(S["lv"], 1)
            t_.matmul(out=P[0][0:1, 0:8], lhsT=onesc[:], rhs=lv8[:],
                      start=True, stop=True).then_inc(S["tls"], 1)
            t_.wait_ge(S["xe"], 128)
            t_.wait_ge(S["lf"], 16 * NFFN)
            for hf in range(2):
                for kc in range(32):
                    mm = t_.matmul(
                        out=P[4 + hf][0:64, 0:512],
                        lhsT=wqtb[:, 64 * kc:64 * (kc + 1)],
                        rhs=xT[:, 1024 * kc + 512 * hf:1024 * kc + 512 * (hf + 1)],
                        start=(kc == 0), stop=(kc == 31))
                mm.then_inc(S["tq"], 1)
            t_.wait_ge(S["sq2"], 1)
            for hf in range(2):
                t_.matmul(out=P[6 + hf][0:48, 0:512], lhsT=wrtb[0:64, 0:48],
                          rhs=qbuf[:, 512 * hf:512 * (hf + 1)],
                          start=True, stop=True).then_inc(S["tr"], 1)
            t_.wait_ge(S["vls"], 1)
            t_.wait_ge(S["sr"], 1)
            for t in range(8):
                t_.wait_ge(S["vA"], t)
                for hf in range(2):
                    t_.matmul(out=P[4 + hf][:, 0:512],
                              lhsT=rqb[:, 128 * t:128 * (t + 1)],
                              rhs=rktb[:, 512 * hf:512 * (hf + 1)],
                              start=True, stop=True).then_inc(S["trA"], 1)
                if t >= 1:
                    t_.wait_ge(S["sa1t"], t)
                for hf in range(2):
                    t_.matmul(out=P[6 + hf][:, 0:512],
                              lhsT=qbuf[:, 128 * t:128 * (t + 1)],
                              rhs=keystb[:, 512 * hf:512 * (hf + 1)],
                              start=True, stop=True).then_inc(S["trA"], 1)
                t_.wait_ge(S[f"lz{t % 2}"], 16 * (t // 2 + 1))
                if t >= 2:
                    t_.wait_ge(S["szb"], t - 1)
                    t_.wait_ge(S["vw2"], t - 1)
                if t >= 1:
                    t_.wait_ge(S["va2"], 8 * t)
                for hf in range(2):
                    bank = (0 if hf == 0 else 2) + t % 2
                    for kc in range(32):
                        mm = t_.matmul(
                            out=P[bank][:, 0:512],
                            lhsT=w1b[:, 4096 * (t % 2) + 128 * kc:
                                     4096 * (t % 2) + 128 * (kc + 1)],
                            rhs=xT[:, 1024 * kc + 512 * hf:1024 * kc + 512 * (hf + 1)],
                            start=(kc == 0), stop=(kc == 31))
                    mm.then_inc(S["tzc"], 1)
                if t >= 1:
                    guc = t - 1
                    t_.wait_ge(S["vg"], guc + 1)
                    t_.wait_ge(S["szb"], t)
                    for hf in range(2):
                        bank = (0 if hf == 0 else 2) + guc % 2
                        mm = t_.matmul(out=P[bank][0:1, 0:512],
                                       lhsT=wu2tb[:, guc:guc + 1],
                                       rhs=gb[:, 1024 * (guc % 2) + 512 * hf:
                                              1024 * (guc % 2) + 512 * (hf + 1)],
                                       start=True, stop=True)
                    mm.then_inc(S["tl2"], 1)
                t_.wait_ge(S["vR"], t + 1)
                for j in range(8):
                    mm = t_.transpose(out=P[6][:].bitcast(BF16)[:, 128 * j:128 * (j + 1)],
                                      in_=Rbuf[:].bitcast(BF16)[:, 128 * j:128 * (j + 1)],
                                      identity=s_idt[:])
                mm.then_inc(S["tpT"], 1)
                t_.wait_ge(S["spT"], t + 1)
                for j in range(8):
                    mm = t_.matmul(out=P[7][:, 0:256],
                                   lhsT=Abuf[:, 128 * j:128 * (j + 1)],
                                   rhs=vdb[:, 256 * j:256 * (j + 1)],
                                   start=(j == 0), stop=(j == 7))
                mm.then_inc(S["ta1"], 1)
                t_.wait_ge(S["sa1"], t + 1)
                for v in range(2):
                    mm = t_.transpose(out=P[6][:].bitcast(BF16)[:, 128 * v:128 * (v + 1)],
                                      in_=a1b[:, 128 * v:128 * (v + 1)],
                                      identity=s_idt[:])
                mm.then_inc(S["ta1t"], 1)
                t_.wait_ge(S["sa1t"], t + 1)
                if t >= 1:
                    t_.wait_ge(S["vw2"], t)
                for hs in range(8):
                    if hs >= 2:
                        t_.wait_ge(S["va2"], 8 * t + hs - 1)
                    for v in range(2):
                        mm = t_.matmul(out=P[hs % 2][:, 0:512],
                                       lhsT=a1Tb[:, 128 * v:128 * (v + 1)],
                                       rhs=wvutb[:, 4096 * v + 512 * hs:
                                                 4096 * v + 512 * (hs + 1)],
                                       start=(v == 0), stop=(v == 1))
                    mm.then_inc(S["ta2"], 1)
            t_.wait_ge(S["vg"], 8)
            t_.wait_ge(S["szb"], 8)
            for hf in range(2):
                bank = (0 if hf == 0 else 2) + 1
                mm = t_.matmul(out=P[bank][0:1, 0:512], lhsT=wu2tb[:, 7:8],
                               rhs=gb[:, 1024 + 512 * hf:1024 + 512 * (hf + 1)],
                               start=True, stop=True)
            mm.then_inc(S["tl2"], 1)
            t_.wait_ge(S["vlr"], 1)
            for j in range(8):
                mm = t_.transpose(out=P[4][:, j:j + 1],
                                  in_=Rbuf[0:1, 128 * j:128 * (j + 1)],
                                  identity=id1f[:])
            mm.then_inc(S["tls"], 1)
        @block.vector
        def _(v):
            vch = _Chain(v, S["vch"])
            v.memset(nbig[:], NEGBIG)
            for t in range(8):
                v.wait_ge(S["sq"], t + 1)
                vch.step(v.reduce_sum(out=ccol[:, 0:1],
                                      in_=mpart[:, NSL * t:NSL * (t + 1)],
                                      axis=AX.X))
                v.tensor_scalar(out=ccol[:, 1:2], in0=sqc[:, t:t + 1],
                                scalar1=1.0 / 4096, scalar2=None, op0=ALU.mult)
                vch.step(v.tensor_scalar(out=mbuf[:, t:t + 1], in0=ccol[:, 0:1],
                                         scalar1=1.0 / 4096, scalar2=None,
                                         op0=ALU.mult))
                vch.step(v.tensor_tensor(out=ccol[:, 2:3], in0=mbuf[:, t:t + 1],
                                         in1=mbuf[:, t:t + 1], op=ALU.mult))
                v.tensor_sub(out=vbuf[:, t:t + 1], in0=ccol[:, 1:2],
                             in1=ccol[:, 2:3]).then_inc(S["va"], 1)
            for t in range(8):
                v.wait_ge(S["rc"], t + 1)
                if t >= 1:
                    v.wait_ge(S["nm"], t)
                vch.step(v.tensor_tensor(out=ccol[:, 3:4], in0=rbuf[:, t:t + 1],
                                         in1=rbuf[:, t:t + 1], op=ALU.mult))
                v.scalar_tensor_tensor(out=nbuf[:, t:t + 1], in0=mbuf[:, t:t + 1],
                                       scalar=-1.0, in1=rbuf[:, t:t + 1],
                                       op0=ALU.mult, op1=ALU.mult)
                v.tensor_tensor(out=colb[:, 8 * t + 6:8 * t + 7],
                                in0=vbuf[:, t:t + 1], in1=ccol[:, 3:4],
                                op=ALU.mult).then_inc(S["nm"], 1)
            for t in range(8):
                v.wait_ge(S["ha"], t + 1)
                cp = v.tensor_copy(out=lv8[:, t:t + 1], in_=colb[:, 8 * t:8 * t + 1])
            cp.then_inc(S["lv"], 1)
            if DBG_L1 >= 1:
                return
            for t in range(8):
                for b in range(8):
                    gi = t * 8 + b
                    v.wait_ge(S["tt"], 64 + gi + 1)
                    dst = AP(xT, 4096 * b + 128 * t,
                             [[32768, 128], [1024, 4], [1, 128]])
                    v.tensor_copy(out=dst,
                                  in_=P[2 + gi % 2][:].bitcast(BF16)[:, 0:512]
                                  .rearrange("p (a c) -> p a c", c=128)
                                  ).then_inc(S["xe"], 1)
            v.wait_ge(S["tls"], 1)
            v.reduce_sum(out=lsum_sb[0:1, 0:1], in_=P[0][0:1, 0:8],
                         axis=AX.X).then_inc(S["vls"], 1)
            v.memset(learnacc[0:1, :], 0.0)
            v.wait_ge(S["lf"], 16 * NFFN)

            def v_tail(tt):
                v.wait_ge(S["va2"], 8 * (tt + 1))
                v.wait_ge(S["ha"], tt + 1)
                v.wait_ge(S["so"], 16 * (tt + 1))
                vch.step(v.reduce_sum(out=colb[:, 8 * tt + 2:8 * tt + 3],
                                      in_=apart[:], axis=AX.X))
                vch.step(v.scalar_tensor_tensor(
                    out=y[tt][:], in0=a2t[:], scalar=1.0,
                    in1=y[tt][:], op0=ALU.mult, op1=ALU.mult,
                    accum_out=colb[:, 8 * tt + 3:8 * tt + 4]))
                v.scalar_tensor_tensor(out=y[tt][:],
                                       in0=a2t[:], scalar=1.0,
                                       in1=a2t[:], op0=ALU.mult, op1=ALU.mult,
                                       accum_out=colb[:, 8 * tt + 4:8 * tt + 5]
                                       ).then_inc(S["vst"], 1)
                v.wait_ge(S["szb"], tt + 1)
                if tt >= 2:
                    v.wait_ge(S["tl2"], tt - 1)
                vch.step(v.tensor_tensor(out=gx1[:], in0=zbt[:], in1=zbt[:],
                                         op=ALU.mult))
                vch.step(v.tensor_tensor(out=gx2[:], in0=gx1[:], in1=zbt[:],
                                         op=ALU.mult))
                v.scalar_tensor_tensor(out=gx1[:], in0=gx2[:], scalar=0.044715,
                                       in1=zbt[:], op0=ALU.mult,
                                       op1=ALU.add).then_inc(S["vg1"], 1)
                v.wait_ge(S["st"], tt + 1)
                vch.step(v.tensor_scalar(out=gx1[:], in0=gx2[:], scalar1=1.0,
                                         scalar2=None, op0=ALU.add))
                vch.step(v.tensor_scalar(out=gx2[:], in0=zbt[:], scalar1=0.5,
                                         scalar2=None, op0=ALU.mult))
                v.tensor_tensor(out=gb[:, 1024 * (tt % 2):1024 * (tt % 2 + 1)],
                                in0=gx1[:], in1=gx2[:],
                                op=ALU.mult).then_inc(S["vg"], 1)

            def v_accum(guc):
                v.wait_ge(S["tl2"], guc + 1)
                for hf in range(2):
                    bank = (0 if hf == 0 else 2) + guc % 2
                    acc_mm = v.tensor_tensor(
                        out=learnacc[0:1, 512 * hf:512 * (hf + 1)],
                        in0=learnacc[0:1, 512 * hf:512 * (hf + 1)],
                        in1=P[bank][0:1, 0:512], op=ALU.add)
                acc_mm.then_inc(S["vw2"], 1)

            for t in range(8):
                for hf in range(2):
                    v.wait_ge(S["trA"], 4 * t + hf + 1)
                    vch.step(v.tensor_tensor(out=Rbuf[:, 512 * hf:512 * (hf + 1)],
                                    in0=P[4 + hf][:, 0:512],
                                    in1=logrelb[:, 512 * hf:512 * (hf + 1)],
                                    op=ALU.add))
                for hf in range(2):
                    v.wait_ge(S["trA"], 4 * t + 2 + hf + 1)
                    if hf == 0:
                        vch.step(v.tensor_tensor(out=Abuf[:, 0:512],
                                        in0=P[6][:, 0:512],
                                        in1=logrelb[:, 0:512], op=ALU.add))
                    else:
                        v.tensor_tensor(out=Abuf[:, 512:1024],
                                        in0=P[7][:, 0:512],
                                        in1=logrelb[:, 512:1024],
                                        op=ALU.add).then_inc(S["vA"], 1)
                v.wait_ge(S["vA"], t + 1)
                vch.step(v.max(out=top8[:], in_=Rbuf[:]))
                vch.step(v.tensor_copy(out=tauc[:], in_=top8[:, 7:8]))
                vch.step(v.scalar_tensor_tensor(
                    out=gb[:, 1024:2048], in0=Rbuf[:], scalar=tauc[:, 0:1],
                    in1=nbig[:, 0:1].to_broadcast([128, 1024]),
                    op0=ALU.is_lt, op1=ALU.mult))
                vch.step(v.tensor_tensor(out=gb[:, 0:1024], in0=Abuf[:],
                                         in1=gb[:, 1024:2048], op=ALU.add))
                v.reduce_max(out=nmx[:], in_=gb[:, 0:1024], axis=AX.X,
                             negate=True).then_inc(S["vtk"], 1)
                v.wait_ge(S["sex"], t + 1)
                vch.step(v.reciprocal(out=rzc[:], in_=zsc[:]))
                v.scalar_tensor_tensor(out=Rbuf[:].bitcast(BF16)[:, 0:1024],
                                       in0=gb[:, 1024:2048], scalar=rzc[:, 0:1],
                                       in1=gb[:, 1024:2048], op0=ALU.mult,
                                       op1=ALU.bypass).then_inc(S["vR"], 1)
                if t >= 1:
                    v_tail(t - 1)
                    v_accum(t - 1)
            v_tail(7)
            v_accum(7)
            v.wait_ge(S["vw2"], 8)
            vch.step(v.tensor_copy(out=dmv[:, 0:1], in_=dmv[:, 1:2]))
            v.scalar_tensor_tensor(out=Rbuf[0:1, 0:1024], in0=learnacc[0:1, :],
                                   scalar=bu2b[0:1, 0:1], in1=learnacc[0:1, :],
                                   op0=ALU.add, op1=ALU.bypass).then_inc(S["vlr"], 1)
            v.wait_ge(S["sl"], 2)
            for t in range(8):
                cp = v.tensor_copy(out=colb[:, 8 * t + 1:8 * t + 2],
                                   in_=learn8[:, t:t + 1])
            cp.then_inc(S["vlc"], 1)
        @block.scalar
        def _(s_):
            sch = _Chain(s_, S["sch"])
            for t in range(8):
                s_.wait_ge(S["ae"], 128)
                if t >= 1:
                    s_.wait_ge(S["sq"], t)
                dump = a2t[:] if t % 2 == 0 else xT[:, 0:4096].bitcast(BF16)[:, 0:4096]
                s_.activation(out=dump, in_=y[t][:], func=AF.Square,
                              accum_out=sqc[:, t:t + 1]).then_inc(S["sq"], 1)
            for t in range(8):
                s_.wait_ge(S["va"], t + 1)
                if t >= 1:
                    s_.wait_ge(S["rc"], t)
                sch.step(s_.activation(out=ccol[:, 4:5], in_=vbuf[:, t:t + 1],
                                       func=AF.Ln, bias=cst[:, 0:1], scale=1.0))
                s_.activation(out=rbuf[:, t:t + 1], in_=ccol[:, 4:5], func=AF.Exp,
                              scale=-0.5).then_inc(S["rc"], 1)
            for t in range(8):
                s_.wait_ge(S["nm"], t + 1)
                if t >= 1:
                    s_.wait_ge(S["ha"], t)
                s_.activation(out=colb[:, 8 * t:8 * t + 1],
                              in_=colb[:, 8 * t + 6:8 * t + 7], func=AF.Ln,
                              bias=cst[:, 1:2], scale=1.0)
                apply_out = y[t][:] if DBG_L1 < 2 else a2t[:]
                sch.step(s_.activation(out=apply_out, in_=y[t][:], func=AF.Identity,
                              scale=rbuf[:, t:t + 1], bias=nbuf[:, t:t + 1],
                              accum_out=ccol[:, 5:6]))
                s_.copy(out=colb[:, 8 * t + 5:8 * t + 6],
                        in_=ccol[:, 5:6]).then_inc(S["ha"], 1)
            if DBG_L1 >= 1:
                return
            s_.wait_ge(S["tq"], 2)
            s_.copy(out=qbuf[:, 0:512], in_=P[4][0:64, 0:512])
            s_.copy(out=qbuf[:, 512:1024], in_=P[5][0:64, 0:512]).then_inc(S["sq2"], 1)
            s_.wait_ge(S["tr"], 2)
            s_.copy(out=rqb[:, 0:512], in_=P[6][0:48, 0:512])
            s_.copy(out=rqb[:, 512:1024], in_=P[7][0:48, 0:512]).then_inc(S["sr"], 1)
            s_.wait_ge(S["lf"], 16 * NFFN)
            for t in range(8):
                s_.wait_ge(S["vtk"], t + 1)
                s_.activation(out=gb[:, 1024:2048], in_=gb[:, 0:1024], func=AF.Exp,
                              bias=nmx[:, 0:1], scale=1.0,
                              accum_out=zsc[:]).then_inc(S["sex"], 1)
                if t >= 1:
                    s_.wait_ge(S["vg1"], t)
                    s_.activation(out=gx2[:], in_=gx1[:], func=AF.Tanh,
                                  scale=0.7978845608028654).then_inc(S["st"], 1)
                s_.wait_ge(S["tzc"], 2 * t + 2)
                if t >= 1:
                    s_.wait_ge(S["vg"], t)
                s_.activation(out=zbt[:, 0:512], in_=P[t % 2][:, 0:512],
                              func=AF.Identity, bias=bu1b[:, t:t + 1], scale=1.0)
                s_.activation(out=zbt[:, 512:1024], in_=P[2 + t % 2][:, 0:512],
                              func=AF.Identity, bias=bu1b[:, t:t + 1],
                              scale=1.0).then_inc(S["szb"], 1)
                s_.wait_ge(S["tpT"], t + 1)
                s_.copy(out=Abuf[:], in_=P[6][:].bitcast(BF16)[:, 0:1024]
                        ).then_inc(S["spT"], 1)
                s_.wait_ge(S["ta1"], t + 1)
                s_.copy(out=a1b[:], in_=P[7][:, 0:256]).then_inc(S["sa1"], 1)
                s_.wait_ge(S["ta1t"], t + 1)
                s_.copy(out=a1Tb[:], in_=P[6][:].bitcast(BF16)[:, 0:256]
                        ).then_inc(S["sa1t"], 1)
                for hs in range(8):
                    s_.wait_ge(S["ta2"], 8 * t + hs + 1)
                    if hs == 0 and t >= 1:
                        s_.wait_ge(S["ya2"], 16 * t)
                        s_.wait_ge(S["vst"], t)
                    s_.activation(out=a2t[:, 512 * hs:512 * (hs + 1)],
                                  in_=P[hs % 2][:, 0:512], func=AF.Identity,
                                  accum_out=apart[:, hs:hs + 1]
                                  ).then_inc(S["va2"], 1)
            s_.wait_ge(S["vg1"], 8)
            s_.activation(out=gx2[:], in_=gx1[:], func=AF.Tanh,
                          scale=0.7978845608028654).then_inc(S["st"], 1)
            s_.wait_ge(S["tls"], 2)
            s_.copy(out=learn8[:], in_=P[4][:, 0:8]).then_inc(S["sl"], 2)
    return nc


def build_l2():
    nc = bass.Bass("TRN2", target_bir_lowering=False, debug=False)
    i_h = nc.dram_tensor("i_h", [8, 128, 4096], BF16, kind="ExternalInput").ap()
    i_a2 = nc.dram_tensor("i_a2", [8, 128, 4096], BF16, kind="ExternalInput").ap()
    i_cols = nc.dram_tensor("i_cols", [128, 8, 8], F32, kind="ExternalInput").ap()
    i_sums = nc.dram_tensor("i_sums", [1, 8], F32, kind="ExternalInput").ap()
    i_gw = nc.dram_tensor("i_gw", [1, 1], F32, kind="ExternalInput").ap()
    i_gb = nc.dram_tensor("i_gb", [1, 1], F32, kind="ExternalInput").ap()
    o_out = nc.dram_tensor("o_out", [8, 128, 4096], F32, kind="ExternalOutput").ap()

    with ExitStack() as ctx:
        e = ctx.enter_context
        hb = e(nc.sbuf_tensor("hb", [128, 2 * 4096], BF16))
        ab = e(nc.sbuf_tensor("ab", [128, 2 * 4096], BF16))
        tmpb = e(nc.sbuf_tensor("tmpb", [128, 4096], BF16))
        outb = e(nc.sbuf_tensor("outb", [128, 2 * 4096], F32))
        colsb = e(nc.sbuf_tensor("colsb", [128, 64], F32))
        sumsb = e(nc.sbuf_tensor("sumsb", [1, 8], F32))
        g3 = e(nc.sbuf_tensor("g3", [1, 4], F32))
        onesf = e(nc.sbuf_tensor("onesf", [1, 128], F32))
        bc3 = e(nc.sbuf_tensor("bc3", [128, 4], F32))
        cc = e(nc.sbuf_tensor("cc", [128, 16], F32))
        gcol = e(nc.sbuf_tensor("gcol", [128, 8], F32))
        cst2 = e(nc.sbuf_tensor("cst2", [128, 1], F32))
        dmv = e(nc.sbuf_tensor("dmv", [1, 4], F32))
        dms = e(nc.sbuf_tensor("dms", [1, 4], F32))
        rcol = e(nc.sbuf_tensor("rcol", [128, 8], F32))
        ncol = e(nc.sbuf_tensor("ncol", [128, 8], F32))
        P = [e(nc.psum_tensor(f"P{i}", [128, 512], F32)) for i in range(2)]

        sem_names = ["ld", "lh0", "lh1", "la0", "la1", "vb", "sb", "tb", "ve", "se", "so0", "so1", "os", "vch", "sch"]
        S = {n: e(nc.semaphore(n)) for n in sem_names}
        block = e(nc.Block())

        @block.gpsimd
        def _(g):
            g.dma_start(out=colsb[:].rearrange("p (t c) -> p t c", c=8),
                        in_=i_cols).then_inc(S["ld"], 16)
            g.dma_start(out=sumsb[:], in_=i_sums).then_inc(S["ld"], 16)
            g.dma_start(out=g3[0:1, 1:2], in_=i_gw).then_inc(S["ld"], 16)
            g.dma_start(out=g3[0:1, 2:3], in_=i_gb).then_inc(S["ld"], 16)
            for t in range(8):
                if t >= 2:
                    g.wait_ge(S["ve"], 3 * (t - 2) + 3)
                g.dma_start(out=ab[:, 4096 * (t % 2):4096 * (t % 2 + 1)],
                            in_=i_a2[t]).then_inc(S[f"la{t % 2}"], 16)
                if t >= 2:
                    g.wait_ge(S["se"], t - 1)
                    g.dma_start(out=o_out[t - 2],
                                in_=outb[:, 4096 * (t % 2):4096 * (t % 2 + 1)]
                                ).then_inc(S[f"so{t % 2}"], 16)
            for t in range(6, 8):
                g.wait_ge(S["se"], t + 1)
                g.dma_start(out=o_out[t],
                            in_=outb[:, 4096 * (t % 2):4096 * (t % 2 + 1)]
                            ).then_inc(S[f"so{t % 2}"], 16)
            g.wait_ge(S["so0"], 16 * 4)
            g.wait_ge(S["so1"], 16 * 4)

        @block.sync
        def _(sy):
            for t in range(8):
                if t >= 2:
                    sy.wait_ge(S["ve"], 3 * (t - 2) + 3)
                sy.dma_start(out=hb[:, 4096 * (t % 2):4096 * (t % 2 + 1)],
                             in_=i_h[t]).then_inc(S[f"lh{t % 2}"], 16)

        @block.vector
        def _(v):
            vch = _Chain(v, S["vch"])
            v.memset(onesf[:], 1.0)
            v.memset(dmv[:], 0.0)
            v.memset(g3[0:1, 3:4], 0.0)
            v.memset(cst2[:], 1e-5)
            v.wait_ge(S["ld"], 64)
            vch.step(v.reduce_sum(out=cc[0:1, 0:1], in_=sumsb[:], axis=AX.X))
            vch.step(v.tensor_scalar(out=cc[0:1, 1:2], in0=cc[0:1, 0:1],
                                     scalar1=1.0 / 8192, scalar2=1e-6,
                                     op0=ALU.mult, op1=ALU.add))
            v.reciprocal(out=g3[0:1, 0:1], in_=cc[0:1, 1:2]).then_inc(S["vb"], 1)
            for t in range(8):
                v.wait_ge(S["sb"], 1)
                c0 = colsb[:, 8 * t:8 * t + 1]
                c2 = colsb[:, 8 * t + 2:8 * t + 3]
                c3 = colsb[:, 8 * t + 3:8 * t + 4]
                c4 = colsb[:, 8 * t + 4:8 * t + 5]
                c5 = colsb[:, 8 * t + 5:8 * t + 6]
                c6 = colsb[:, 8 * t + 6:8 * t + 7]
                vch.step(v.scalar_tensor_tensor(out=cc[:, 0:1], in0=c0,
                                       scalar=bc3[:, 0:1], in1=c0,
                                       op0=ALU.mult, op1=ALU.bypass))
                v.wait_ge(S["sb"], 3 * t + 2)
                vch.step(v.tensor_scalar(out=cc[:, 2:3], in0=cc[:, 1:2], scalar1=2.5,
                                scalar2=None, op0=ALU.mult))
                vch.step(v.scalar_tensor_tensor(out=cc[:, 3:4], in0=cc[:, 0:1], scalar=0.5,
                                       in1=cc[:, 2:3], op0=ALU.mult, op1=ALU.add))
                vch.step(v.tensor_scalar(out=cc[:, 4:5], in0=cc[:, 3:4], scalar1=0.0,
                                scalar2=5.0, op0=ALU.max, op1=ALU.min))
                vch.step(v.tensor_scalar(out=cc[:, 5:6], in0=cc[:, 4:5], scalar1=-0.5,
                                scalar2=1.0 / 1.5, op0=ALU.add, op1=ALU.mult))
                v.tensor_scalar(out=cc[:, 6:7], in0=cc[:, 5:6], scalar1=0.0,
                                scalar2=1.0, op0=ALU.max,
                                op1=ALU.min).then_inc(S["ve"], 1)
                v.wait_ge(S["sb"], 3 * t + 3)
                vch.step(v.tensor_scalar(out=cc[:, 8:9], in0=cc[:, 7:8], scalar1=0.05,
                                scalar2=None, op0=ALU.is_ge))
                vch.step(v.tensor_tensor(out=gcol[:, t:t + 1], in0=cc[:, 7:8],
                                in1=cc[:, 8:9], op=ALU.mult))
                vch.step(v.scalar_tensor_tensor(out=cc[:, 9:10], in0=c2,
                                       scalar=gcol[:, t:t + 1], in1=c5,
                                       op0=ALU.mult, op1=ALU.add))
                vch.step(v.tensor_tensor(out=cc[:, 11:12], in0=gcol[:, t:t + 1],
                                in1=gcol[:, t:t + 1], op=ALU.mult))
                vch.step(v.tensor_scalar(out=cc[:, 10:11], in0=cc[:, 9:10],
                                scalar1=1.0 / 4096, scalar2=None, op0=ALU.mult))
                vch.step(v.scalar_tensor_tensor(out=cc[:, 13:14], in0=c3,
                                       scalar=gcol[:, t:t + 1], in1=c3,
                                       op0=ALU.mult, op1=ALU.bypass))
                vch.step(v.scalar_tensor_tensor(out=cc[:, 12:13], in0=c4,
                                       scalar=cc[:, 11:12], in1=c4,
                                       op0=ALU.mult, op1=ALU.bypass))
                vch.step(v.scalar_tensor_tensor(out=cc[:, 14:15], in0=cc[:, 13:14],
                                       scalar=2.0, in1=cc[:, 12:13],
                                       op0=ALU.mult, op1=ALU.add))
                vch.step(v.scalar_tensor_tensor(out=cc[:, 15:16], in0=c6,
                                       scalar=4096.0, in1=cc[:, 14:15],
                                       op0=ALU.mult, op1=ALU.add))
                vch.step(v.tensor_tensor(out=cc[:, 13:14], in0=cc[:, 10:11],
                                in1=cc[:, 10:11], op=ALU.mult))
                vch.step(v.tensor_scalar(out=cc[:, 12:13], in0=cc[:, 15:16],
                                scalar1=1.0 / 4096, scalar2=None, op0=ALU.mult))
                v.tensor_sub(out=cc[:, 14:15], in0=cc[:, 12:13],
                             in1=cc[:, 13:14]).then_inc(S["ve"], 1)
                v.wait_ge(S["sb"], 3 * t + 4)
                vch.step(v.scalar_tensor_tensor(out=ncol[:, t:t + 1], in0=cc[:, 10:11],
                                       scalar=-1.0, in1=rcol[:, t:t + 1],
                                       op0=ALU.mult, op1=ALU.mult))
                v.wait_ge(S[f"lh{t % 2}"], 16 * (t // 2 + 1))
                v.wait_ge(S[f"la{t % 2}"], 16 * (t // 2 + 1))
                if t >= 1:
                    v.wait_ge(S["se"], t)
                v.scalar_tensor_tensor(
                    out=tmpb[:], in0=ab[:, 4096 * (t % 2):4096 * (t % 2 + 1)],
                    scalar=gcol[:, t:t + 1],
                    in1=hb[:, 4096 * (t % 2):4096 * (t % 2 + 1)],
                    op0=ALU.mult, op1=ALU.add).then_inc(S["ve"], 1)

        @block.tensor
        def _(t_):
            t_.wait_ge(S["vb"], 1)
            t_.matmul(out=P[0][:, 0:4], lhsT=onesf[:], rhs=g3[0:1, 0:4],
                      start=True, stop=True).then_inc(S["tb"], 1)

        @block.scalar
        def _(s_):
            sch = _Chain(s_, S["sch"])
            s_.wait_ge(S["tb"], 1)
            s_.copy(out=bc3[:], in_=P[0][:, 0:4]).then_inc(S["sb"], 1)
            for t in range(8):
                s_.wait_ge(S["ld"], 64)
                s_.activation(out=cc[:, 1:2], in_=colsb[:, 8 * t + 1:8 * t + 2],
                              func=AF.Sigmoid).then_inc(S["sb"], 1)
                s_.wait_ge(S["ve"], 3 * t + 1)
                s_.activation(out=cc[:, 7:8], in_=cc[:, 6:7], func=AF.Sigmoid,
                              scale=bc3[:, 1:2], bias=bc3[:, 2:3]
                              ).then_inc(S["sb"], 1)
                s_.wait_ge(S["ve"], 3 * t + 2)
                sch.step(s_.activation(out=cc[:, 15:16], in_=cc[:, 14:15],
                                       func=AF.Ln, bias=cst2[:, 0:1], scale=1.0))
                s_.activation(out=rcol[:, t:t + 1], in_=cc[:, 15:16], func=AF.Exp,
                              scale=-0.5).then_inc(S["sb"], 1)
                s_.wait_ge(S["ve"], 3 * t + 3)
                if t >= 2:
                    s_.wait_ge(S[f"so{t % 2}"], 16 * (t // 2))
                s_.activation(out=outb[:, 4096 * (t % 2):4096 * (t % 2 + 1)],
                              in_=tmpb[:], func=AF.Identity,
                              scale=rcol[:, t:t + 1], bias=ncol[:, t:t + 1]
                              ).then_inc(S["se"], 1)
    return nc


def _l1_in_maps(inputs, asm, consts):
    idt = consts[0]
    x = np.asarray(inputs["L0_attn_data"], dtype=np.float32).reshape(8192, 4096)
    bias = np.asarray(inputs["L0_attn_bias"], dtype=np.float32)
    log_rel = np.asarray(inputs["L0_ffn_log_rel"], dtype=np.float32)
    bu1 = np.asarray(inputs["L0_ffn_bu1"], dtype=np.float32)
    wu2 = np.asarray(inputs["L0_ffn_Wu2"], dtype=np.float32)
    bu2 = np.asarray(inputs["L0_ffn_bu2"], dtype=np.float32)
    logrel_exp = np.ascontiguousarray(
        np.broadcast_to(log_rel[None, :], (128, 1024)))
    shared = {
        "w_l1": asm["w_l1"], "wu1t": asm["wu1t_l1"], "i_vd": asm["vd_l1"],
        "i_wvut": asm["wvut_l1"], "i_wqt": asm["wqt_l1"],
        "i_keyst": np.ascontiguousarray(asm["keysT"]),
        "i_rkt": np.ascontiguousarray(asm["rkT"]),
        "i_wrt": np.ascontiguousarray(asm["wrT"]),
        "i_logrel": logrel_exp,
        "bias_r": np.ascontiguousarray(bias[None, :]),
        "i_bu1": np.ascontiguousarray(bu1.reshape(8, 128).T),
        "i_wu2t": np.ascontiguousarray(wu2.reshape(8, 128).T),
        "i_bu2": bu2.reshape(1, 1),
        "idt": idt,
    }
    maps = []
    for c in range(NC):
        xs = x[T * c:T * (c + 1)]
        m = dict(shared)
        m["x_rows"] = np.ascontiguousarray(xs.reshape(8, 128, 4096))
        m["x_cols"] = np.ascontiguousarray(
            xs.reshape(8, 128, NSL, SLW).transpose(2, 1, 0, 3))
        maps.append(m)
    return maps


def _l2_in_maps(inputs, res1):
    gw = np.asarray(inputs["L0_ffn_gate_w1"], dtype=np.float32).reshape(1, 1)
    gb = np.asarray(inputs["L0_ffn_gate_bias"], dtype=np.float32).reshape(1, 1)
    sums = np.ascontiguousarray(
        np.concatenate([np.asarray(res1[c]["o_lsum"]).reshape(1, 1)
                        for c in range(NC)], axis=1))
    maps = []
    for c in range(NC):
        cols = np.asarray(res1[c]["o_cols"]).reshape(8, 128, 8)
        maps.append({
            "i_h": np.asarray(res1[c]["o_h"]).reshape(8, 128, 4096),
            "i_a2": np.asarray(res1[c]["o_a2"]).reshape(8, 128, 4096),
            "i_cols": np.ascontiguousarray(cols.transpose(1, 0, 2)),
            "i_sums": sums,
            "i_gw": gw, "i_gb": gb,
        })
    return maps


def kernel(**inputs):
    consts = _consts()
    nc0 = build_l0()
    res0 = run_bass_kernel_spmd(nc0, _l0_in_maps(inputs, consts),
                                core_ids=list(range(NC)))
    asm = _l0_assemble(res0.results)
    nc1 = build_l1()
    res1 = run_bass_kernel_spmd(nc1, _l1_in_maps(inputs, asm, consts),
                                core_ids=list(range(NC)))
    nc2 = build_l2()
    res2 = run_bass_kernel_spmd(nc2, _l2_in_maps(inputs, res1.results),
                                core_ids=list(range(NC)))
    out = np.concatenate(
        [np.asarray(res2.results[c]["o_out"]).reshape(T, D) for c in range(NC)],
        axis=0)
    return np.ascontiguousarray(out.reshape(2, 4096, 4096), dtype=np.float32)


# version 04
# speedup vs baseline: 1.0375x; 1.0375x over previous
import numpy as np
import ml_dtypes
from contextlib import ExitStack

import concourse.bass as bass
import concourse.mybir as mybir
from concourse.bass_utils import run_bass_kernel_spmd
from concourse.bass_types import AP

F32 = mybir.dt.float32
BF16 = mybir.dt.bfloat16
I32 = mybir.dt.int32
I8 = mybir.dt.int8
AF = mybir.ActivationFunctionType
ALU = mybir.AluOpType
AX = mybir.AxisListType
BF = ml_dtypes.bfloat16

NC = 8
D = 4096
T = 1024
NS = 1024
VB = 256
BOT = 64
RD = 48
HID4 = 1024
KS = D // NC

class _Chain:

    def __init__(self, eng, sem):
        self.eng = eng
        self.sem = sem
        self.n = 0

    def step(self, instr):
        instr.then_inc(self.sem, 1)
        self.n += 1
        self.eng.wait_ge(self.sem, self.n)
        return instr


RSQRT_RD = 1.0 / np.sqrt(48.0)
NEGBIG = -1e30


def _consts():
    idt = np.eye(128, dtype=BF)
    e4c = np.zeros((128, 4), dtype=BF)
    for q in range(4):
        e4c[q * 32:(q + 1) * 32, q] = 1
    place = np.zeros((4, 4096), dtype=np.float32)
    for t in range(32):
        for q in range(4):
            place[q, 128 * t + 4 * t + q] = 1
    ks16 = np.zeros((16, 512), dtype=BF)
    for j in range(4):
        for i in range(128):
            ks16[4 * j + i // 32, 128 * j + i] = 1
    idtf = np.eye(128, dtype=np.float32)
    return idt, e4c, place, ks16, idtf


def build_l0():
    nc = bass.Bass("TRN2", target_bir_lowering=False, debug=False)
    m_cols = nc.dram_tensor("m_cols", [128, 32, 512], I8, kind="ExternalInput").ap()
    w_cols = nc.dram_tensor("w_cols", [32, 128, 512], F32, kind="ExternalInput").ap()
    v_rows = nc.dram_tensor("v_rows", [128, 4096], F32, kind="ExternalInput").ap()
    wvd = nc.dram_tensor("wvd", [128, 2, 4096], F32, kind="ExternalInput").ap()
    wu1_cols = nc.dram_tensor("wu1_cols", [128, 8, 512], F32, kind="ExternalInput").ap()
    wvu_rows = nc.dram_tensor("wvu_rows", [128, 4, 256], F32, kind="ExternalInput").ap()
    wq_cols = nc.dram_tensor("wq_cols", [64, 512], F32, kind="ExternalInput").ap()
    keys_rows = nc.dram_tensor("keys_rows", [128, 64], F32, kind="ExternalInput").ap()
    wr = nc.dram_tensor("wr", [48, 64], F32, kind="ExternalInput").ap()
    idt = nc.dram_tensor("idt", [128, 128], BF16, kind="ExternalInput").ap()
    e4c = nc.dram_tensor("e4c", [128, 4], BF16, kind="ExternalInput").ap()
    idtf = nc.dram_tensor("idtf", [128, 128], F32, kind="ExternalInput").ap()
    place = nc.dram_tensor("place", [4, 4096], F32, kind="ExternalInput").ap()
    ks16 = nc.dram_tensor("ks16", [16, 512], BF16, kind="ExternalInput").ap()
    o_wtm = nc.dram_tensor("o_wtm", [512, 4096], BF16, kind="ExternalOutput").ap()
    o_vd = nc.dram_tensor("o_vd", [128, 256], BF16, kind="ExternalOutput").ap()
    o_wu1t = nc.dram_tensor("o_wu1t", [512, 1024], BF16, kind="ExternalOutput").ap()
    o_wvut = nc.dram_tensor("o_wvut", [256, 512], BF16, kind="ExternalOutput").ap()
    o_wqt = nc.dram_tensor("o_wqt", [512, 64], BF16, kind="ExternalOutput").ap()
    o_keyst = nc.dram_tensor("o_keyst", [64, 128], BF16, kind="ExternalOutput").ap()
    o_rk = nc.dram_tensor("o_rk", [48, 128], BF16, kind="ExternalOutput").ap()
    o_wrt = nc.dram_tensor("o_wrt", [64, 48], BF16, kind="ExternalOutput").ap()

    with ExitStack() as ctx:
        e = ctx.enter_context
        s_mask = e(nc.sbuf_tensor("s_mask", [128, 32 * 512], BF16))
        s_wstg = e(nc.sbuf_tensor("s_wstg", [128, 8 * 512], F32))
        s_idtf = e(nc.sbuf_tensor("s_idtf", [128, 128], F32))
        s_v = e(nc.sbuf_tensor("s_v", [128, 4096], BF16))
        s_wvd = e(nc.sbuf_tensor("s_wvd", [128, 2 * 4096], BF16))
        s_wu1 = e(nc.sbuf_tensor("s_wu1", [128, 8 * 512], BF16))
        s_wvu = e(nc.sbuf_tensor("s_wvu", [128, 4 * 256], BF16))
        s_wq = e(nc.sbuf_tensor("s_wq", [64, 512], BF16))
        s_keys = e(nc.sbuf_tensor("s_keys", [128, 64], BF16))
        s_wr = e(nc.sbuf_tensor("s_wr", [48, 64], BF16))
        s_idt = e(nc.sbuf_tensor("s_idt", [128, 128], BF16))
        s_e4c = e(nc.sbuf_tensor("s_e4c", [128, 4], BF16))
        s_place = e(nc.sbuf_tensor("s_place", [4, 4096], F32))
        s_ks16 = e(nc.sbuf_tensor("s_ks16", [16, 512], BF16))

        s2 = e(nc.sbuf_tensor("s2", [4, 2 * 16], F32))
        bm01 = e(nc.sbuf_tensor("bm01", [128, 16], BF16))
        bmt = e(nc.sbuf_tensor("bmt", [16, 128], BF16))
        bmte = e(nc.sbuf_tensor("bmte", [128, 512], F32))
        wtm = e(nc.sbuf_tensor("wtm", [128, 4 * 4096], BF16))
        vt = e(nc.sbuf_tensor("vt", [128, 32 * 128], BF16))
        wvdt = e(nc.sbuf_tensor("wvdt", [128, 32 * 256], BF16))
        s_vd = e(nc.sbuf_tensor("s_vd", [128, 256], BF16))
        wu1t = e(nc.sbuf_tensor("wu1t", [128, 4 * 1024], BF16))
        wvut = e(nc.sbuf_tensor("wvut", [128, 2 * 512], BF16))
        wqt = e(nc.sbuf_tensor("wqt", [128, 4 * 64], BF16))
        keyst = e(nc.sbuf_tensor("keyst", [64, 128], BF16))
        keystu = e(nc.sbuf_tensor("keystu", [64, 128], BF16))
        s_rk = e(nc.sbuf_tensor("s_rk", [48, 128], BF16))
        s_wrt = e(nc.sbuf_tensor("s_wrt", [64, 48], BF16))

        P = [e(nc.psum_tensor(f"P{i}", [128, 512], F32)) for i in range(8)]

        ld = e(nc.semaphore("ld"))
        tb = e(nc.semaphore("tb"))
        tp = e(nc.semaphore("tp"))
        vb = e(nc.semaphore("vb"))
        tw = e(nc.semaphore("tw"))
        ve = e(nc.semaphore("ve"))
        sb = e(nc.semaphore("sb"))
        tx = e(nc.semaphore("tx"))
        so = e(nc.semaphore("so"))
        LW = [e(nc.semaphore(f"lw{k}")) for k in range(8)]
        LG = {n: e(nc.semaphore(n)) for n in
              ["lda", "ldb", "ldc", "ldd", "lde", "ldf", "ldg", "ldh", "ldi"]}
        block = e(nc.Block())

        NLD = 13

        @block.gpsimd
        def _(g):
            g.dma_start(out=s_e4c[:], in_=e4c).then_inc(LG["lda"], 16)
            g.dma_start(out=s_mask[:].rearrange("p (t k) -> p t k", k=512),
                        in_=m_cols).then_inc(LG["lda"], 16)
            g.dma_start(out=s_place[:], in_=place).then_inc(LG["ldb"], 16)
            g.dma_start(out=s_idt[:], in_=idt).then_inc(LG["ldc"], 16)
            g.dma_start(out=s_ks16[:], in_=ks16).then_inc(LG["ldd"], 16)
            g.dma_start(out=s_idtf[:], in_=idtf).then_inc(LG["lde"], 16)
            g.dma_start(out=s_v[:], in_=v_rows).then_inc(LG["ldf"], 16)
            g.dma_start(out=s_wvd[:].rearrange("p (t k) -> p t k", k=4096),
                        in_=wvd).then_inc(LG["ldf"], 16)
            g.dma_start(out=s_wu1[:].rearrange("p (t k) -> p t k", k=512),
                        in_=wu1_cols).then_inc(LG["ldg"], 16)
            g.dma_start(out=s_wvu[:].rearrange("p (t k) -> p t k", k=256),
                        in_=wvu_rows).then_inc(LG["ldh"], 16)
            g.dma_start(out=s_wq[:], in_=wq_cols).then_inc(LG["ldi"], 16)
            g.dma_start(out=s_keys[:], in_=keys_rows).then_inc(LG["ldi"], 16)
            g.dma_start(out=s_wr[:], in_=wr).then_inc(LG["ldi"], 16)

        @block.tensor
        def _(t):
            t.wait_ge(LG["lda"], 32)
            for i in range(32):
                if i >= 2:
                    t.wait_ge(vb, i - 1)
                t.matmul(out=P[i % 2][0:4, :], lhsT=s_e4c[:],
                         rhs=s_mask[:, 512 * i:512 * (i + 1)],
                         start=True, stop=True).then_inc(tb, 1)
                if i >= 1:
                    if i == 1:
                        t.wait_ge(LG["ldb"], 16)
                    t.wait_ge(vb, i)
                    t.matmul(out=P[2][:, 0:16],
                             lhsT=s_place[:, 128 * (i - 1):128 * i],
                             rhs=s2[0:4, 16 * ((i - 1) % 2):16 * ((i - 1) % 2) + 16],
                             start=(i - 1 == 0), stop=False).then_inc(tp, 1)
            t.wait_ge(vb, 32)
            t.matmul(out=P[2][:, 0:16], lhsT=s_place[:, 128 * 31:128 * 32],
                     rhs=s2[0:4, 16 * (31 % 2):16 * (31 % 2) + 16],
                     start=False, stop=True).then_inc(tp, 1)
            t.wait_ge(vb, 33)
            t.wait_ge(LG["ldc"], 16)
            t.transpose(out=P[3][:].bitcast(BF16)[0:16, 0:128], in_=bm01[:],
                        identity=s_idt[:]).then_inc(tx, 1)
            t.wait_ge(sb, 1)
            t.wait_ge(LG["ldd"], 16)
            for j in range(4):
                mm = t.matmul(out=P[4][:, 128 * j:128 * (j + 1)],
                              lhsT=s_ks16[:, 128 * j:128 * (j + 1)], rhs=bmt[:],
                              start=True, stop=True)
            mm.then_inc(tx, 1)
            t.wait_ge(sb, 2)
            t.wait_ge(LG["lde"], 16)
            for i in range(32):
                t.wait_ge(LW[i % 8], 16 * (i // 8 + 1))
                if i >= 2:
                    t.wait_ge(ve, 4 * (i - 1))
                for j in range(4):
                    mm = t.transpose(out=P[i % 2][:, 128 * j:128 * (j + 1)],
                                in_=s_wstg[:, 512 * (i % 8) + 128 * j:512 * (i % 8) + 128 * (j + 1)],
                                identity=s_idtf[:])
                mm.then_inc(tw, 1)
            t.wait_ge(ve, 128)
            t.wait_ge(LG["ldf"], 32)
            for m in range(32):
                if m >= 2:
                    t.wait_ge(sb, 2 + (m - 1))
                t.transpose(out=P[m % 2][:].bitcast(BF16)[:, 0:128],
                            in_=s_v[:, 128 * m:128 * (m + 1)], identity=s_idt[:])
                for v in range(2):
                    mm = t.transpose(out=P[m % 2][:].bitcast(BF16)[:, 128 + 128 * v:256 + 128 * v],
                                in_=s_wvd[:, 4096 * v + 128 * m:4096 * v + 128 * (m + 1)],
                                identity=s_idt[:])
                mm.then_inc(tw, 1)
            t.wait_ge(sb, 2 + 32)
            for m in range(32):
                mm = t.matmul(out=P[5][:, 0:256], lhsT=vt[:, 128 * m:128 * (m + 1)],
                         rhs=wvdt[:, 256 * m:256 * (m + 1)],
                         start=(m == 0), stop=(m == 31))
            mm.then_inc(tx, 1)
            t.wait_ge(LG["ldg"], 16)
            cnt = 0
            for j in range(4):
                for tt in range(8):
                    slot = cnt % 8
                    if cnt >= 8 and cnt % 8 == 0:
                        t.wait_ge(sb, 2 + 32 + cnt // 8)
                    mm = t.transpose(out=P[6][:].bitcast(BF16)[:, 128 * slot:128 * (slot + 1)],
                                in_=s_wu1[:, 512 * tt + 128 * j:512 * tt + 128 * (j + 1)],
                                identity=s_idt[:])
                    cnt += 1
                    if cnt % 8 == 0:
                        mm.then_inc(tx, 1)
            t.wait_ge(LG["ldh"], 16)
            for v in range(2):
                for tt in range(4):
                    mm = t.transpose(out=P[7][:].bitcast(BF16)[:, 128 * (4 * v + tt):128 * (4 * v + tt + 1)],
                                in_=s_wvu[:, 256 * tt + 128 * v:256 * tt + 128 * (v + 1)],
                                identity=s_idt[:])
            mm.then_inc(tx, 1)
            t.wait_ge(LG["ldi"], 48)
            for tt in range(4):
                t.transpose(out=P[3][:].bitcast(BF16)[:, 64 * tt:64 * (tt + 1)],
                            in_=s_wq[0:64, 128 * tt:128 * (tt + 1)],
                            identity=s_idt[0:64, 0:64])
            t.transpose(out=P[3][:].bitcast(BF16)[0:64, 256:384], in_=s_keys[:],
                        identity=s_idt[:])
            mm = t.transpose(out=P[3][:].bitcast(BF16)[0:64, 384:432], in_=s_wr[0:48, 0:64],
                        identity=s_idt[0:48, 0:48])
            mm.then_inc(tx, 1)
            t.wait_ge(sb, 2 + 32 + 4 + 1 + 1 + 1)
            t.matmul(out=P[3][0:48, 256:384], lhsT=s_wrt[0:64, 0:48], rhs=keystu[:],
                     start=True, stop=True).then_inc(tx, 1)

        @block.vector
        def _(v):
            for i in range(32):
                v.wait_ge(tb, i + 1)
                if i >= 2:
                    v.wait_ge(tp, i - 1)
                v.tensor_reduce(
                    out=s2[0:4, 16 * (i % 2):16 * (i % 2) + 16],
                    in_=P[i % 2][0:4, :].rearrange("p (a b) -> p a b", b=32),
                    axis=AX.X, op=ALU.add).then_inc(vb, 1)
            v.wait_ge(tp, 32)
            v.tensor_scalar(out=bm01[:], in0=P[2][:, 0:16], scalar1=0.0, scalar2=None,
                            op0=ALU.is_gt).then_inc(vb, 1)
            for i in range(32):
                v.wait_ge(tw, i + 1)
                for j in range(4):
                    mv = bmte[:, 128 * j + 4 * i:128 * j + 4 * i + 4]
                    v.tensor_tensor(
                        out=wtm[:, 4096 * j + 128 * i:4096 * j + 128 * (i + 1)]
                            .rearrange("p (a b) -> p a b", b=32),
                        in0=P[i % 2][:, 128 * j:128 * (j + 1)]
                            .rearrange("p (a b) -> p a b", b=32),
                        in1=mv.unsqueeze(2).broadcast_to([128, 4, 32]),
                        op=ALU.mult).then_inc(ve, 1)

        @block.scalar
        def _(s):
            s.wait_ge(tx, 1)
            s.copy(out=bmt[:], in_=P[3][:].bitcast(BF16)[0:16, 0:128]).then_inc(sb, 1)
            s.wait_ge(tx, 2)
            s.copy(out=bmte[:], in_=P[4][:, 0:512]).then_inc(sb, 1)
            for m in range(32):
                s.wait_ge(tw, 32 + m + 1)
                s.copy(out=vt[:, 128 * m:128 * (m + 1)],
                       in_=P[m % 2][:].bitcast(BF16)[:, 0:128])
                s.copy(out=wvdt[:, 256 * m:256 * (m + 1)],
                       in_=P[m % 2][:].bitcast(BF16)[:, 128:384]).then_inc(sb, 1)
            s.wait_ge(tx, 3)
            s.copy(out=s_vd[:], in_=P[5][:, 0:256])
            for b in range(4):
                s.wait_ge(tx, 4 + b)
                s.copy(out=wu1t[:, 1024 * b:1024 * (b + 1)],
                       in_=P[6][:].bitcast(BF16)[:, 0:1024]).then_inc(sb, 1)
            s.wait_ge(tx, 8)
            s.copy(out=wvut[:], in_=P[7][:].bitcast(BF16)[:, 0:1024]).then_inc(sb, 1)
            s.wait_ge(tx, 9)
            s.copy(out=wqt[:], in_=P[3][:].bitcast(BF16)[:, 0:256])
            s.activation(out=keyst[:], in_=P[3][:].bitcast(BF16)[0:64, 256:384],
                         func=AF.Copy, scale=0.125)
            s.copy(out=keystu[:], in_=P[3][:].bitcast(BF16)[0:64, 256:384])
            s.copy(out=s_wrt[0:64, 0:48],
                   in_=P[3][:].bitcast(BF16)[0:64, 384:432]).then_inc(sb, 2)
            s.wait_ge(tx, 10)
            s.activation(out=s_rk[:], in_=P[3][0:48, 256:384], func=AF.Copy,
                         scale=float(RSQRT_RD)).then_inc(sb, 1)

        @block.sync
        def _(sy):
            for i in range(32):
                if i >= 8:
                    sy.wait_ge(tw, i - 7)
                sy.dma_start(out=s_wstg[:, 512 * (i % 8):512 * (i % 8 + 1)],
                             in_=w_cols[i]).then_inc(LW[i % 8], 16)
            sy.wait_ge(ve, 128)
            sy.dma_start(out=o_wtm.rearrange("(j k) n -> k j n", j=4),
                         in_=wtm[:].rearrange("p (j n) -> p j n", n=4096)).then_inc(so, 16)
            sy.wait_ge(sb, 2 + 32 + 4 + 1 + 1 + 1 + 1)
            sy.dma_start(out=o_vd, in_=s_vd[:]).then_inc(so, 16)
            sy.dma_start(out=o_wu1t.rearrange("(j k) u -> k j u", j=4),
                         in_=wu1t[:].rearrange("p (j u) -> p j u", j=4)).then_inc(so, 16)
            sy.dma_start(out=o_wvut.rearrange("(v k) h -> k v h", v=2),
                         in_=wvut[:].rearrange("p (v h) -> p v h", v=2)).then_inc(so, 16)
            sy.dma_start(out=o_wqt.rearrange("(t k) d -> k t d", t=4),
                         in_=wqt[:].rearrange("p (t d) -> p t d", t=4)).then_inc(so, 16)
            sy.dma_start(out=o_keyst, in_=keyst[:]).then_inc(so, 16)
            sy.dma_start(out=o_rk, in_=s_rk[:]).then_inc(so, 16)
            sy.dma_start(out=o_wrt, in_=s_wrt[0:64, 0:48]).then_inc(so, 16)
            sy.wait_ge(so, 16 * 8)
    return nc


def _l0_in_maps(inp, consts):
    idt, e4c, place, ks16, idtf = consts
    mask = np.asarray(inp["L0_attn_mask"]).view(np.int8)[:, ::4]
    w = np.asarray(inp["L0_attn_weight"], dtype=np.float32)
    V = np.asarray(inp["L0_ffn_aux_values"], dtype=np.float32)
    Wvd = np.asarray(inp["L0_ffn_Wvd"], dtype=np.float32)
    Wu1 = np.asarray(inp["L0_ffn_Wu1"], dtype=np.float32)
    Wvu = np.asarray(inp["L0_ffn_Wvu"], dtype=np.float32)
    Wq = np.asarray(inp["L0_ffn_Wq"], dtype=np.float32)
    keys = np.asarray(inp["L0_ffn_aux_keys"], dtype=np.float32)
    Wr = np.asarray(inp["L0_ffn_Wr"], dtype=np.float32)
    maps = []
    for c in range(NC):
        sl = slice(512 * c, 512 * (c + 1))
        maps.append({
            "m_cols": np.ascontiguousarray(
                mask[:, sl].reshape(32, 128, 512).transpose(1, 0, 2)),
            "w_cols": np.ascontiguousarray(w[:, sl].reshape(32, 128, 512)),
            "v_rows": np.ascontiguousarray(V[128 * c:128 * (c + 1), :]),
            "wvd": np.ascontiguousarray(Wvd.reshape(2, 128, 4096).transpose(1, 0, 2)),
            "wu1_cols": np.ascontiguousarray(
                Wu1[:, sl].reshape(8, 128, 512).transpose(1, 0, 2)),
            "wvu_rows": np.ascontiguousarray(
                Wvu[sl, :].reshape(4, 128, 256).transpose(1, 0, 2)),
            "wq_cols": np.ascontiguousarray(Wq[:, sl]),
            "keys_rows": np.ascontiguousarray(keys[128 * c:128 * (c + 1), :]),
            "wr": Wr,
            "idt": idt, "e4c": e4c, "place": place, "ks16": ks16,
            "idtf": idtf,
        })
    return maps


def _l0_assemble(res):
    wT = np.concatenate([np.asarray(res[c]["o_wtm"]) for c in range(NC)], axis=0)
    w_l1 = np.ascontiguousarray(
        wT.reshape(32, 128, NSL, SLW).transpose(2, 1, 0, 3))
    vd = np.concatenate([np.asarray(res[c]["o_vd"]) for c in range(NC)], axis=0)
    vd_l1 = np.ascontiguousarray(vd.reshape(8, 128, 256).transpose(1, 0, 2))
    wu1T = np.concatenate([np.asarray(res[c]["o_wu1t"]) for c in range(NC)], axis=0)
    wu1t_l1 = np.ascontiguousarray(
        wu1T.reshape(32, 128, 8, 128).transpose(2, 1, 0, 3))
    wvuT = np.concatenate([np.asarray(res[c]["o_wvut"]) for c in range(NC)], axis=1)
    wvut_l1 = np.ascontiguousarray(wvuT.reshape(2, 128, 4096).transpose(1, 0, 2))
    wqT = np.concatenate([np.asarray(res[c]["o_wqt"]) for c in range(NC)], axis=0)
    wqt_l1 = np.ascontiguousarray(wqT.reshape(32, 128, 64).transpose(1, 0, 2))
    keysT = np.concatenate([np.asarray(res[c]["o_keyst"]) for c in range(NC)], axis=1)
    rkT = np.concatenate([np.asarray(res[c]["o_rk"]) for c in range(NC)], axis=1)
    wrT = np.asarray(res[0]["o_wrt"])
    return dict(w_l1=w_l1, vd_l1=vd_l1, wu1t_l1=wu1t_l1, wvut_l1=wvut_l1,
                wqt_l1=wqt_l1, keysT=keysT, rkT=rkT, wrT=wrT)


DBG_L1 = 0
NSL = 16
SLW = 256
NFFN = 10


def build_l1():
    nc = bass.Bass("TRN2", target_bir_lowering=False, debug=False)
    x_rows = nc.dram_tensor("x_rows", [8, 128, 4096], F32, kind="ExternalInput").ap()
    x_cols = nc.dram_tensor("x_cols", [NSL, 128, 8, SLW], F32, kind="ExternalInput").ap()
    w_l1 = nc.dram_tensor("w_l1", [NSL, 128, 32, SLW], BF16, kind="ExternalInput").ap()
    wu1t = nc.dram_tensor("wu1t", [8, 128, 32, 128], BF16, kind="ExternalInput").ap()
    i_vd = nc.dram_tensor("i_vd", [128, 8, 256], BF16, kind="ExternalInput").ap()
    i_wvut = nc.dram_tensor("i_wvut", [128, 2, 4096], BF16, kind="ExternalInput").ap()
    i_wqt = nc.dram_tensor("i_wqt", [128, 32, 64], BF16, kind="ExternalInput").ap()
    i_keyst = nc.dram_tensor("i_keyst", [64, 1024], BF16, kind="ExternalInput").ap()
    i_rkt = nc.dram_tensor("i_rkt", [48, 1024], BF16, kind="ExternalInput").ap()
    i_wrt = nc.dram_tensor("i_wrt", [64, 48], BF16, kind="ExternalInput").ap()
    i_logrel = nc.dram_tensor("i_logrel", [128, 1024], F32, kind="ExternalInput").ap()
    bias_r = nc.dram_tensor("bias_r", [1, 4096], F32, kind="ExternalInput").ap()
    i_bu1 = nc.dram_tensor("i_bu1", [128, 8], F32, kind="ExternalInput").ap()
    i_wu2t = nc.dram_tensor("i_wu2t", [128, 8], F32, kind="ExternalInput").ap()
    i_bu2 = nc.dram_tensor("i_bu2", [1, 1], F32, kind="ExternalInput").ap()
    idt = nc.dram_tensor("idt", [128, 128], BF16, kind="ExternalInput").ap()

    o_h = nc.dram_tensor("o_h", [8, 128, 4096], BF16, kind="ExternalOutput").ap()
    o_a2 = nc.dram_tensor("o_a2", [8, 128, 4096], BF16, kind="ExternalOutput").ap()
    o_cols = nc.dram_tensor("o_cols", [8, 128, 8], F32, kind="ExternalOutput").ap()
    o_lsum = nc.dram_tensor("o_lsum", [1, 1], F32, kind="ExternalOutput").ap()
    if DBG_L1 >= 1:
        o_dbg = nc.dram_tensor("o_dbg", [2, 128, 4096], BF16, kind="ExternalOutput").ap()
        o_dbg2 = nc.dram_tensor("o_dbg2", [128, 8192], BF16, kind="ExternalOutput").ap()

    with ExitStack() as ctx:
        e = ctx.enter_context
        y = [e(nc.sbuf_tensor(f"y{t}", [128, 4096], BF16)) for t in range(8)]
        xT = e(nc.sbuf_tensor("xT", [128, 32 * 1024], BF16))
        colb = e(nc.sbuf_tensor("colb", [128, 8 * 8], F32))
        s_idt = e(nc.sbuf_tensor("s_idt", [128, 128], BF16))
        ones1 = e(nc.sbuf_tensor("ones1", [1, 128], BF16))
        onesc = e(nc.sbuf_tensor("onesc", [128, 1], BF16))
        id1f = e(nc.sbuf_tensor("id1f", [1, 1], F32))
        mpart = e(nc.sbuf_tensor("mpart", [128, 8 * NSL], F32))
        sqc = e(nc.sbuf_tensor("sqc", [128, 8], F32))
        vbuf = e(nc.sbuf_tensor("vbuf", [128, 8], F32))
        rbuf = e(nc.sbuf_tensor("rbuf", [128, 8], F32))
        nbuf = e(nc.sbuf_tensor("nbuf", [128, 8], F32))
        mbuf = e(nc.sbuf_tensor("mbuf", [128, 8], F32))
        ccol = e(nc.sbuf_tensor("ccol", [128, 16], F32))
        lv8 = e(nc.sbuf_tensor("lv8", [128, 8], BF16))
        lsum_sb = e(nc.sbuf_tensor("lsum_sb", [1, 8], F32))
        cst = e(nc.sbuf_tensor("cst", [128, 2], F32))
        dmv = e(nc.sbuf_tensor("dmv", [1, 4], F32))
        dms = e(nc.sbuf_tensor("dms", [1, 4], F32))
        P = [e(nc.psum_tensor(f"P{i}", [128, 512], F32)) for i in range(8)]

        sem_names = ["ld", "lx0", "lx1", "l20", "l21", "lw0", "lw1", "lf", "lz0", "lz1", "tt", "xe", "tm", "ae",
                     "sq", "va", "rc", "nm", "ha", "lv", "tls", "vls",
                     "tq", "sq2", "tr", "sr", "tzc", "szb", "vg1", "st", "vg",
                     "tw2", "sl", "trA", "vA", "vtk", "sex", "vR", "tpT", "spT",
                     "ta1", "sa1", "ta1t", "sa1t", "ta2", "va2", "ya2", "vst",
                     "vlc", "vini", "so", "vch", "sch", "tl2", "vw2", "vlr"]
        S = {n: e(nc.semaphore(n)) for n in sem_names}
        block = e(nc.Block())

        with ExitStack() as actx:
            ea = actx.enter_context
            xload = ea(nc.sbuf_tensor("xload", [128, 2 * 4096], BF16))
            biasb = ea(nc.sbuf_tensor("biasb", [1, 4096], BF16))
            x2b = ea(nc.sbuf_tensor("x2b", [128, 2 * 8 * SLW], BF16))
            wbuf = ea(nc.sbuf_tensor("wbuf", [128, 2 * 32 * SLW], BF16))

            @block.gpsimd
            def _(g):
                g.dma_start(out=s_idt[:], in_=idt).then_inc(S["ld"], 16)
                g.dma_start(out=biasb[:], in_=bias_r).then_inc(S["ld"], 16)
                for t in range(8):
                    if t >= 2:
                        g.wait_ge(S["xe"], 8 * (t - 1))
                    g.dma_start(out=xload[:, 4096 * (t % 2):4096 * (t % 2 + 1)],
                                in_=x_rows[t]).then_inc(S[f"lx{t % 2}"], 16)
                for s in range(NSL):
                    if s >= 2:
                        g.wait_ge(S["ae"], 8 * (s - 1))
                    g.dma_start(
                        out=x2b[:, 8 * SLW * (s % 2):8 * SLW * (s % 2 + 1)]
                            .rearrange("p (a b) -> p a b", b=SLW),
                        in_=x_cols[s]).then_inc(S[f"l2{s % 2}"], 16)

            @block.sync
            def _(sy):
                for s in range(NSL):
                    if s >= 2:
                        sy.wait_ge(S["ae"], 8 * (s - 1))
                    sy.dma_start(
                        out=wbuf[:, 32 * SLW * (s % 2):32 * SLW * (s % 2 + 1)]
                            .rearrange("p (a b) -> p a b", b=SLW),
                        in_=w_l1[s]).then_inc(S[f"lw{s % 2}"], 16)

            @block.tensor
            def _(t_):
                t_.wait_ge(S["ld"], 32)
                t_.wait_ge(S["vini"], 1)
                for t in range(8):
                    t_.wait_ge(S[f"lx{t % 2}"], 16 * (t // 2 + 1))
                    for b in range(8):
                        gi = t * 8 + b
                        if gi >= 2:
                            t_.wait_ge(S["xe"], gi - 1)
                        for j in range(4):
                            kc = 4 * b + j
                            mm = t_.transpose(
                                out=P[gi % 2][:].bitcast(BF16)[:, 128 * j:128 * (j + 1)],
                                in_=xload[:, 4096 * (t % 2) + 128 * kc:
                                          4096 * (t % 2) + 128 * (kc + 1)],
                                identity=s_idt[:])
                        mm.then_inc(S["tt"], 1)
                for s in range(NSL):
                    t_.wait_ge(S[f"lw{s % 2}"], 16 * (s // 2 + 1))
                    if s == 0:
                        t_.wait_ge(S["xe"], 64)
                    for t in range(8):
                        if s >= 1:
                            t_.wait_ge(S["ae"], 8 * (s - 1) + t + 1)
                        t_.matmul(out=P[t][:, 0:SLW], lhsT=ones1[:],
                                  rhs=biasb[0:1, SLW * s:SLW * (s + 1)],
                                  start=True, stop=False)
                        for kc in range(32):
                            mm = t_.matmul(
                                out=P[t][:, 0:SLW],
                                lhsT=xT[:, 1024 * kc + 128 * t:1024 * kc + 128 * (t + 1)],
                                rhs=wbuf[:, 32 * SLW * (s % 2) + SLW * kc:
                                         32 * SLW * (s % 2) + SLW * (kc + 1)],
                                start=False, stop=(kc == 31))
                        mm.then_inc(S["tm"], 1)

            @block.vector
            def _(v):
                v.memset(ones1[:], 1.0)
                v.memset(onesc[:], 1.0)
                v.memset(dmv[:], 0.0)
                v.memset(colb[:], 0.0)
                v.memset(cst[:, 0:1], 1e-5)
                v.memset(cst[:, 1:2], 1.0)
                v.memset(id1f[:], 1.0).then_inc(S["vini"], 1)
                for t in range(8):
                    for b in range(8):
                        gi = t * 8 + b
                        v.wait_ge(S["tt"], gi + 1)
                        dst = AP(xT, 4096 * b + 128 * t,
                                 [[32768, 128], [1024, 4], [1, 128]])
                        v.tensor_copy(out=dst,
                                      in_=P[gi % 2][:].bitcast(BF16)[:, 0:512]
                                      .rearrange("p (a c) -> p a c", c=128)
                                      ).then_inc(S["xe"], 1)
                for s in range(NSL):
                    v.wait_ge(S[f"l2{s % 2}"], 16 * (s // 2 + 1))
                    for t in range(8):
                        v.wait_ge(S["tm"], 8 * s + t + 1)
                        v.scalar_tensor_tensor(
                            out=y[t][:, SLW * s:SLW * (s + 1)],
                            in0=P[t][:, 0:SLW], scalar=1.0,
                            in1=x2b[:, 8 * SLW * (s % 2) + SLW * t:
                                    8 * SLW * (s % 2) + SLW * (t + 1)],
                            op0=ALU.mult, op1=ALU.add,
                            accum_out=mpart[:, NSL * t + s:NSL * t + s + 1],
                        ).then_inc(S["ae"], 1)
        w1b = e(nc.sbuf_tensor("w1b", [128, 2 * 4096], BF16))
        wvutb = e(nc.sbuf_tensor("wvutb", [128, 2 * 4096], BF16))
        vdb = e(nc.sbuf_tensor("vdb", [128, 8 * 256], BF16))
        wqtb = e(nc.sbuf_tensor("wqtb", [128, 32 * 64], BF16))
        keystb = e(nc.sbuf_tensor("keystb", [64, 1024], BF16))
        rktb = e(nc.sbuf_tensor("rktb", [48, 1024], BF16))
        wrtb = e(nc.sbuf_tensor("wrtb", [64, 48], BF16))
        logrelb = e(nc.sbuf_tensor("logrelb", [128, 1024], BF16))
        bu1b = e(nc.sbuf_tensor("bu1b", [128, 8], F32))
        wu2tb = e(nc.sbuf_tensor("wu2tb", [128, 8], BF16))
        bu2b = e(nc.sbuf_tensor("bu2b", [1, 1], F32))
        qbuf = e(nc.sbuf_tensor("qbuf", [64, 1024], BF16))
        rqb = e(nc.sbuf_tensor("rqb", [48, 1024], BF16))
        zbt = e(nc.sbuf_tensor("zbt", [128, 1024], BF16))
        gb = e(nc.sbuf_tensor("gb", [128, 2 * 1024], BF16))
        gx1 = e(nc.sbuf_tensor("gx1", [128, 1024], BF16))
        gx2 = e(nc.sbuf_tensor("gx2", [128, 1024], BF16))
        learn8 = e(nc.sbuf_tensor("learn8", [128, 8], F32))
        Rbuf = e(nc.sbuf_tensor("Rbuf", [128, 1024], F32))
        Abuf = e(nc.sbuf_tensor("Abuf", [128, 1024], BF16))
        top8 = e(nc.sbuf_tensor("top8", [128, 8], F32))
        tauc = e(nc.sbuf_tensor("tauc", [128, 1], F32))
        nbig = e(nc.sbuf_tensor("nbig", [128, 1], BF16))
        zsc = e(nc.sbuf_tensor("zsc", [128, 1], F32))
        rzc = e(nc.sbuf_tensor("rzc", [128, 1], F32))
        nmx = e(nc.sbuf_tensor("nmx", [128, 1], F32))
        a1b = e(nc.sbuf_tensor("a1b", [128, 256], BF16))
        a1Tb = e(nc.sbuf_tensor("a1Tb", [128, 256], BF16))
        a2t = e(nc.sbuf_tensor("a2t", [128, 4096], BF16))
        apart = e(nc.sbuf_tensor("apart", [128, 8], F32))
        learnacc = e(nc.sbuf_tensor("learnacc", [1, 1024], BF16))

        @block.gpsimd
        def _(g):
            if DBG_L1 >= 1:
                return
            g.wait_ge(S["ae"], 128)
            g.dma_start(out=vdb[:].rearrange("p (a b) -> p a b", b=256),
                        in_=i_vd).then_inc(S["lf"], 16)
            g.dma_start(out=wvutb[:].rearrange("p (a b) -> p a b", b=4096),
                        in_=i_wvut).then_inc(S["lf"], 16)
            g.dma_start(out=wqtb[:].rearrange("p (a b) -> p a b", b=64),
                        in_=i_wqt).then_inc(S["lf"], 16)
            g.dma_start(out=keystb[:], in_=i_keyst).then_inc(S["lf"], 16)
            g.dma_start(out=rktb[:], in_=i_rkt).then_inc(S["lf"], 16)
            g.dma_start(out=wrtb[:], in_=i_wrt).then_inc(S["lf"], 16)
            g.dma_start(out=logrelb[:], in_=i_logrel).then_inc(S["lf"], 16)
            g.dma_start(out=bu1b[:], in_=i_bu1).then_inc(S["lf"], 16)
            g.dma_start(out=wu2tb[:], in_=i_wu2t).then_inc(S["lf"], 16)
            g.dma_start(out=bu2b[:], in_=i_bu2).then_inc(S["lf"], 16)
            for uc in range(8):
                if uc >= 2:
                    g.wait_ge(S["szb"], uc - 1)
                g.dma_start(out=w1b[:, 4096 * (uc % 2):4096 * (uc % 2 + 1)]
                            .rearrange("p (a b) -> p a b", b=128),
                            in_=wu1t[uc]).then_inc(S[f"lz{uc % 2}"], 16)

        @block.sync
        def _(sy):
            for t in range(8):
                sy.wait_ge(S["ha"], t + 1)
                sy.wait_ge(S["so"], 16 * t)
                sy.dma_start(out=o_h[t], in_=y[t][:]).then_inc(S["so"], 16)
            if DBG_L1 >= 1:
                sy.wait_ge(S["xe"], 64)
                sy.dma_start(out=o_dbg2, in_=xT[:, 0:8192]).then_inc(S["so"], 16)
                sy.wait_ge(S["sq"], 1)
                sy.dma_start(out=o_dbg[0], in_=y[0][:]).then_inc(S["so"], 16)
                sy.wait_ge(S["ha"], 8)
                sy.dma_start(out=o_dbg[1], in_=y[0][:]).then_inc(S["so"], 16)
                sy.wait_ge(S["so"], 16 * 11)
                return
            for t in range(8):
                sy.wait_ge(S["va2"], 8 * (t + 1))
                sy.dma_start(out=o_a2[t], in_=a2t[:]).then_inc(S["ya2"], 16)
            sy.wait_ge(S["vst"], 8)
            sy.wait_ge(S["vlc"], 1)
            sy.wait_ge(S["ha"], 8)
            sy.wait_ge(S["nm"], 8)
            sy.dma_start(out=o_cols.rearrange("t p c -> p t c"),
                         in_=colb[:].rearrange("p (t c) -> p t c", c=8)
                         ).then_inc(S["so"], 16)
            sy.wait_ge(S["vls"], 1)
            sy.dma_start(out=o_lsum, in_=lsum_sb[0:1, 0:1]).then_inc(S["so"], 16)
            sy.wait_ge(S["so"], 16 * 10)
            sy.wait_ge(S["ya2"], 16 * 8)

        @block.tensor
        def _(t_):
            if DBG_L1 >= 1:
                return
            for t in range(8):
                t_.wait_ge(S["ha"], t + 1)
                for b in range(8):
                    gi = t * 8 + b
                    if gi >= 2:
                        t_.wait_ge(S["xe"], 64 + gi - 1)
                    for j in range(4):
                        kc = 4 * b + j
                        mm = t_.transpose(
                            out=P[2 + gi % 2][:].bitcast(BF16)[:, 128 * j:128 * (j + 1)],
                            in_=y[t][:, 128 * kc:128 * (kc + 1)],
                            identity=s_idt[:])
                    mm.then_inc(S["tt"], 1)
            t_.wait_ge(S["lv"], 1)
            t_.matmul(out=P[0][0:1, 0:8], lhsT=onesc[:], rhs=lv8[:],
                      start=True, stop=True).then_inc(S["tls"], 1)
            t_.wait_ge(S["xe"], 128)
            t_.wait_ge(S["lf"], 16 * NFFN)
            for hf in range(2):
                for kc in range(32):
                    mm = t_.matmul(
                        out=P[4 + hf][0:64, 0:512],
                        lhsT=wqtb[:, 64 * kc:64 * (kc + 1)],
                        rhs=xT[:, 1024 * kc + 512 * hf:1024 * kc + 512 * (hf + 1)],
                        start=(kc == 0), stop=(kc == 31))
                mm.then_inc(S["tq"], 1)
            t_.wait_ge(S["sq2"], 1)
            for hf in range(2):
                t_.matmul(out=P[6 + hf][0:48, 0:512], lhsT=wrtb[0:64, 0:48],
                          rhs=qbuf[:, 512 * hf:512 * (hf + 1)],
                          start=True, stop=True).then_inc(S["tr"], 1)
            t_.wait_ge(S["vls"], 1)
            t_.wait_ge(S["sr"], 1)
            for t in range(8):
                t_.wait_ge(S["vA"], t)
                for hf in range(2):
                    t_.matmul(out=P[4 + hf][:, 0:512],
                              lhsT=rqb[:, 128 * t:128 * (t + 1)],
                              rhs=rktb[:, 512 * hf:512 * (hf + 1)],
                              start=True, stop=True).then_inc(S["trA"], 1)
                if t >= 1:
                    t_.wait_ge(S["sa1t"], t)
                for hf in range(2):
                    t_.matmul(out=P[6 + hf][:, 0:512],
                              lhsT=qbuf[:, 128 * t:128 * (t + 1)],
                              rhs=keystb[:, 512 * hf:512 * (hf + 1)],
                              start=True, stop=True).then_inc(S["trA"], 1)
                t_.wait_ge(S[f"lz{t % 2}"], 16 * (t // 2 + 1))
                if t >= 2:
                    t_.wait_ge(S["szb"], t - 1)
                    t_.wait_ge(S["vw2"], t - 1)
                if t >= 1:
                    t_.wait_ge(S["va2"], 8 * t)
                for hf in range(2):
                    bank = (0 if hf == 0 else 2) + t % 2
                    for kc in range(32):
                        mm = t_.matmul(
                            out=P[bank][:, 0:512],
                            lhsT=w1b[:, 4096 * (t % 2) + 128 * kc:
                                     4096 * (t % 2) + 128 * (kc + 1)],
                            rhs=xT[:, 1024 * kc + 512 * hf:1024 * kc + 512 * (hf + 1)],
                            start=(kc == 0), stop=(kc == 31))
                    mm.then_inc(S["tzc"], 1)
                if t >= 1:
                    guc = t - 1
                    t_.wait_ge(S["vg"], guc + 1)
                    t_.wait_ge(S["szb"], t)
                    for hf in range(2):
                        bank = (0 if hf == 0 else 2) + guc % 2
                        mm = t_.matmul(out=P[bank][0:1, 0:512],
                                       lhsT=wu2tb[:, guc:guc + 1],
                                       rhs=gb[:, 1024 * (guc % 2) + 512 * hf:
                                              1024 * (guc % 2) + 512 * (hf + 1)],
                                       start=True, stop=True)
                    mm.then_inc(S["tl2"], 1)
                t_.wait_ge(S["vR"], t + 1)
                for j in range(8):
                    mm = t_.transpose(out=P[6][:].bitcast(BF16)[:, 128 * j:128 * (j + 1)],
                                      in_=Rbuf[:].bitcast(BF16)[:, 128 * j:128 * (j + 1)],
                                      identity=s_idt[:])
                mm.then_inc(S["tpT"], 1)
                t_.wait_ge(S["spT"], t + 1)
                for j in range(8):
                    mm = t_.matmul(out=P[7][:, 0:256],
                                   lhsT=Abuf[:, 128 * j:128 * (j + 1)],
                                   rhs=vdb[:, 256 * j:256 * (j + 1)],
                                   start=(j == 0), stop=(j == 7))
                mm.then_inc(S["ta1"], 1)
                t_.wait_ge(S["sa1"], t + 1)
                for v in range(2):
                    mm = t_.transpose(out=P[6][:].bitcast(BF16)[:, 128 * v:128 * (v + 1)],
                                      in_=a1b[:, 128 * v:128 * (v + 1)],
                                      identity=s_idt[:])
                mm.then_inc(S["ta1t"], 1)
                t_.wait_ge(S["sa1t"], t + 1)
                if t >= 1:
                    t_.wait_ge(S["vw2"], t)
                for hs in range(8):
                    if hs >= 2:
                        t_.wait_ge(S["va2"], 8 * t + hs - 1)
                    for v in range(2):
                        mm = t_.matmul(out=P[hs % 2][:, 0:512],
                                       lhsT=a1Tb[:, 128 * v:128 * (v + 1)],
                                       rhs=wvutb[:, 4096 * v + 512 * hs:
                                                 4096 * v + 512 * (hs + 1)],
                                       start=(v == 0), stop=(v == 1))
                    mm.then_inc(S["ta2"], 1)
            t_.wait_ge(S["vg"], 8)
            t_.wait_ge(S["szb"], 8)
            for hf in range(2):
                bank = (0 if hf == 0 else 2) + 1
                mm = t_.matmul(out=P[bank][0:1, 0:512], lhsT=wu2tb[:, 7:8],
                               rhs=gb[:, 1024 + 512 * hf:1024 + 512 * (hf + 1)],
                               start=True, stop=True)
            mm.then_inc(S["tl2"], 1)
            t_.wait_ge(S["vlr"], 1)
            for j in range(8):
                mm = t_.transpose(out=P[4][:, j:j + 1],
                                  in_=Rbuf[0:1, 128 * j:128 * (j + 1)],
                                  identity=id1f[:])
            mm.then_inc(S["tls"], 1)
        @block.vector
        def _(v):
            vch = _Chain(v, S["vch"])
            v.memset(nbig[:], NEGBIG)
            for t in range(8):
                v.wait_ge(S["sq"], t + 1)
                vch.step(v.reduce_sum(out=ccol[:, 0:1],
                                      in_=mpart[:, NSL * t:NSL * (t + 1)],
                                      axis=AX.X))
                v.tensor_scalar(out=ccol[:, 1:2], in0=sqc[:, t:t + 1],
                                scalar1=1.0 / 4096, scalar2=None, op0=ALU.mult)
                vch.step(v.tensor_scalar(out=mbuf[:, t:t + 1], in0=ccol[:, 0:1],
                                         scalar1=1.0 / 4096, scalar2=None,
                                         op0=ALU.mult))
                vch.step(v.tensor_tensor(out=ccol[:, 2:3], in0=mbuf[:, t:t + 1],
                                         in1=mbuf[:, t:t + 1], op=ALU.mult))
                v.tensor_sub(out=vbuf[:, t:t + 1], in0=ccol[:, 1:2],
                             in1=ccol[:, 2:3]).then_inc(S["va"], 1)
            for t in range(8):
                v.wait_ge(S["rc"], t + 1)
                if t >= 1:
                    v.wait_ge(S["nm"], t)
                vch.step(v.tensor_tensor(out=ccol[:, 3:4], in0=rbuf[:, t:t + 1],
                                         in1=rbuf[:, t:t + 1], op=ALU.mult))
                v.scalar_tensor_tensor(out=nbuf[:, t:t + 1], in0=mbuf[:, t:t + 1],
                                       scalar=-1.0, in1=rbuf[:, t:t + 1],
                                       op0=ALU.mult, op1=ALU.mult)
                v.tensor_tensor(out=colb[:, 8 * t + 6:8 * t + 7],
                                in0=vbuf[:, t:t + 1], in1=ccol[:, 3:4],
                                op=ALU.mult).then_inc(S["nm"], 1)
            for t in range(8):
                v.wait_ge(S["ha"], t + 1)
                cp = v.tensor_copy(out=lv8[:, t:t + 1], in_=colb[:, 8 * t:8 * t + 1])
            cp.then_inc(S["lv"], 1)
            if DBG_L1 >= 1:
                return
            for t in range(8):
                for b in range(8):
                    gi = t * 8 + b
                    v.wait_ge(S["tt"], 64 + gi + 1)
                    dst = AP(xT, 4096 * b + 128 * t,
                             [[32768, 128], [1024, 4], [1, 128]])
                    v.tensor_copy(out=dst,
                                  in_=P[2 + gi % 2][:].bitcast(BF16)[:, 0:512]
                                  .rearrange("p (a c) -> p a c", c=128)
                                  ).then_inc(S["xe"], 1)
            v.wait_ge(S["tls"], 1)
            v.reduce_sum(out=lsum_sb[0:1, 0:1], in_=P[0][0:1, 0:8],
                         axis=AX.X).then_inc(S["vls"], 1)
            v.memset(learnacc[0:1, :], 0.0)
            v.wait_ge(S["lf"], 16 * NFFN)

            def v_tail(tt):
                v.wait_ge(S["va2"], 8 * (tt + 1))
                v.wait_ge(S["ha"], tt + 1)
                v.wait_ge(S["so"], 16 * (tt + 1))
                vch.step(v.reduce_sum(out=colb[:, 8 * tt + 2:8 * tt + 3],
                                      in_=apart[:], axis=AX.X))
                vch.step(v.scalar_tensor_tensor(
                    out=y[tt][:], in0=a2t[:], scalar=1.0,
                    in1=y[tt][:], op0=ALU.mult, op1=ALU.mult,
                    accum_out=colb[:, 8 * tt + 3:8 * tt + 4]))
                v.scalar_tensor_tensor(out=y[tt][:],
                                       in0=a2t[:], scalar=1.0,
                                       in1=a2t[:], op0=ALU.mult, op1=ALU.mult,
                                       accum_out=colb[:, 8 * tt + 4:8 * tt + 5]
                                       ).then_inc(S["vst"], 1)
                v.wait_ge(S["szb"], tt + 1)
                if tt >= 2:
                    v.wait_ge(S["tl2"], tt - 1)
                vch.step(v.tensor_tensor(out=gx1[:], in0=zbt[:], in1=zbt[:],
                                         op=ALU.mult))
                vch.step(v.tensor_tensor(out=gx2[:], in0=gx1[:], in1=zbt[:],
                                         op=ALU.mult))
                v.scalar_tensor_tensor(out=gx1[:], in0=gx2[:], scalar=0.044715,
                                       in1=zbt[:], op0=ALU.mult,
                                       op1=ALU.add).then_inc(S["vg1"], 1)
                v.wait_ge(S["st"], tt + 1)
                vch.step(v.tensor_scalar(out=gx1[:], in0=gx2[:], scalar1=1.0,
                                         scalar2=None, op0=ALU.add))
                vch.step(v.tensor_scalar(out=gx2[:], in0=zbt[:], scalar1=0.5,
                                         scalar2=None, op0=ALU.mult))
                v.tensor_tensor(out=gb[:, 1024 * (tt % 2):1024 * (tt % 2 + 1)],
                                in0=gx1[:], in1=gx2[:],
                                op=ALU.mult).then_inc(S["vg"], 1)

            def v_accum(guc):
                v.wait_ge(S["tl2"], guc + 1)
                for hf in range(2):
                    bank = (0 if hf == 0 else 2) + guc % 2
                    acc_mm = v.tensor_tensor(
                        out=learnacc[0:1, 512 * hf:512 * (hf + 1)],
                        in0=learnacc[0:1, 512 * hf:512 * (hf + 1)],
                        in1=P[bank][0:1, 0:512], op=ALU.add)
                acc_mm.then_inc(S["vw2"], 1)

            for t in range(8):
                for hf in range(2):
                    v.wait_ge(S["trA"], 4 * t + hf + 1)
                    vch.step(v.tensor_tensor(out=Rbuf[:, 512 * hf:512 * (hf + 1)],
                                    in0=P[4 + hf][:, 0:512],
                                    in1=logrelb[:, 512 * hf:512 * (hf + 1)],
                                    op=ALU.add))
                for hf in range(2):
                    v.wait_ge(S["trA"], 4 * t + 2 + hf + 1)
                    if hf == 0:
                        vch.step(v.tensor_tensor(out=Abuf[:, 0:512],
                                        in0=P[6][:, 0:512],
                                        in1=logrelb[:, 0:512], op=ALU.add))
                    else:
                        v.tensor_tensor(out=Abuf[:, 512:1024],
                                        in0=P[7][:, 0:512],
                                        in1=logrelb[:, 512:1024],
                                        op=ALU.add).then_inc(S["vA"], 1)
                v.wait_ge(S["vA"], t + 1)
                vch.step(v.max(out=top8[:], in_=Rbuf[:]))
                vch.step(v.tensor_copy(out=tauc[:], in_=top8[:, 7:8]))
                vch.step(v.scalar_tensor_tensor(
                    out=gb[:, 1024:2048], in0=Rbuf[:], scalar=tauc[:, 0:1],
                    in1=nbig[:, 0:1].to_broadcast([128, 1024]),
                    op0=ALU.is_lt, op1=ALU.mult))
                vch.step(v.tensor_tensor(out=gb[:, 0:1024], in0=Abuf[:],
                                         in1=gb[:, 1024:2048], op=ALU.add))
                v.reduce_max(out=nmx[:], in_=gb[:, 0:1024], axis=AX.X,
                             negate=True).then_inc(S["vtk"], 1)
                v.wait_ge(S["sex"], t + 1)
                vch.step(v.reciprocal(out=rzc[:], in_=zsc[:]))
                v.scalar_tensor_tensor(out=Rbuf[:].bitcast(BF16)[:, 0:1024],
                                       in0=gb[:, 1024:2048], scalar=rzc[:, 0:1],
                                       in1=gb[:, 1024:2048], op0=ALU.mult,
                                       op1=ALU.bypass).then_inc(S["vR"], 1)
                if t >= 1:
                    v_tail(t - 1)
                    v_accum(t - 1)
            v_tail(7)
            v_accum(7)
            v.wait_ge(S["vw2"], 8)
            vch.step(v.tensor_copy(out=dmv[:, 0:1], in_=dmv[:, 1:2]))
            v.scalar_tensor_tensor(out=Rbuf[0:1, 0:1024], in0=learnacc[0:1, :],
                                   scalar=bu2b[0:1, 0:1], in1=learnacc[0:1, :],
                                   op0=ALU.add, op1=ALU.bypass).then_inc(S["vlr"], 1)
            v.wait_ge(S["sl"], 2)
            for t in range(8):
                cp = v.tensor_copy(out=colb[:, 8 * t + 1:8 * t + 2],
                                   in_=learn8[:, t:t + 1])
            cp.then_inc(S["vlc"], 1)
        @block.scalar
        def _(s_):
            sch = _Chain(s_, S["sch"])
            for t in range(8):
                s_.wait_ge(S["ae"], 128)
                if t >= 1:
                    s_.wait_ge(S["sq"], t)
                dump = a2t[:] if t % 2 == 0 else xT[:, 0:4096].bitcast(BF16)[:, 0:4096]
                s_.activation(out=dump, in_=y[t][:], func=AF.Square,
                              accum_out=sqc[:, t:t + 1]).then_inc(S["sq"], 1)
            for t in range(8):
                s_.wait_ge(S["va"], t + 1)
                if t >= 1:
                    s_.wait_ge(S["rc"], t)
                sch.step(s_.activation(out=ccol[:, 4:5], in_=vbuf[:, t:t + 1],
                                       func=AF.Ln, bias=cst[:, 0:1], scale=1.0))
                s_.activation(out=rbuf[:, t:t + 1], in_=ccol[:, 4:5], func=AF.Exp,
                              scale=-0.5).then_inc(S["rc"], 1)
            for t in range(8):
                s_.wait_ge(S["nm"], t + 1)
                if t >= 1:
                    s_.wait_ge(S["ha"], t)
                s_.activation(out=colb[:, 8 * t:8 * t + 1],
                              in_=colb[:, 8 * t + 6:8 * t + 7], func=AF.Ln,
                              bias=cst[:, 1:2], scale=1.0)
                apply_out = y[t][:] if DBG_L1 < 2 else a2t[:]
                sch.step(s_.activation(out=apply_out, in_=y[t][:], func=AF.Identity,
                              scale=rbuf[:, t:t + 1], bias=nbuf[:, t:t + 1],
                              accum_out=ccol[:, 5:6]))
                s_.copy(out=colb[:, 8 * t + 5:8 * t + 6],
                        in_=ccol[:, 5:6]).then_inc(S["ha"], 1)
            if DBG_L1 >= 1:
                return
            s_.wait_ge(S["tq"], 2)
            s_.copy(out=qbuf[:, 0:512], in_=P[4][0:64, 0:512])
            s_.copy(out=qbuf[:, 512:1024], in_=P[5][0:64, 0:512]).then_inc(S["sq2"], 1)
            s_.wait_ge(S["tr"], 2)
            s_.copy(out=rqb[:, 0:512], in_=P[6][0:48, 0:512])
            s_.copy(out=rqb[:, 512:1024], in_=P[7][0:48, 0:512]).then_inc(S["sr"], 1)
            s_.wait_ge(S["lf"], 16 * NFFN)
            for t in range(8):
                s_.wait_ge(S["vtk"], t + 1)
                s_.activation(out=gb[:, 1024:2048], in_=gb[:, 0:1024], func=AF.Exp,
                              bias=nmx[:, 0:1], scale=1.0,
                              accum_out=zsc[:]).then_inc(S["sex"], 1)
                if t >= 1:
                    s_.wait_ge(S["vg1"], t)
                    s_.activation(out=gx2[:], in_=gx1[:], func=AF.Tanh,
                                  scale=0.7978845608028654).then_inc(S["st"], 1)
                s_.wait_ge(S["tzc"], 2 * t + 2)
                if t >= 1:
                    s_.wait_ge(S["vg"], t)
                s_.activation(out=zbt[:, 0:512], in_=P[t % 2][:, 0:512],
                              func=AF.Identity, bias=bu1b[:, t:t + 1], scale=1.0)
                s_.activation(out=zbt[:, 512:1024], in_=P[2 + t % 2][:, 0:512],
                              func=AF.Identity, bias=bu1b[:, t:t + 1],
                              scale=1.0).then_inc(S["szb"], 1)
                s_.wait_ge(S["tpT"], t + 1)
                s_.copy(out=Abuf[:], in_=P[6][:].bitcast(BF16)[:, 0:1024]
                        ).then_inc(S["spT"], 1)
                s_.wait_ge(S["ta1"], t + 1)
                s_.copy(out=a1b[:], in_=P[7][:, 0:256]).then_inc(S["sa1"], 1)
                s_.wait_ge(S["ta1t"], t + 1)
                s_.copy(out=a1Tb[:], in_=P[6][:].bitcast(BF16)[:, 0:256]
                        ).then_inc(S["sa1t"], 1)
                for hs in range(8):
                    s_.wait_ge(S["ta2"], 8 * t + hs + 1)
                    if hs == 0 and t >= 1:
                        s_.wait_ge(S["ya2"], 16 * t)
                        s_.wait_ge(S["vst"], t)
                    s_.activation(out=a2t[:, 512 * hs:512 * (hs + 1)],
                                  in_=P[hs % 2][:, 0:512], func=AF.Identity,
                                  accum_out=apart[:, hs:hs + 1]
                                  ).then_inc(S["va2"], 1)
            s_.wait_ge(S["vg1"], 8)
            s_.activation(out=gx2[:], in_=gx1[:], func=AF.Tanh,
                          scale=0.7978845608028654).then_inc(S["st"], 1)
            s_.wait_ge(S["tls"], 2)
            s_.copy(out=learn8[:], in_=P[4][:, 0:8]).then_inc(S["sl"], 2)
    return nc


def build_l2():
    nc = bass.Bass("TRN2", target_bir_lowering=False, debug=False)
    i_h = nc.dram_tensor("i_h", [8, 128, 4096], BF16, kind="ExternalInput").ap()
    i_a2 = nc.dram_tensor("i_a2", [8, 128, 4096], BF16, kind="ExternalInput").ap()
    i_cols = nc.dram_tensor("i_cols", [128, 8, 8], F32, kind="ExternalInput").ap()
    i_sums = nc.dram_tensor("i_sums", [1, 8], F32, kind="ExternalInput").ap()
    i_gw = nc.dram_tensor("i_gw", [1, 1], F32, kind="ExternalInput").ap()
    i_gb = nc.dram_tensor("i_gb", [1, 1], F32, kind="ExternalInput").ap()
    o_out = nc.dram_tensor("o_out", [8, 128, 4096], F32, kind="ExternalOutput").ap()

    with ExitStack() as ctx:
        e = ctx.enter_context
        hb = e(nc.sbuf_tensor("hb", [128, 2 * 4096], BF16))
        ab = e(nc.sbuf_tensor("ab", [128, 2 * 4096], BF16))
        tmpb = e(nc.sbuf_tensor("tmpb", [128, 4096], BF16))
        outb = e(nc.sbuf_tensor("outb", [128, 2 * 4096], F32))
        colsb = e(nc.sbuf_tensor("colsb", [128, 64], F32))
        sumsb = e(nc.sbuf_tensor("sumsb", [1, 8], F32))
        g3 = e(nc.sbuf_tensor("g3", [1, 4], F32))
        onesf = e(nc.sbuf_tensor("onesf", [1, 128], F32))
        bc3 = e(nc.sbuf_tensor("bc3", [128, 4], F32))
        cc = e(nc.sbuf_tensor("cc", [128, 16], F32))
        gcol = e(nc.sbuf_tensor("gcol", [128, 8], F32))
        cst2 = e(nc.sbuf_tensor("cst2", [128, 1], F32))
        dmv = e(nc.sbuf_tensor("dmv", [1, 4], F32))
        dms = e(nc.sbuf_tensor("dms", [1, 4], F32))
        rcol = e(nc.sbuf_tensor("rcol", [128, 8], F32))
        ncol = e(nc.sbuf_tensor("ncol", [128, 8], F32))
        P = [e(nc.psum_tensor(f"P{i}", [128, 512], F32)) for i in range(2)]

        sem_names = ["ld", "lh0", "lh1", "la0", "la1", "vb", "sb", "tb", "ve", "se", "so0", "so1", "sos0", "sos1", "os", "vch", "sch"]
        S = {n: e(nc.semaphore(n)) for n in sem_names}
        block = e(nc.Block())

        @block.gpsimd
        def _(g):
            g.dma_start(out=colsb[:].rearrange("p (t c) -> p t c", c=8),
                        in_=i_cols).then_inc(S["ld"], 16)
            g.dma_start(out=sumsb[:], in_=i_sums).then_inc(S["ld"], 16)
            g.dma_start(out=g3[0:1, 1:2], in_=i_gw).then_inc(S["ld"], 16)
            g.dma_start(out=g3[0:1, 2:3], in_=i_gb).then_inc(S["ld"], 16)
            for t in range(8):
                if t >= 2:
                    g.wait_ge(S["ve"], 3 * (t - 2) + 3)
                g.dma_start(out=ab[:, 4096 * (t % 2):4096 * (t % 2 + 1)],
                            in_=i_a2[t]).then_inc(S[f"la{t % 2}"], 16)
            g.wait_ge(S["se"], 8)
            g.dma_start(out=o_out[7],
                        in_=outb[:, 4096:8192]).then_inc(S["so1"], 16)
            g.wait_ge(S["so0"], 16)
            g.wait_ge(S["so1"], 16)
            g.wait_ge(S["sos0"], 16 * 3)
            g.wait_ge(S["sos1"], 16 * 3)

        @block.sync
        def _(sy):
            for t in range(8):
                if t >= 2:
                    sy.wait_ge(S["ve"], 3 * (t - 2) + 3)
                sy.dma_start(out=hb[:, 4096 * (t % 2):4096 * (t % 2 + 1)],
                             in_=i_h[t]).then_inc(S[f"lh{t % 2}"], 16)
            sy.wait_ge(S["se"], 7)
            sy.dma_start(out=o_out[6],
                         in_=outb[:, 0:4096]).then_inc(S["so0"], 16)

        @block.vector
        def _(v):
            vch = _Chain(v, S["vch"])
            v.memset(onesf[:], 1.0)
            v.memset(dmv[:], 0.0)
            v.memset(g3[0:1, 3:4], 0.0)
            v.memset(cst2[:], 1e-5)
            v.wait_ge(S["ld"], 64)
            vch.step(v.reduce_sum(out=cc[0:1, 0:1], in_=sumsb[:], axis=AX.X))
            vch.step(v.tensor_scalar(out=cc[0:1, 1:2], in0=cc[0:1, 0:1],
                                     scalar1=1.0 / 8192, scalar2=1e-6,
                                     op0=ALU.mult, op1=ALU.add))
            v.reciprocal(out=g3[0:1, 0:1], in_=cc[0:1, 1:2]).then_inc(S["vb"], 1)
            for t in range(8):
                v.wait_ge(S["sb"], 1)
                c0 = colsb[:, 8 * t:8 * t + 1]
                c2 = colsb[:, 8 * t + 2:8 * t + 3]
                c3 = colsb[:, 8 * t + 3:8 * t + 4]
                c4 = colsb[:, 8 * t + 4:8 * t + 5]
                c5 = colsb[:, 8 * t + 5:8 * t + 6]
                c6 = colsb[:, 8 * t + 6:8 * t + 7]
                vch.step(v.scalar_tensor_tensor(out=cc[:, 0:1], in0=c0,
                                       scalar=bc3[:, 0:1], in1=c0,
                                       op0=ALU.mult, op1=ALU.bypass))
                v.wait_ge(S["sb"], 3 * t + 2)
                vch.step(v.tensor_scalar(out=cc[:, 2:3], in0=cc[:, 1:2], scalar1=2.5,
                                scalar2=None, op0=ALU.mult))
                vch.step(v.scalar_tensor_tensor(out=cc[:, 3:4], in0=cc[:, 0:1], scalar=0.5,
                                       in1=cc[:, 2:3], op0=ALU.mult, op1=ALU.add))
                vch.step(v.tensor_scalar(out=cc[:, 4:5], in0=cc[:, 3:4], scalar1=0.0,
                                scalar2=5.0, op0=ALU.max, op1=ALU.min))
                vch.step(v.tensor_scalar(out=cc[:, 5:6], in0=cc[:, 4:5], scalar1=-0.5,
                                scalar2=1.0 / 1.5, op0=ALU.add, op1=ALU.mult))
                v.tensor_scalar(out=cc[:, 6:7], in0=cc[:, 5:6], scalar1=0.0,
                                scalar2=1.0, op0=ALU.max,
                                op1=ALU.min).then_inc(S["ve"], 1)
                v.wait_ge(S["sb"], 3 * t + 3)
                vch.step(v.tensor_scalar(out=cc[:, 8:9], in0=cc[:, 7:8], scalar1=0.05,
                                scalar2=None, op0=ALU.is_ge))
                vch.step(v.tensor_tensor(out=gcol[:, t:t + 1], in0=cc[:, 7:8],
                                in1=cc[:, 8:9], op=ALU.mult))
                vch.step(v.scalar_tensor_tensor(out=cc[:, 9:10], in0=c2,
                                       scalar=gcol[:, t:t + 1], in1=c5,
                                       op0=ALU.mult, op1=ALU.add))
                vch.step(v.tensor_tensor(out=cc[:, 11:12], in0=gcol[:, t:t + 1],
                                in1=gcol[:, t:t + 1], op=ALU.mult))
                vch.step(v.tensor_scalar(out=cc[:, 10:11], in0=cc[:, 9:10],
                                scalar1=1.0 / 4096, scalar2=None, op0=ALU.mult))
                vch.step(v.scalar_tensor_tensor(out=cc[:, 13:14], in0=c3,
                                       scalar=gcol[:, t:t + 1], in1=c3,
                                       op0=ALU.mult, op1=ALU.bypass))
                vch.step(v.scalar_tensor_tensor(out=cc[:, 12:13], in0=c4,
                                       scalar=cc[:, 11:12], in1=c4,
                                       op0=ALU.mult, op1=ALU.bypass))
                vch.step(v.scalar_tensor_tensor(out=cc[:, 14:15], in0=cc[:, 13:14],
                                       scalar=2.0, in1=cc[:, 12:13],
                                       op0=ALU.mult, op1=ALU.add))
                vch.step(v.scalar_tensor_tensor(out=cc[:, 15:16], in0=c6,
                                       scalar=4096.0, in1=cc[:, 14:15],
                                       op0=ALU.mult, op1=ALU.add))
                vch.step(v.tensor_tensor(out=cc[:, 13:14], in0=cc[:, 10:11],
                                in1=cc[:, 10:11], op=ALU.mult))
                vch.step(v.tensor_scalar(out=cc[:, 12:13], in0=cc[:, 15:16],
                                scalar1=1.0 / 4096, scalar2=None, op0=ALU.mult))
                v.tensor_sub(out=cc[:, 14:15], in0=cc[:, 12:13],
                             in1=cc[:, 13:14]).then_inc(S["ve"], 1)
                v.wait_ge(S["sb"], 3 * t + 4)
                vch.step(v.scalar_tensor_tensor(out=ncol[:, t:t + 1], in0=cc[:, 10:11],
                                       scalar=-1.0, in1=rcol[:, t:t + 1],
                                       op0=ALU.mult, op1=ALU.mult))
                v.wait_ge(S[f"lh{t % 2}"], 16 * (t // 2 + 1))
                v.wait_ge(S[f"la{t % 2}"], 16 * (t // 2 + 1))
                if t >= 1:
                    v.wait_ge(S["se"], t)
                v.scalar_tensor_tensor(
                    out=tmpb[:], in0=ab[:, 4096 * (t % 2):4096 * (t % 2 + 1)],
                    scalar=gcol[:, t:t + 1],
                    in1=hb[:, 4096 * (t % 2):4096 * (t % 2 + 1)],
                    op0=ALU.mult, op1=ALU.add).then_inc(S["ve"], 1)

        @block.tensor
        def _(t_):
            t_.wait_ge(S["vb"], 1)
            t_.matmul(out=P[0][:, 0:4], lhsT=onesf[:], rhs=g3[0:1, 0:4],
                      start=True, stop=True).then_inc(S["tb"], 1)

        @block.scalar
        def _(s_):
            sch = _Chain(s_, S["sch"])
            s_.wait_ge(S["tb"], 1)
            s_.copy(out=bc3[:], in_=P[0][:, 0:4]).then_inc(S["sb"], 1)
            for t in range(8):
                s_.wait_ge(S["ld"], 64)
                s_.activation(out=cc[:, 1:2], in_=colsb[:, 8 * t + 1:8 * t + 2],
                              func=AF.Sigmoid).then_inc(S["sb"], 1)
                s_.wait_ge(S["ve"], 3 * t + 1)
                s_.activation(out=cc[:, 7:8], in_=cc[:, 6:7], func=AF.Sigmoid,
                              scale=bc3[:, 1:2], bias=bc3[:, 2:3]
                              ).then_inc(S["sb"], 1)
                s_.wait_ge(S["ve"], 3 * t + 2)
                sch.step(s_.activation(out=cc[:, 15:16], in_=cc[:, 14:15],
                                       func=AF.Ln, bias=cst2[:, 0:1], scale=1.0))
                s_.activation(out=rcol[:, t:t + 1], in_=cc[:, 15:16], func=AF.Exp,
                              scale=-0.5).then_inc(S["sb"], 1)
                s_.wait_ge(S["ve"], 3 * t + 3)
                if t >= 2:
                    s_.wait_ge(S[f"sos{t % 2}"], 16 * ((t - 2) // 2 + 1))
                s_.activation(out=outb[:, 4096 * (t % 2):4096 * (t % 2 + 1)],
                              in_=tmpb[:], func=AF.Identity,
                              scale=rcol[:, t:t + 1], bias=ncol[:, t:t + 1]
                              ).then_inc(S["se"], 1)
                if t <= 5:
                    s_.wait_ge(S["se"], t + 1)
                    s_.dma_start(out=o_out[t],
                                 in_=outb[:, 4096 * (t % 2):4096 * (t % 2 + 1)]
                                 ).then_inc(S[f"sos{t % 2}"], 16)
    return nc


def _l1_in_maps(inputs, asm, consts):
    idt = consts[0]
    x = np.asarray(inputs["L0_attn_data"], dtype=np.float32).reshape(8192, 4096)
    bias = np.asarray(inputs["L0_attn_bias"], dtype=np.float32)
    log_rel = np.asarray(inputs["L0_ffn_log_rel"], dtype=np.float32)
    bu1 = np.asarray(inputs["L0_ffn_bu1"], dtype=np.float32)
    wu2 = np.asarray(inputs["L0_ffn_Wu2"], dtype=np.float32)
    bu2 = np.asarray(inputs["L0_ffn_bu2"], dtype=np.float32)
    logrel_exp = np.ascontiguousarray(
        np.broadcast_to(log_rel[None, :], (128, 1024)))
    shared = {
        "w_l1": asm["w_l1"], "wu1t": asm["wu1t_l1"], "i_vd": asm["vd_l1"],
        "i_wvut": asm["wvut_l1"], "i_wqt": asm["wqt_l1"],
        "i_keyst": np.ascontiguousarray(asm["keysT"]),
        "i_rkt": np.ascontiguousarray(asm["rkT"]),
        "i_wrt": np.ascontiguousarray(asm["wrT"]),
        "i_logrel": logrel_exp,
        "bias_r": np.ascontiguousarray(bias[None, :]),
        "i_bu1": np.ascontiguousarray(bu1.reshape(8, 128).T),
        "i_wu2t": np.ascontiguousarray(wu2.reshape(8, 128).T),
        "i_bu2": bu2.reshape(1, 1),
        "idt": idt,
    }
    maps = []
    for c in range(NC):
        xs = x[T * c:T * (c + 1)]
        m = dict(shared)
        m["x_rows"] = np.ascontiguousarray(xs.reshape(8, 128, 4096))
        m["x_cols"] = np.ascontiguousarray(
            xs.reshape(8, 128, NSL, SLW).transpose(2, 1, 0, 3))
        maps.append(m)
    return maps


def _l2_in_maps(inputs, res1):
    gw = np.asarray(inputs["L0_ffn_gate_w1"], dtype=np.float32).reshape(1, 1)
    gb = np.asarray(inputs["L0_ffn_gate_bias"], dtype=np.float32).reshape(1, 1)
    sums = np.ascontiguousarray(
        np.concatenate([np.asarray(res1[c]["o_lsum"]).reshape(1, 1)
                        for c in range(NC)], axis=1))
    maps = []
    for c in range(NC):
        cols = np.asarray(res1[c]["o_cols"]).reshape(8, 128, 8)
        maps.append({
            "i_h": np.asarray(res1[c]["o_h"]).reshape(8, 128, 4096),
            "i_a2": np.asarray(res1[c]["o_a2"]).reshape(8, 128, 4096),
            "i_cols": np.ascontiguousarray(cols.transpose(1, 0, 2)),
            "i_sums": sums,
            "i_gw": gw, "i_gb": gb,
        })
    return maps


def kernel(**inputs):
    consts = _consts()
    nc0 = build_l0()
    res0 = run_bass_kernel_spmd(nc0, _l0_in_maps(inputs, consts),
                                core_ids=list(range(NC)))
    asm = _l0_assemble(res0.results)
    nc1 = build_l1()
    res1 = run_bass_kernel_spmd(nc1, _l1_in_maps(inputs, asm, consts),
                                core_ids=list(range(NC)))
    nc2 = build_l2()
    res2 = run_bass_kernel_spmd(nc2, _l2_in_maps(inputs, res1.results),
                                core_ids=list(range(NC)))
    out = np.concatenate(
        [np.asarray(res2.results[c]["o_out"]).reshape(T, D) for c in range(NC)],
        axis=0)
    return np.ascontiguousarray(out.reshape(2, 4096, 4096), dtype=np.float32)


# version 05 (final)
# speedup vs baseline: 1.0453x; 1.0075x over previous
import numpy as np
import ml_dtypes
from contextlib import ExitStack

import concourse.bass as bass
import concourse.mybir as mybir
from concourse.bass_utils import run_bass_kernel_spmd
from concourse.bass_types import AP

F32 = mybir.dt.float32
BF16 = mybir.dt.bfloat16
I32 = mybir.dt.int32
I8 = mybir.dt.int8
AF = mybir.ActivationFunctionType
ALU = mybir.AluOpType
AX = mybir.AxisListType
BF = ml_dtypes.bfloat16

NC = 8
D = 4096
T = 1024
NS = 1024
VB = 256
BOT = 64
RD = 48
HID4 = 1024
KS = D // NC

class _Chain:

    def __init__(self, eng, sem):
        self.eng = eng
        self.sem = sem
        self.n = 0

    def step(self, instr):
        instr.then_inc(self.sem, 1)
        self.n += 1
        self.eng.wait_ge(self.sem, self.n)
        return instr


RSQRT_RD = 1.0 / np.sqrt(48.0)
NEGBIG = -1e30


def _consts():
    idt = np.eye(128, dtype=BF)
    e4c = np.zeros((128, 4), dtype=BF)
    for q in range(4):
        e4c[q * 32:(q + 1) * 32, q] = 1
    place = np.zeros((4, 4096), dtype=np.float32)
    for t in range(32):
        for q in range(4):
            place[q, 128 * t + 4 * t + q] = 1
    ks16 = np.zeros((16, 512), dtype=BF)
    for j in range(4):
        for i in range(128):
            ks16[4 * j + i // 32, 128 * j + i] = 1
    idtf = np.eye(128, dtype=np.float32)
    return idt, e4c, place, ks16, idtf


def build_l0():
    nc = bass.Bass("TRN2", target_bir_lowering=False, debug=False)
    m_cols = nc.dram_tensor("m_cols", [128, 32, 512], I8, kind="ExternalInput").ap()
    w_cols = nc.dram_tensor("w_cols", [32, 128, 512], F32, kind="ExternalInput").ap()
    v_rows = nc.dram_tensor("v_rows", [128, 4096], F32, kind="ExternalInput").ap()
    wvd = nc.dram_tensor("wvd", [128, 2, 4096], F32, kind="ExternalInput").ap()
    wu1_cols = nc.dram_tensor("wu1_cols", [128, 8, 512], F32, kind="ExternalInput").ap()
    wvu_rows = nc.dram_tensor("wvu_rows", [128, 4, 256], F32, kind="ExternalInput").ap()
    wq_cols = nc.dram_tensor("wq_cols", [64, 512], F32, kind="ExternalInput").ap()
    keys_rows = nc.dram_tensor("keys_rows", [128, 64], F32, kind="ExternalInput").ap()
    wr = nc.dram_tensor("wr", [48, 64], F32, kind="ExternalInput").ap()
    idt = nc.dram_tensor("idt", [128, 128], BF16, kind="ExternalInput").ap()
    e4c = nc.dram_tensor("e4c", [128, 4], BF16, kind="ExternalInput").ap()
    idtf = nc.dram_tensor("idtf", [128, 128], F32, kind="ExternalInput").ap()
    place = nc.dram_tensor("place", [4, 4096], F32, kind="ExternalInput").ap()
    ks16 = nc.dram_tensor("ks16", [16, 512], BF16, kind="ExternalInput").ap()
    o_wtm = nc.dram_tensor("o_wtm", [512, 4096], BF16, kind="ExternalOutput").ap()
    o_vd = nc.dram_tensor("o_vd", [128, 256], BF16, kind="ExternalOutput").ap()
    o_wu1t = nc.dram_tensor("o_wu1t", [512, 1024], BF16, kind="ExternalOutput").ap()
    o_wvut = nc.dram_tensor("o_wvut", [256, 512], BF16, kind="ExternalOutput").ap()
    o_wqt = nc.dram_tensor("o_wqt", [512, 64], BF16, kind="ExternalOutput").ap()
    o_keyst = nc.dram_tensor("o_keyst", [64, 128], BF16, kind="ExternalOutput").ap()
    o_rk = nc.dram_tensor("o_rk", [48, 128], BF16, kind="ExternalOutput").ap()
    o_wrt = nc.dram_tensor("o_wrt", [64, 48], BF16, kind="ExternalOutput").ap()

    with ExitStack() as ctx:
        e = ctx.enter_context
        s_mask = e(nc.sbuf_tensor("s_mask", [128, 32 * 512], BF16))
        s_wstg = e(nc.sbuf_tensor("s_wstg", [128, 8 * 512], F32))
        s_idtf = e(nc.sbuf_tensor("s_idtf", [128, 128], F32))
        s_v = e(nc.sbuf_tensor("s_v", [128, 4096], BF16))
        s_wvd = e(nc.sbuf_tensor("s_wvd", [128, 2 * 4096], BF16))
        s_wu1 = e(nc.sbuf_tensor("s_wu1", [128, 8 * 512], BF16))
        s_wvu = e(nc.sbuf_tensor("s_wvu", [128, 4 * 256], BF16))
        s_wq = e(nc.sbuf_tensor("s_wq", [64, 512], BF16))
        s_keys = e(nc.sbuf_tensor("s_keys", [128, 64], BF16))
        s_wr = e(nc.sbuf_tensor("s_wr", [48, 64], BF16))
        s_idt = e(nc.sbuf_tensor("s_idt", [128, 128], BF16))
        s_e4c = e(nc.sbuf_tensor("s_e4c", [128, 4], BF16))
        s_place = e(nc.sbuf_tensor("s_place", [4, 4096], F32))
        s_ks16 = e(nc.sbuf_tensor("s_ks16", [16, 512], BF16))

        s2 = e(nc.sbuf_tensor("s2", [4, 2 * 16], F32))
        bm01 = e(nc.sbuf_tensor("bm01", [128, 16], BF16))
        bmt = e(nc.sbuf_tensor("bmt", [16, 128], BF16))
        bmte = e(nc.sbuf_tensor("bmte", [128, 512], F32))
        wtm = e(nc.sbuf_tensor("wtm", [128, 4 * 4096], BF16))
        vt = e(nc.sbuf_tensor("vt", [128, 32 * 128], BF16))
        wvdt = e(nc.sbuf_tensor("wvdt", [128, 32 * 256], BF16))
        s_vd = e(nc.sbuf_tensor("s_vd", [128, 256], BF16))
        wu1t = e(nc.sbuf_tensor("wu1t", [128, 4 * 1024], BF16))
        wvut = e(nc.sbuf_tensor("wvut", [128, 2 * 512], BF16))
        wqt = e(nc.sbuf_tensor("wqt", [128, 4 * 64], BF16))
        keyst = e(nc.sbuf_tensor("keyst", [64, 128], BF16))
        keystu = e(nc.sbuf_tensor("keystu", [64, 128], BF16))
        s_rk = e(nc.sbuf_tensor("s_rk", [48, 128], BF16))
        s_wrt = e(nc.sbuf_tensor("s_wrt", [64, 48], BF16))

        P = [e(nc.psum_tensor(f"P{i}", [128, 512], F32)) for i in range(8)]

        ld = e(nc.semaphore("ld"))
        tb = e(nc.semaphore("tb"))
        tp = e(nc.semaphore("tp"))
        vb = e(nc.semaphore("vb"))
        tw = e(nc.semaphore("tw"))
        ve = e(nc.semaphore("ve"))
        sb = e(nc.semaphore("sb"))
        tx = e(nc.semaphore("tx"))
        so = e(nc.semaphore("so"))
        LW = [e(nc.semaphore(f"lw{k}")) for k in range(8)]
        LG = {n: e(nc.semaphore(n)) for n in
              ["lda", "ldb", "ldc", "ldd", "lde", "ldf", "ldg", "ldh", "ldi"]}
        block = e(nc.Block())

        NLD = 13

        @block.gpsimd
        def _(g):
            g.dma_start(out=s_e4c[:], in_=e4c).then_inc(LG["lda"], 16)
            g.dma_start(out=s_mask[:].rearrange("p (t k) -> p t k", k=512),
                        in_=m_cols).then_inc(LG["lda"], 16)
            g.dma_start(out=s_place[:], in_=place).then_inc(LG["ldb"], 16)
            g.dma_start(out=s_idt[:], in_=idt).then_inc(LG["ldc"], 16)
            g.dma_start(out=s_ks16[:], in_=ks16).then_inc(LG["ldd"], 16)
            g.dma_start(out=s_idtf[:], in_=idtf).then_inc(LG["lde"], 16)
            g.dma_start(out=s_v[:], in_=v_rows).then_inc(LG["ldf"], 16)
            g.dma_start(out=s_wvd[:].rearrange("p (t k) -> p t k", k=4096),
                        in_=wvd).then_inc(LG["ldf"], 16)
            g.dma_start(out=s_wu1[:].rearrange("p (t k) -> p t k", k=512),
                        in_=wu1_cols).then_inc(LG["ldg"], 16)
            g.dma_start(out=s_wvu[:].rearrange("p (t k) -> p t k", k=256),
                        in_=wvu_rows).then_inc(LG["ldh"], 16)
            g.dma_start(out=s_wq[:], in_=wq_cols).then_inc(LG["ldi"], 16)
            g.dma_start(out=s_keys[:], in_=keys_rows).then_inc(LG["ldi"], 16)
            g.dma_start(out=s_wr[:], in_=wr).then_inc(LG["ldi"], 16)

        @block.tensor
        def _(t):
            t.wait_ge(LG["lda"], 32)
            for i in range(32):
                if i >= 2:
                    t.wait_ge(vb, i - 1)
                t.matmul(out=P[i % 2][0:4, :], lhsT=s_e4c[:],
                         rhs=s_mask[:, 512 * i:512 * (i + 1)],
                         start=True, stop=True).then_inc(tb, 1)
                if i >= 1:
                    if i == 1:
                        t.wait_ge(LG["ldb"], 16)
                    t.wait_ge(vb, i)
                    t.matmul(out=P[2][:, 0:16],
                             lhsT=s_place[:, 128 * (i - 1):128 * i],
                             rhs=s2[0:4, 16 * ((i - 1) % 2):16 * ((i - 1) % 2) + 16],
                             start=(i - 1 == 0), stop=False).then_inc(tp, 1)
            t.wait_ge(vb, 32)
            t.matmul(out=P[2][:, 0:16], lhsT=s_place[:, 128 * 31:128 * 32],
                     rhs=s2[0:4, 16 * (31 % 2):16 * (31 % 2) + 16],
                     start=False, stop=True).then_inc(tp, 1)
            t.wait_ge(vb, 33)
            t.wait_ge(LG["ldc"], 16)
            t.transpose(out=P[3][:].bitcast(BF16)[0:16, 0:128], in_=bm01[:],
                        identity=s_idt[:]).then_inc(tx, 1)
            t.wait_ge(sb, 1)
            t.wait_ge(LG["ldd"], 16)
            for j in range(4):
                mm = t.matmul(out=P[4][:, 128 * j:128 * (j + 1)],
                              lhsT=s_ks16[:, 128 * j:128 * (j + 1)], rhs=bmt[:],
                              start=True, stop=True)
            mm.then_inc(tx, 1)
            t.wait_ge(sb, 2)
            t.wait_ge(LG["lde"], 16)
            for i in range(32):
                t.wait_ge(LW[i % 8], 16 * (i // 8 + 1))
                if i >= 2:
                    t.wait_ge(ve, 4 * (i - 1))
                for j in range(4):
                    mm = t.transpose(out=P[i % 2][:, 128 * j:128 * (j + 1)],
                                in_=s_wstg[:, 512 * (i % 8) + 128 * j:512 * (i % 8) + 128 * (j + 1)],
                                identity=s_idtf[:])
                mm.then_inc(tw, 1)
            t.wait_ge(ve, 128)
            t.wait_ge(LG["ldf"], 32)
            for m in range(32):
                if m >= 2:
                    t.wait_ge(sb, 2 + (m - 1))
                t.transpose(out=P[m % 2][:].bitcast(BF16)[:, 0:128],
                            in_=s_v[:, 128 * m:128 * (m + 1)], identity=s_idt[:])
                for v in range(2):
                    mm = t.transpose(out=P[m % 2][:].bitcast(BF16)[:, 128 + 128 * v:256 + 128 * v],
                                in_=s_wvd[:, 4096 * v + 128 * m:4096 * v + 128 * (m + 1)],
                                identity=s_idt[:])
                mm.then_inc(tw, 1)
            t.wait_ge(sb, 2 + 32)
            for m in range(32):
                mm = t.matmul(out=P[5][:, 0:256], lhsT=vt[:, 128 * m:128 * (m + 1)],
                         rhs=wvdt[:, 256 * m:256 * (m + 1)],
                         start=(m == 0), stop=(m == 31))
            mm.then_inc(tx, 1)
            t.wait_ge(LG["ldg"], 16)
            cnt = 0
            for j in range(4):
                for tt in range(8):
                    slot = cnt % 8
                    if cnt >= 8 and cnt % 8 == 0:
                        t.wait_ge(sb, 2 + 32 + cnt // 8)
                    mm = t.transpose(out=P[6][:].bitcast(BF16)[:, 128 * slot:128 * (slot + 1)],
                                in_=s_wu1[:, 512 * tt + 128 * j:512 * tt + 128 * (j + 1)],
                                identity=s_idt[:])
                    cnt += 1
                    if cnt % 8 == 0:
                        mm.then_inc(tx, 1)
            t.wait_ge(LG["ldh"], 16)
            for v in range(2):
                for tt in range(4):
                    mm = t.transpose(out=P[7][:].bitcast(BF16)[:, 128 * (4 * v + tt):128 * (4 * v + tt + 1)],
                                in_=s_wvu[:, 256 * tt + 128 * v:256 * tt + 128 * (v + 1)],
                                identity=s_idt[:])
            mm.then_inc(tx, 1)
            t.wait_ge(LG["ldi"], 48)
            for tt in range(4):
                t.transpose(out=P[3][:].bitcast(BF16)[:, 64 * tt:64 * (tt + 1)],
                            in_=s_wq[0:64, 128 * tt:128 * (tt + 1)],
                            identity=s_idt[0:64, 0:64])
            t.transpose(out=P[3][:].bitcast(BF16)[0:64, 256:384], in_=s_keys[:],
                        identity=s_idt[:])
            mm = t.transpose(out=P[3][:].bitcast(BF16)[0:64, 384:432], in_=s_wr[0:48, 0:64],
                        identity=s_idt[0:48, 0:48])
            mm.then_inc(tx, 1)
            t.wait_ge(sb, 2 + 32 + 4 + 1 + 1 + 1)
            t.matmul(out=P[3][0:48, 256:384], lhsT=s_wrt[0:64, 0:48], rhs=keystu[:],
                     start=True, stop=True).then_inc(tx, 1)

        @block.vector
        def _(v):
            for i in range(32):
                v.wait_ge(tb, i + 1)
                if i >= 2:
                    v.wait_ge(tp, i - 1)
                v.tensor_reduce(
                    out=s2[0:4, 16 * (i % 2):16 * (i % 2) + 16],
                    in_=P[i % 2][0:4, :].rearrange("p (a b) -> p a b", b=32),
                    axis=AX.X, op=ALU.add).then_inc(vb, 1)
            v.wait_ge(tp, 32)
            v.tensor_scalar(out=bm01[:], in0=P[2][:, 0:16], scalar1=0.0, scalar2=None,
                            op0=ALU.is_gt).then_inc(vb, 1)
            for i in range(32):
                v.wait_ge(tw, i + 1)
                for j in range(4):
                    mv = bmte[:, 128 * j + 4 * i:128 * j + 4 * i + 4]
                    v.tensor_tensor(
                        out=wtm[:, 4096 * j + 128 * i:4096 * j + 128 * (i + 1)]
                            .rearrange("p (a b) -> p a b", b=32),
                        in0=P[i % 2][:, 128 * j:128 * (j + 1)]
                            .rearrange("p (a b) -> p a b", b=32),
                        in1=mv.unsqueeze(2).broadcast_to([128, 4, 32]),
                        op=ALU.mult).then_inc(ve, 1)

        @block.scalar
        def _(s):
            s.wait_ge(tx, 1)
            s.copy(out=bmt[:], in_=P[3][:].bitcast(BF16)[0:16, 0:128]).then_inc(sb, 1)
            s.wait_ge(tx, 2)
            s.copy(out=bmte[:], in_=P[4][:, 0:512]).then_inc(sb, 1)
            for m in range(32):
                s.wait_ge(tw, 32 + m + 1)
                s.copy(out=vt[:, 128 * m:128 * (m + 1)],
                       in_=P[m % 2][:].bitcast(BF16)[:, 0:128])
                s.copy(out=wvdt[:, 256 * m:256 * (m + 1)],
                       in_=P[m % 2][:].bitcast(BF16)[:, 128:384]).then_inc(sb, 1)
            s.wait_ge(tx, 3)
            s.copy(out=s_vd[:], in_=P[5][:, 0:256])
            for b in range(4):
                s.wait_ge(tx, 4 + b)
                s.copy(out=wu1t[:, 1024 * b:1024 * (b + 1)],
                       in_=P[6][:].bitcast(BF16)[:, 0:1024]).then_inc(sb, 1)
            s.wait_ge(tx, 8)
            s.copy(out=wvut[:], in_=P[7][:].bitcast(BF16)[:, 0:1024]).then_inc(sb, 1)
            s.wait_ge(tx, 9)
            s.copy(out=wqt[:], in_=P[3][:].bitcast(BF16)[:, 0:256])
            s.activation(out=keyst[:], in_=P[3][:].bitcast(BF16)[0:64, 256:384],
                         func=AF.Copy, scale=0.125)
            s.copy(out=keystu[:], in_=P[3][:].bitcast(BF16)[0:64, 256:384])
            s.copy(out=s_wrt[0:64, 0:48],
                   in_=P[3][:].bitcast(BF16)[0:64, 384:432]).then_inc(sb, 2)
            s.wait_ge(tx, 10)
            s.activation(out=s_rk[:], in_=P[3][0:48, 256:384], func=AF.Copy,
                         scale=float(RSQRT_RD)).then_inc(sb, 1)

        @block.sync
        def _(sy):
            for i in range(32):
                if i >= 8:
                    sy.wait_ge(tw, i - 7)
                sy.dma_start(out=s_wstg[:, 512 * (i % 8):512 * (i % 8 + 1)],
                             in_=w_cols[i]).then_inc(LW[i % 8], 16)
            sy.wait_ge(ve, 128)
            sy.dma_start(out=o_wtm.rearrange("(j k) n -> k j n", j=4),
                         in_=wtm[:].rearrange("p (j n) -> p j n", n=4096)).then_inc(so, 16)
            sy.wait_ge(sb, 2 + 32 + 4 + 1 + 1 + 1 + 1)
            sy.dma_start(out=o_vd, in_=s_vd[:]).then_inc(so, 16)
            sy.dma_start(out=o_wu1t.rearrange("(j k) u -> k j u", j=4),
                         in_=wu1t[:].rearrange("p (j u) -> p j u", j=4)).then_inc(so, 16)
            sy.dma_start(out=o_wvut.rearrange("(v k) h -> k v h", v=2),
                         in_=wvut[:].rearrange("p (v h) -> p v h", v=2)).then_inc(so, 16)
            sy.dma_start(out=o_wqt.rearrange("(t k) d -> k t d", t=4),
                         in_=wqt[:].rearrange("p (t d) -> p t d", t=4)).then_inc(so, 16)
            sy.dma_start(out=o_keyst, in_=keyst[:]).then_inc(so, 16)
            sy.dma_start(out=o_rk, in_=s_rk[:]).then_inc(so, 16)
            sy.dma_start(out=o_wrt, in_=s_wrt[0:64, 0:48]).then_inc(so, 16)
            sy.wait_ge(so, 16 * 8)
    return nc


def _l0_in_maps(inp, consts):
    idt, e4c, place, ks16, idtf = consts
    mask = np.asarray(inp["L0_attn_mask"]).view(np.int8)[:, ::4]
    w = np.asarray(inp["L0_attn_weight"], dtype=np.float32)
    V = np.asarray(inp["L0_ffn_aux_values"], dtype=np.float32)
    Wvd = np.asarray(inp["L0_ffn_Wvd"], dtype=np.float32)
    Wu1 = np.asarray(inp["L0_ffn_Wu1"], dtype=np.float32)
    Wvu = np.asarray(inp["L0_ffn_Wvu"], dtype=np.float32)
    Wq = np.asarray(inp["L0_ffn_Wq"], dtype=np.float32)
    keys = np.asarray(inp["L0_ffn_aux_keys"], dtype=np.float32)
    Wr = np.asarray(inp["L0_ffn_Wr"], dtype=np.float32)
    maps = []
    for c in range(NC):
        sl = slice(512 * c, 512 * (c + 1))
        maps.append({
            "m_cols": np.ascontiguousarray(
                mask[:, sl].reshape(32, 128, 512).transpose(1, 0, 2)),
            "w_cols": np.ascontiguousarray(w[:, sl].reshape(32, 128, 512)),
            "v_rows": np.ascontiguousarray(V[128 * c:128 * (c + 1), :]),
            "wvd": np.ascontiguousarray(Wvd.reshape(2, 128, 4096).transpose(1, 0, 2)),
            "wu1_cols": np.ascontiguousarray(
                Wu1[:, sl].reshape(8, 128, 512).transpose(1, 0, 2)),
            "wvu_rows": np.ascontiguousarray(
                Wvu[sl, :].reshape(4, 128, 256).transpose(1, 0, 2)),
            "wq_cols": np.ascontiguousarray(Wq[:, sl]),
            "keys_rows": np.ascontiguousarray(keys[128 * c:128 * (c + 1), :]),
            "wr": Wr,
            "idt": idt, "e4c": e4c, "place": place, "ks16": ks16,
            "idtf": idtf,
        })
    return maps


def _l0_assemble(res):
    wT = np.concatenate([np.asarray(res[c]["o_wtm"]) for c in range(NC)], axis=0)
    w_l1 = np.ascontiguousarray(
        wT.reshape(32, 128, NSL, SLW).transpose(2, 1, 0, 3))
    vd = np.concatenate([np.asarray(res[c]["o_vd"]) for c in range(NC)], axis=0)
    vd_l1 = np.ascontiguousarray(vd.reshape(8, 128, 256).transpose(1, 0, 2))
    wu1T = np.concatenate([np.asarray(res[c]["o_wu1t"]) for c in range(NC)], axis=0)
    wu1t_l1 = np.ascontiguousarray(
        wu1T.reshape(32, 128, 8, 128).transpose(2, 1, 0, 3))
    wvuT = np.concatenate([np.asarray(res[c]["o_wvut"]) for c in range(NC)], axis=1)
    wvut_l1 = np.ascontiguousarray(wvuT.reshape(2, 128, 4096).transpose(1, 0, 2))
    wqT = np.concatenate([np.asarray(res[c]["o_wqt"]) for c in range(NC)], axis=0)
    wqt_l1 = np.ascontiguousarray(wqT.reshape(32, 128, 64).transpose(1, 0, 2))
    keysT = np.concatenate([np.asarray(res[c]["o_keyst"]) for c in range(NC)], axis=1)
    rkT = np.concatenate([np.asarray(res[c]["o_rk"]) for c in range(NC)], axis=1)
    wrT = np.asarray(res[0]["o_wrt"])
    return dict(w_l1=w_l1, vd_l1=vd_l1, wu1t_l1=wu1t_l1, wvut_l1=wvut_l1,
                wqt_l1=wqt_l1, keysT=keysT, rkT=rkT, wrT=wrT)


DBG_L1 = 0
NSL = 16
SLW = 256
NFFN = 10


def build_l1():
    nc = bass.Bass("TRN2", target_bir_lowering=False, debug=False)
    x_rows = nc.dram_tensor("x_rows", [8, 128, 4096], F32, kind="ExternalInput").ap()
    x_cols = nc.dram_tensor("x_cols", [NSL, 128, 8, SLW], F32, kind="ExternalInput").ap()
    w_l1 = nc.dram_tensor("w_l1", [NSL, 128, 32, SLW], BF16, kind="ExternalInput").ap()
    wu1t = nc.dram_tensor("wu1t", [8, 128, 32, 128], BF16, kind="ExternalInput").ap()
    i_vd = nc.dram_tensor("i_vd", [128, 8, 256], BF16, kind="ExternalInput").ap()
    i_wvut = nc.dram_tensor("i_wvut", [128, 2, 4096], BF16, kind="ExternalInput").ap()
    i_wqt = nc.dram_tensor("i_wqt", [128, 32, 64], BF16, kind="ExternalInput").ap()
    i_keyst = nc.dram_tensor("i_keyst", [64, 1024], BF16, kind="ExternalInput").ap()
    i_rkt = nc.dram_tensor("i_rkt", [48, 1024], BF16, kind="ExternalInput").ap()
    i_wrt = nc.dram_tensor("i_wrt", [64, 48], BF16, kind="ExternalInput").ap()
    i_logrel = nc.dram_tensor("i_logrel", [128, 1024], F32, kind="ExternalInput").ap()
    bias_r = nc.dram_tensor("bias_r", [1, 4096], F32, kind="ExternalInput").ap()
    i_bu1 = nc.dram_tensor("i_bu1", [128, 8], F32, kind="ExternalInput").ap()
    i_wu2t = nc.dram_tensor("i_wu2t", [128, 8], F32, kind="ExternalInput").ap()
    i_bu2 = nc.dram_tensor("i_bu2", [1, 1], F32, kind="ExternalInput").ap()
    idt = nc.dram_tensor("idt", [128, 128], BF16, kind="ExternalInput").ap()

    o_h = nc.dram_tensor("o_h", [8, 128, 4096], BF16, kind="ExternalOutput").ap()
    o_a2 = nc.dram_tensor("o_a2", [8, 128, 4096], BF16, kind="ExternalOutput").ap()
    o_cols = nc.dram_tensor("o_cols", [8, 128, 8], F32, kind="ExternalOutput").ap()
    o_lsum = nc.dram_tensor("o_lsum", [1, 1], F32, kind="ExternalOutput").ap()
    if DBG_L1 >= 1:
        o_dbg = nc.dram_tensor("o_dbg", [2, 128, 4096], BF16, kind="ExternalOutput").ap()
        o_dbg2 = nc.dram_tensor("o_dbg2", [128, 8192], BF16, kind="ExternalOutput").ap()

    with ExitStack() as ctx:
        e = ctx.enter_context
        y = [e(nc.sbuf_tensor(f"y{t}", [128, 4096], BF16)) for t in range(8)]
        xT = e(nc.sbuf_tensor("xT", [128, 32 * 1024], BF16))
        colb = e(nc.sbuf_tensor("colb", [128, 8 * 8], F32))
        s_idt = e(nc.sbuf_tensor("s_idt", [128, 128], BF16))
        ones1 = e(nc.sbuf_tensor("ones1", [1, 128], BF16))
        onesc = e(nc.sbuf_tensor("onesc", [128, 1], BF16))
        id1f = e(nc.sbuf_tensor("id1f", [1, 1], F32))
        mpart = e(nc.sbuf_tensor("mpart", [128, 8 * NSL], F32))
        sqc = e(nc.sbuf_tensor("sqc", [128, 8], F32))
        vbuf = e(nc.sbuf_tensor("vbuf", [128, 8], F32))
        rbuf = e(nc.sbuf_tensor("rbuf", [128, 8], F32))
        nbuf = e(nc.sbuf_tensor("nbuf", [128, 8], F32))
        mbuf = e(nc.sbuf_tensor("mbuf", [128, 8], F32))
        ccol = e(nc.sbuf_tensor("ccol", [128, 16], F32))
        lv8 = e(nc.sbuf_tensor("lv8", [128, 8], BF16))
        lsum_sb = e(nc.sbuf_tensor("lsum_sb", [1, 8], F32))
        cst = e(nc.sbuf_tensor("cst", [128, 2], F32))
        dmv = e(nc.sbuf_tensor("dmv", [1, 4], F32))
        dms = e(nc.sbuf_tensor("dms", [1, 4], F32))
        P = [e(nc.psum_tensor(f"P{i}", [128, 512], F32)) for i in range(8)]

        sem_names = ["ld", "lx0", "lx1", "l20", "l21", "lw0", "lw1", "lf", "lz0", "lz1", "tt", "xe", "tm", "ae",
                     "sq", "va", "rc", "nm", "ha", "lv", "tls", "vls",
                     "tq", "sq2", "tr", "sr", "tzc", "szb", "vg1", "st", "vg",
                     "tw2", "sl", "trA", "vA", "vtk", "sex", "vR", "tpT", "spT",
                     "ta1", "sa1", "ta1t", "sa1t", "ta2", "va2", "ya2", "vst",
                     "vlc", "vini", "so", "vch", "sch", "tl2", "vw2", "vlr"]
        S = {n: e(nc.semaphore(n)) for n in sem_names}
        block = e(nc.Block())

        with ExitStack() as actx:
            ea = actx.enter_context
            xload = ea(nc.sbuf_tensor("xload", [128, 2 * 4096], BF16))
            biasb = ea(nc.sbuf_tensor("biasb", [1, 4096], BF16))
            x2b = ea(nc.sbuf_tensor("x2b", [128, 2 * 8 * SLW], BF16))
            wbuf = ea(nc.sbuf_tensor("wbuf", [128, 2 * 32 * SLW], BF16))

            @block.gpsimd
            def _(g):
                g.dma_start(out=s_idt[:], in_=idt).then_inc(S["ld"], 16)
                g.dma_start(out=biasb[:], in_=bias_r).then_inc(S["ld"], 16)
                for t in range(8):
                    if t >= 2:
                        g.wait_ge(S["xe"], 8 * (t - 1))
                    g.dma_start(out=xload[:, 4096 * (t % 2):4096 * (t % 2 + 1)],
                                in_=x_rows[t]).then_inc(S[f"lx{t % 2}"], 16)
                for s in range(NSL):
                    if s >= 2:
                        g.wait_ge(S["ae"], 8 * (s - 1))
                    g.dma_start(
                        out=x2b[:, 8 * SLW * (s % 2):8 * SLW * (s % 2 + 1)]
                            .rearrange("p (a b) -> p a b", b=SLW),
                        in_=x_cols[s]).then_inc(S[f"l2{s % 2}"], 16)

            @block.sync
            def _(sy):
                for s in range(NSL):
                    if s >= 2:
                        sy.wait_ge(S["ae"], 8 * (s - 1))
                    sy.dma_start(
                        out=wbuf[:, 32 * SLW * (s % 2):32 * SLW * (s % 2 + 1)]
                            .rearrange("p (a b) -> p a b", b=SLW),
                        in_=w_l1[s]).then_inc(S[f"lw{s % 2}"], 16)

            @block.tensor
            def _(t_):
                t_.wait_ge(S["ld"], 32)
                t_.wait_ge(S["vini"], 1)
                for t in range(8):
                    t_.wait_ge(S[f"lx{t % 2}"], 16 * (t // 2 + 1))
                    for b in range(8):
                        gi = t * 8 + b
                        if gi >= 2:
                            t_.wait_ge(S["xe"], gi - 1)
                        for j in range(4):
                            kc = 4 * b + j
                            mm = t_.transpose(
                                out=P[gi % 2][:].bitcast(BF16)[:, 128 * j:128 * (j + 1)],
                                in_=xload[:, 4096 * (t % 2) + 128 * kc:
                                          4096 * (t % 2) + 128 * (kc + 1)],
                                identity=s_idt[:])
                        mm.then_inc(S["tt"], 1)
                for s in range(NSL):
                    t_.wait_ge(S[f"lw{s % 2}"], 16 * (s // 2 + 1))
                    if s == 0:
                        t_.wait_ge(S["xe"], 64)
                    for t in range(8):
                        if s >= 1:
                            t_.wait_ge(S["ae"], 8 * (s - 1) + t + 1)
                        t_.matmul(out=P[t][:, 0:SLW], lhsT=ones1[:],
                                  rhs=biasb[0:1, SLW * s:SLW * (s + 1)],
                                  start=True, stop=False)
                        for kc in range(32):
                            mm = t_.matmul(
                                out=P[t][:, 0:SLW],
                                lhsT=xT[:, 1024 * kc + 128 * t:1024 * kc + 128 * (t + 1)],
                                rhs=wbuf[:, 32 * SLW * (s % 2) + SLW * kc:
                                         32 * SLW * (s % 2) + SLW * (kc + 1)],
                                start=False, stop=(kc == 31))
                        mm.then_inc(S["tm"], 1)

            @block.vector
            def _(v):
                v.memset(ones1[:], 1.0)
                v.memset(onesc[:], 1.0)
                v.memset(dmv[:], 0.0)
                v.memset(colb[:], 0.0)
                v.memset(cst[:, 0:1], 1e-5)
                v.memset(cst[:, 1:2], 1.0)
                v.memset(id1f[:], 1.0).then_inc(S["vini"], 1)
                for t in range(8):
                    for b in range(8):
                        gi = t * 8 + b
                        v.wait_ge(S["tt"], gi + 1)
                        dst = AP(xT, 4096 * b + 128 * t,
                                 [[32768, 128], [1024, 4], [1, 128]])
                        v.tensor_copy(out=dst,
                                      in_=P[gi % 2][:].bitcast(BF16)[:, 0:512]
                                      .rearrange("p (a c) -> p a c", c=128)
                                      ).then_inc(S["xe"], 1)
                for s in range(NSL):
                    v.wait_ge(S[f"l2{s % 2}"], 16 * (s // 2 + 1))
                    for t in range(8):
                        v.wait_ge(S["tm"], 8 * s + t + 1)
                        v.scalar_tensor_tensor(
                            out=y[t][:, SLW * s:SLW * (s + 1)],
                            in0=P[t][:, 0:SLW], scalar=1.0,
                            in1=x2b[:, 8 * SLW * (s % 2) + SLW * t:
                                    8 * SLW * (s % 2) + SLW * (t + 1)],
                            op0=ALU.mult, op1=ALU.add,
                            accum_out=mpart[:, NSL * t + s:NSL * t + s + 1],
                        ).then_inc(S["ae"], 1)
        w1b = e(nc.sbuf_tensor("w1b", [128, 2 * 4096], BF16))
        wvutb = e(nc.sbuf_tensor("wvutb", [128, 2 * 4096], BF16))
        vdb = e(nc.sbuf_tensor("vdb", [128, 8 * 256], BF16))
        wqtb = e(nc.sbuf_tensor("wqtb", [128, 32 * 64], BF16))
        keystb = e(nc.sbuf_tensor("keystb", [64, 1024], BF16))
        rktb = e(nc.sbuf_tensor("rktb", [48, 1024], BF16))
        wrtb = e(nc.sbuf_tensor("wrtb", [64, 48], BF16))
        logrelb = e(nc.sbuf_tensor("logrelb", [128, 1024], BF16))
        bu1b = e(nc.sbuf_tensor("bu1b", [128, 8], F32))
        wu2tb = e(nc.sbuf_tensor("wu2tb", [128, 8], BF16))
        bu2b = e(nc.sbuf_tensor("bu2b", [1, 1], F32))
        qbuf = e(nc.sbuf_tensor("qbuf", [64, 1024], BF16))
        rqb = e(nc.sbuf_tensor("rqb", [48, 1024], BF16))
        zbt = e(nc.sbuf_tensor("zbt", [128, 1024], BF16))
        gb = e(nc.sbuf_tensor("gb", [128, 2 * 1024], BF16))
        gx1 = e(nc.sbuf_tensor("gx1", [128, 1024], BF16))
        gx2 = e(nc.sbuf_tensor("gx2", [128, 1024], BF16))
        learn8 = e(nc.sbuf_tensor("learn8", [128, 8], F32))
        Rbuf = e(nc.sbuf_tensor("Rbuf", [128, 1024], F32))
        Abuf = e(nc.sbuf_tensor("Abuf", [128, 1024], BF16))
        top8 = e(nc.sbuf_tensor("top8", [128, 8], F32))
        tauc = e(nc.sbuf_tensor("tauc", [128, 1], F32))
        nbig = e(nc.sbuf_tensor("nbig", [128, 1], BF16))
        zsc = e(nc.sbuf_tensor("zsc", [128, 1], F32))
        rzc = e(nc.sbuf_tensor("rzc", [128, 1], F32))
        nmx = e(nc.sbuf_tensor("nmx", [128, 1], F32))
        a1b = e(nc.sbuf_tensor("a1b", [128, 256], BF16))
        a1Tb = e(nc.sbuf_tensor("a1Tb", [128, 256], BF16))
        a2t = e(nc.sbuf_tensor("a2t", [128, 4096], BF16))
        apart = e(nc.sbuf_tensor("apart", [128, 8], F32))
        learnacc = e(nc.sbuf_tensor("learnacc", [1, 1024], BF16))

        @block.gpsimd
        def _(g):
            if DBG_L1 >= 1:
                return
            g.wait_ge(S["ae"], 128)
            g.dma_start(out=vdb[:].rearrange("p (a b) -> p a b", b=256),
                        in_=i_vd).then_inc(S["lf"], 16)
            g.dma_start(out=wvutb[:].rearrange("p (a b) -> p a b", b=4096),
                        in_=i_wvut).then_inc(S["lf"], 16)
            g.dma_start(out=wqtb[:].rearrange("p (a b) -> p a b", b=64),
                        in_=i_wqt).then_inc(S["lf"], 16)
            g.dma_start(out=keystb[:], in_=i_keyst).then_inc(S["lf"], 16)
            g.dma_start(out=rktb[:], in_=i_rkt).then_inc(S["lf"], 16)
            g.dma_start(out=wrtb[:], in_=i_wrt).then_inc(S["lf"], 16)
            g.dma_start(out=logrelb[:], in_=i_logrel).then_inc(S["lf"], 16)
            g.dma_start(out=bu1b[:], in_=i_bu1).then_inc(S["lf"], 16)
            g.dma_start(out=wu2tb[:], in_=i_wu2t).then_inc(S["lf"], 16)
            g.dma_start(out=bu2b[:], in_=i_bu2).then_inc(S["lf"], 16)
            for uc in range(8):
                if uc >= 2:
                    g.wait_ge(S["szb"], uc - 1)
                g.dma_start(out=w1b[:, 4096 * (uc % 2):4096 * (uc % 2 + 1)]
                            .rearrange("p (a b) -> p a b", b=128),
                            in_=wu1t[uc]).then_inc(S[f"lz{uc % 2}"], 16)

        @block.sync
        def _(sy):
            for t in range(8):
                sy.wait_ge(S["ha"], t + 1)
                sy.wait_ge(S["so"], 16 * t)
                sy.dma_start(out=o_h[t], in_=y[t][:]).then_inc(S["so"], 16)
            if DBG_L1 >= 1:
                sy.wait_ge(S["xe"], 64)
                sy.dma_start(out=o_dbg2, in_=xT[:, 0:8192]).then_inc(S["so"], 16)
                sy.wait_ge(S["sq"], 1)
                sy.dma_start(out=o_dbg[0], in_=y[0][:]).then_inc(S["so"], 16)
                sy.wait_ge(S["ha"], 8)
                sy.dma_start(out=o_dbg[1], in_=y[0][:]).then_inc(S["so"], 16)
                sy.wait_ge(S["so"], 16 * 11)
                return
            for t in range(8):
                sy.wait_ge(S["va2"], 8 * (t + 1))
                sy.dma_start(out=o_a2[t], in_=a2t[:]).then_inc(S["ya2"], 16)
            sy.wait_ge(S["vst"], 8)
            sy.wait_ge(S["vlc"], 1)
            sy.wait_ge(S["ha"], 8)
            sy.wait_ge(S["nm"], 8)
            sy.dma_start(out=o_cols.rearrange("t p c -> p t c"),
                         in_=colb[:].rearrange("p (t c) -> p t c", c=8)
                         ).then_inc(S["so"], 16)
            sy.wait_ge(S["vls"], 1)
            sy.dma_start(out=o_lsum, in_=lsum_sb[0:1, 0:1]).then_inc(S["so"], 16)
            sy.wait_ge(S["so"], 16 * 10)
            sy.wait_ge(S["ya2"], 16 * 8)

        @block.tensor
        def _(t_):
            if DBG_L1 >= 1:
                return
            for t in range(8):
                t_.wait_ge(S["ha"], t + 1)
                for b in range(8):
                    gi = t * 8 + b
                    if gi >= 2:
                        t_.wait_ge(S["xe"], 64 + gi - 1)
                    for j in range(4):
                        kc = 4 * b + j
                        mm = t_.transpose(
                            out=P[2 + gi % 2][:].bitcast(BF16)[:, 128 * j:128 * (j + 1)],
                            in_=y[t][:, 128 * kc:128 * (kc + 1)],
                            identity=s_idt[:])
                    mm.then_inc(S["tt"], 1)
            t_.wait_ge(S["lv"], 1)
            t_.matmul(out=P[0][0:1, 0:8], lhsT=onesc[:], rhs=lv8[:],
                      start=True, stop=True).then_inc(S["tls"], 1)
            t_.wait_ge(S["xe"], 128)
            t_.wait_ge(S["lf"], 16 * NFFN)
            for hf in range(2):
                for kc in range(32):
                    mm = t_.matmul(
                        out=P[4 + hf][0:64, 0:512],
                        lhsT=wqtb[:, 64 * kc:64 * (kc + 1)],
                        rhs=xT[:, 1024 * kc + 512 * hf:1024 * kc + 512 * (hf + 1)],
                        start=(kc == 0), stop=(kc == 31))
                mm.then_inc(S["tq"], 1)
            t_.wait_ge(S["sq2"], 1)
            for hf in range(2):
                t_.matmul(out=P[6 + hf][0:48, 0:512], lhsT=wrtb[0:64, 0:48],
                          rhs=qbuf[:, 512 * hf:512 * (hf + 1)],
                          start=True, stop=True).then_inc(S["tr"], 1)
            t_.wait_ge(S["vls"], 1)
            t_.wait_ge(S["sr"], 1)
            for t in range(8):
                t_.wait_ge(S["vA"], t)
                if t >= 2:
                    t_.wait_ge(S["vw2"], t - 1)
                for hf in range(2):
                    t_.matmul(out=P[4 + hf][:, 0:512],
                              lhsT=rqb[:, 128 * t:128 * (t + 1)],
                              rhs=rktb[:, 512 * hf:512 * (hf + 1)],
                              start=True, stop=True).then_inc(S["trA"], 1)
                if t >= 1:
                    t_.wait_ge(S["sa1t"], t)
                for hf in range(2):
                    t_.matmul(out=P[6 + hf][:, 0:512],
                              lhsT=qbuf[:, 128 * t:128 * (t + 1)],
                              rhs=keystb[:, 512 * hf:512 * (hf + 1)],
                              start=True, stop=True).then_inc(S["trA"], 1)
                t_.wait_ge(S[f"lz{t % 2}"], 16 * (t // 2 + 1))
                if t >= 1:
                    t_.wait_ge(S["szb"], t)
                for hf in range(2):
                    bank = 2 + hf
                    for kc in range(32):
                        mm = t_.matmul(
                            out=P[bank][:, 0:512],
                            lhsT=w1b[:, 4096 * (t % 2) + 128 * kc:
                                     4096 * (t % 2) + 128 * (kc + 1)],
                            rhs=xT[:, 1024 * kc + 512 * hf:1024 * kc + 512 * (hf + 1)],
                            start=(kc == 0), stop=(kc == 31))
                    mm.then_inc(S["tzc"], 1)
                if t >= 1:
                    guc = t - 1
                    t_.wait_ge(S["vg"], guc + 1)
                    t_.wait_ge(S["vA"], t + 1)
                    for hf in range(2):
                        bank = 4 + hf
                        mm = t_.matmul(out=P[bank][0:1, 0:512],
                                       lhsT=wu2tb[:, guc:guc + 1],
                                       rhs=gb[:, 1024 * (guc % 2) + 512 * hf:
                                              1024 * (guc % 2) + 512 * (hf + 1)],
                                       start=True, stop=True)
                    mm.then_inc(S["tl2"], 1)
                t_.wait_ge(S["vR"], t + 1)
                for j in range(8):
                    mm = t_.transpose(out=P[6][:].bitcast(BF16)[:, 128 * j:128 * (j + 1)],
                                      in_=Rbuf[:].bitcast(BF16)[:, 128 * j:128 * (j + 1)],
                                      identity=s_idt[:])
                mm.then_inc(S["tpT"], 1)
                t_.wait_ge(S["spT"], t + 1)
                for j in range(8):
                    mm = t_.matmul(out=P[7][:, 0:256],
                                   lhsT=Abuf[:, 128 * j:128 * (j + 1)],
                                   rhs=vdb[:, 256 * j:256 * (j + 1)],
                                   start=(j == 0), stop=(j == 7))
                mm.then_inc(S["ta1"], 1)
                t_.wait_ge(S["sa1"], t + 1)
                for v in range(2):
                    mm = t_.transpose(out=P[6][:].bitcast(BF16)[:, 128 * v:128 * (v + 1)],
                                      in_=a1b[:, 128 * v:128 * (v + 1)],
                                      identity=s_idt[:])
                mm.then_inc(S["ta1t"], 1)
                t_.wait_ge(S["sa1t"], t + 1)
                for hs in range(8):
                    if hs >= 2:
                        t_.wait_ge(S["va2"], 8 * t + hs - 1)
                    for v in range(2):
                        mm = t_.matmul(out=P[hs % 2][:, 0:512],
                                       lhsT=a1Tb[:, 128 * v:128 * (v + 1)],
                                       rhs=wvutb[:, 4096 * v + 512 * hs:
                                                 4096 * v + 512 * (hs + 1)],
                                       start=(v == 0), stop=(v == 1))
                    mm.then_inc(S["ta2"], 1)
            t_.wait_ge(S["vg"], 8)
            t_.wait_ge(S["vA"], 8)
            for hf in range(2):
                bank = 4 + hf
                mm = t_.matmul(out=P[bank][0:1, 0:512], lhsT=wu2tb[:, 7:8],
                               rhs=gb[:, 1024 + 512 * hf:1024 + 512 * (hf + 1)],
                               start=True, stop=True)
            mm.then_inc(S["tl2"], 1)
            t_.wait_ge(S["vlr"], 1)
            for j in range(8):
                mm = t_.transpose(out=P[4][:, j:j + 1],
                                  in_=Rbuf[0:1, 128 * j:128 * (j + 1)],
                                  identity=id1f[:])
            mm.then_inc(S["tls"], 1)
        @block.vector
        def _(v):
            vch = _Chain(v, S["vch"])
            v.memset(nbig[:], NEGBIG)
            for t in range(8):
                v.wait_ge(S["sq"], t + 1)
                vch.step(v.reduce_sum(out=ccol[:, 0:1],
                                      in_=mpart[:, NSL * t:NSL * (t + 1)],
                                      axis=AX.X))
                v.tensor_scalar(out=ccol[:, 1:2], in0=sqc[:, t:t + 1],
                                scalar1=1.0 / 4096, scalar2=None, op0=ALU.mult)
                vch.step(v.tensor_scalar(out=mbuf[:, t:t + 1], in0=ccol[:, 0:1],
                                         scalar1=1.0 / 4096, scalar2=None,
                                         op0=ALU.mult))
                vch.step(v.tensor_tensor(out=ccol[:, 2:3], in0=mbuf[:, t:t + 1],
                                         in1=mbuf[:, t:t + 1], op=ALU.mult))
                v.tensor_sub(out=vbuf[:, t:t + 1], in0=ccol[:, 1:2],
                             in1=ccol[:, 2:3]).then_inc(S["va"], 1)
            for t in range(8):
                v.wait_ge(S["rc"], t + 1)
                if t >= 1:
                    v.wait_ge(S["nm"], t)
                vch.step(v.tensor_tensor(out=ccol[:, 3:4], in0=rbuf[:, t:t + 1],
                                         in1=rbuf[:, t:t + 1], op=ALU.mult))
                v.scalar_tensor_tensor(out=nbuf[:, t:t + 1], in0=mbuf[:, t:t + 1],
                                       scalar=-1.0, in1=rbuf[:, t:t + 1],
                                       op0=ALU.mult, op1=ALU.mult)
                v.tensor_tensor(out=colb[:, 8 * t + 6:8 * t + 7],
                                in0=vbuf[:, t:t + 1], in1=ccol[:, 3:4],
                                op=ALU.mult).then_inc(S["nm"], 1)
            for t in range(8):
                v.wait_ge(S["ha"], t + 1)
                cp = v.tensor_copy(out=lv8[:, t:t + 1], in_=colb[:, 8 * t:8 * t + 1])
            cp.then_inc(S["lv"], 1)
            if DBG_L1 >= 1:
                return
            for t in range(8):
                for b in range(8):
                    gi = t * 8 + b
                    v.wait_ge(S["tt"], 64 + gi + 1)
                    dst = AP(xT, 4096 * b + 128 * t,
                             [[32768, 128], [1024, 4], [1, 128]])
                    v.tensor_copy(out=dst,
                                  in_=P[2 + gi % 2][:].bitcast(BF16)[:, 0:512]
                                  .rearrange("p (a c) -> p a c", c=128)
                                  ).then_inc(S["xe"], 1)
            v.wait_ge(S["tls"], 1)
            v.reduce_sum(out=lsum_sb[0:1, 0:1], in_=P[0][0:1, 0:8],
                         axis=AX.X).then_inc(S["vls"], 1)
            v.memset(learnacc[0:1, :], 0.0)
            v.wait_ge(S["lf"], 16 * NFFN)

            def v_tail(tt):
                v.wait_ge(S["va2"], 8 * (tt + 1))
                v.wait_ge(S["ha"], tt + 1)
                v.wait_ge(S["so"], 16 * (tt + 1))
                vch.step(v.reduce_sum(out=colb[:, 8 * tt + 2:8 * tt + 3],
                                      in_=apart[:], axis=AX.X))
                vch.step(v.scalar_tensor_tensor(
                    out=y[tt][:], in0=a2t[:], scalar=1.0,
                    in1=y[tt][:], op0=ALU.mult, op1=ALU.mult,
                    accum_out=colb[:, 8 * tt + 3:8 * tt + 4]))
                v.scalar_tensor_tensor(out=y[tt][:],
                                       in0=a2t[:], scalar=1.0,
                                       in1=a2t[:], op0=ALU.mult, op1=ALU.mult,
                                       accum_out=colb[:, 8 * tt + 4:8 * tt + 5]
                                       ).then_inc(S["vst"], 1)
                v.wait_ge(S["szb"], tt + 1)
                if tt >= 2:
                    v.wait_ge(S["tl2"], tt - 1)
                vch.step(v.tensor_tensor(out=gx1[:], in0=zbt[:], in1=zbt[:],
                                         op=ALU.mult))
                vch.step(v.tensor_tensor(out=gx2[:], in0=gx1[:], in1=zbt[:],
                                         op=ALU.mult))
                v.scalar_tensor_tensor(out=gx1[:], in0=gx2[:], scalar=0.044715,
                                       in1=zbt[:], op0=ALU.mult,
                                       op1=ALU.add).then_inc(S["vg1"], 1)
                v.wait_ge(S["st"], tt + 1)
                vch.step(v.tensor_scalar(out=gx1[:], in0=gx2[:], scalar1=1.0,
                                         scalar2=None, op0=ALU.add))
                vch.step(v.tensor_scalar(out=gx2[:], in0=zbt[:], scalar1=0.5,
                                         scalar2=None, op0=ALU.mult))
                v.tensor_tensor(out=gb[:, 1024 * (tt % 2):1024 * (tt % 2 + 1)],
                                in0=gx1[:], in1=gx2[:],
                                op=ALU.mult).then_inc(S["vg"], 1)

            def v_accum(guc):
                v.wait_ge(S["tl2"], guc + 1)
                for hf in range(2):
                    bank = 4 + hf
                    acc_mm = v.tensor_tensor(
                        out=learnacc[0:1, 512 * hf:512 * (hf + 1)],
                        in0=learnacc[0:1, 512 * hf:512 * (hf + 1)],
                        in1=P[bank][0:1, 0:512], op=ALU.add)
                acc_mm.then_inc(S["vw2"], 1)

            for t in range(8):
                for hf in range(2):
                    v.wait_ge(S["trA"], 4 * t + hf + 1)
                    vch.step(v.tensor_tensor(out=Rbuf[:, 512 * hf:512 * (hf + 1)],
                                    in0=P[4 + hf][:, 0:512],
                                    in1=logrelb[:, 512 * hf:512 * (hf + 1)],
                                    op=ALU.add))
                for hf in range(2):
                    v.wait_ge(S["trA"], 4 * t + 2 + hf + 1)
                    if hf == 0:
                        vch.step(v.tensor_tensor(out=Abuf[:, 0:512],
                                        in0=P[6][:, 0:512],
                                        in1=logrelb[:, 0:512], op=ALU.add))
                    else:
                        v.tensor_tensor(out=Abuf[:, 512:1024],
                                        in0=P[7][:, 0:512],
                                        in1=logrelb[:, 512:1024],
                                        op=ALU.add).then_inc(S["vA"], 1)
                v.wait_ge(S["vA"], t + 1)
                vch.step(v.max(out=top8[:], in_=Rbuf[:]))
                vch.step(v.tensor_copy(out=tauc[:], in_=top8[:, 7:8]))
                vch.step(v.scalar_tensor_tensor(
                    out=gb[:, 1024:2048], in0=Rbuf[:], scalar=tauc[:, 0:1],
                    in1=nbig[:, 0:1].to_broadcast([128, 1024]),
                    op0=ALU.is_lt, op1=ALU.mult))
                vch.step(v.tensor_tensor(out=gb[:, 0:1024], in0=Abuf[:],
                                         in1=gb[:, 1024:2048], op=ALU.add))
                v.reduce_max(out=nmx[:], in_=gb[:, 0:1024], axis=AX.X,
                             negate=True).then_inc(S["vtk"], 1)
                v.wait_ge(S["sex"], t + 1)
                vch.step(v.reciprocal(out=rzc[:], in_=zsc[:]))
                v.scalar_tensor_tensor(out=Rbuf[:].bitcast(BF16)[:, 0:1024],
                                       in0=gb[:, 1024:2048], scalar=rzc[:, 0:1],
                                       in1=gb[:, 1024:2048], op0=ALU.mult,
                                       op1=ALU.bypass).then_inc(S["vR"], 1)
                if t >= 1:
                    v_tail(t - 1)
                    v_accum(t - 1)
            v_tail(7)
            v_accum(7)
            v.wait_ge(S["vw2"], 8)
            vch.step(v.tensor_copy(out=dmv[:, 0:1], in_=dmv[:, 1:2]))
            v.scalar_tensor_tensor(out=Rbuf[0:1, 0:1024], in0=learnacc[0:1, :],
                                   scalar=bu2b[0:1, 0:1], in1=learnacc[0:1, :],
                                   op0=ALU.add, op1=ALU.bypass).then_inc(S["vlr"], 1)
            v.wait_ge(S["sl"], 2)
            for t in range(8):
                cp = v.tensor_copy(out=colb[:, 8 * t + 1:8 * t + 2],
                                   in_=learn8[:, t:t + 1])
            cp.then_inc(S["vlc"], 1)
        @block.scalar
        def _(s_):
            sch = _Chain(s_, S["sch"])
            for t in range(8):
                s_.wait_ge(S["ae"], 128)
                if t >= 1:
                    s_.wait_ge(S["sq"], t)
                dump = a2t[:] if t % 2 == 0 else xT[:, 0:4096].bitcast(BF16)[:, 0:4096]
                s_.activation(out=dump, in_=y[t][:], func=AF.Square,
                              accum_out=sqc[:, t:t + 1]).then_inc(S["sq"], 1)
            for t in range(8):
                s_.wait_ge(S["va"], t + 1)
                if t >= 1:
                    s_.wait_ge(S["rc"], t)
                sch.step(s_.activation(out=ccol[:, 4:5], in_=vbuf[:, t:t + 1],
                                       func=AF.Ln, bias=cst[:, 0:1], scale=1.0))
                s_.activation(out=rbuf[:, t:t + 1], in_=ccol[:, 4:5], func=AF.Exp,
                              scale=-0.5).then_inc(S["rc"], 1)
            for t in range(8):
                s_.wait_ge(S["nm"], t + 1)
                if t >= 1:
                    s_.wait_ge(S["ha"], t)
                s_.activation(out=colb[:, 8 * t:8 * t + 1],
                              in_=colb[:, 8 * t + 6:8 * t + 7], func=AF.Ln,
                              bias=cst[:, 1:2], scale=1.0)
                apply_out = y[t][:] if DBG_L1 < 2 else a2t[:]
                sch.step(s_.activation(out=apply_out, in_=y[t][:], func=AF.Identity,
                              scale=rbuf[:, t:t + 1], bias=nbuf[:, t:t + 1],
                              accum_out=ccol[:, 5:6]))
                s_.copy(out=colb[:, 8 * t + 5:8 * t + 6],
                        in_=ccol[:, 5:6]).then_inc(S["ha"], 1)
            if DBG_L1 >= 1:
                return
            s_.wait_ge(S["tq"], 2)
            s_.copy(out=qbuf[:, 0:512], in_=P[4][0:64, 0:512])
            s_.copy(out=qbuf[:, 512:1024], in_=P[5][0:64, 0:512]).then_inc(S["sq2"], 1)
            s_.wait_ge(S["tr"], 2)
            s_.copy(out=rqb[:, 0:512], in_=P[6][0:48, 0:512])
            s_.copy(out=rqb[:, 512:1024], in_=P[7][0:48, 0:512]).then_inc(S["sr"], 1)
            s_.wait_ge(S["lf"], 16 * NFFN)
            for t in range(8):
                s_.wait_ge(S["vtk"], t + 1)
                s_.activation(out=gb[:, 1024:2048], in_=gb[:, 0:1024], func=AF.Exp,
                              bias=nmx[:, 0:1], scale=1.0,
                              accum_out=zsc[:]).then_inc(S["sex"], 1)
                if t >= 1:
                    s_.wait_ge(S["vg1"], t)
                    s_.activation(out=gx2[:], in_=gx1[:], func=AF.Tanh,
                                  scale=0.7978845608028654).then_inc(S["st"], 1)
                s_.wait_ge(S["tzc"], 2 * t + 2)
                if t >= 1:
                    s_.wait_ge(S["vg"], t)
                s_.activation(out=zbt[:, 0:512], in_=P[2][:, 0:512],
                              func=AF.Identity, bias=bu1b[:, t:t + 1], scale=1.0)
                s_.activation(out=zbt[:, 512:1024], in_=P[3][:, 0:512],
                              func=AF.Identity, bias=bu1b[:, t:t + 1],
                              scale=1.0).then_inc(S["szb"], 1)
                s_.wait_ge(S["tpT"], t + 1)
                s_.copy(out=Abuf[:], in_=P[6][:].bitcast(BF16)[:, 0:1024]
                        ).then_inc(S["spT"], 1)
                s_.wait_ge(S["ta1"], t + 1)
                s_.copy(out=a1b[:], in_=P[7][:, 0:256]).then_inc(S["sa1"], 1)
                s_.wait_ge(S["ta1t"], t + 1)
                s_.copy(out=a1Tb[:], in_=P[6][:].bitcast(BF16)[:, 0:256]
                        ).then_inc(S["sa1t"], 1)
                for hs in range(8):
                    s_.wait_ge(S["ta2"], 8 * t + hs + 1)
                    if hs == 0 and t >= 1:
                        s_.wait_ge(S["ya2"], 16 * t)
                        s_.wait_ge(S["vst"], t)
                    s_.activation(out=a2t[:, 512 * hs:512 * (hs + 1)],
                                  in_=P[hs % 2][:, 0:512], func=AF.Identity,
                                  accum_out=apart[:, hs:hs + 1]
                                  ).then_inc(S["va2"], 1)
            s_.wait_ge(S["vg1"], 8)
            s_.activation(out=gx2[:], in_=gx1[:], func=AF.Tanh,
                          scale=0.7978845608028654).then_inc(S["st"], 1)
            s_.wait_ge(S["tls"], 2)
            s_.copy(out=learn8[:], in_=P[4][:, 0:8]).then_inc(S["sl"], 2)
    return nc


def build_l2():
    nc = bass.Bass("TRN2", target_bir_lowering=False, debug=False)
    i_h = nc.dram_tensor("i_h", [8, 128, 4096], BF16, kind="ExternalInput").ap()
    i_a2 = nc.dram_tensor("i_a2", [8, 128, 4096], BF16, kind="ExternalInput").ap()
    i_cols = nc.dram_tensor("i_cols", [128, 8, 8], F32, kind="ExternalInput").ap()
    i_sums = nc.dram_tensor("i_sums", [1, 8], F32, kind="ExternalInput").ap()
    i_gw = nc.dram_tensor("i_gw", [1, 1], F32, kind="ExternalInput").ap()
    i_gb = nc.dram_tensor("i_gb", [1, 1], F32, kind="ExternalInput").ap()
    o_out = nc.dram_tensor("o_out", [8, 128, 4096], F32, kind="ExternalOutput").ap()

    with ExitStack() as ctx:
        e = ctx.enter_context
        hb = e(nc.sbuf_tensor("hb", [128, 2 * 4096], BF16))
        ab = e(nc.sbuf_tensor("ab", [128, 2 * 4096], BF16))
        tmpb = e(nc.sbuf_tensor("tmpb", [128, 4096], BF16))
        outb = e(nc.sbuf_tensor("outb", [128, 2 * 4096], F32))
        colsb = e(nc.sbuf_tensor("colsb", [128, 64], F32))
        sumsb = e(nc.sbuf_tensor("sumsb", [1, 8], F32))
        g3 = e(nc.sbuf_tensor("g3", [1, 4], F32))
        onesf = e(nc.sbuf_tensor("onesf", [1, 128], F32))
        bc3 = e(nc.sbuf_tensor("bc3", [128, 4], F32))
        cc = e(nc.sbuf_tensor("cc", [128, 16], F32))
        gcol = e(nc.sbuf_tensor("gcol", [128, 8], F32))
        cst2 = e(nc.sbuf_tensor("cst2", [128, 1], F32))
        dmv = e(nc.sbuf_tensor("dmv", [1, 4], F32))
        dms = e(nc.sbuf_tensor("dms", [1, 4], F32))
        rcol = e(nc.sbuf_tensor("rcol", [128, 8], F32))
        ncol = e(nc.sbuf_tensor("ncol", [128, 8], F32))
        P = [e(nc.psum_tensor(f"P{i}", [128, 512], F32)) for i in range(2)]

        sem_names = ["ld", "lh0", "lh1", "la0", "la1", "vb", "sb", "tb", "ve", "se", "so0", "so1", "sos0", "sos1", "os", "vch", "sch"]
        S = {n: e(nc.semaphore(n)) for n in sem_names}
        block = e(nc.Block())

        @block.gpsimd
        def _(g):
            g.dma_start(out=colsb[:].rearrange("p (t c) -> p t c", c=8),
                        in_=i_cols).then_inc(S["ld"], 16)
            g.dma_start(out=sumsb[:], in_=i_sums).then_inc(S["ld"], 16)
            g.dma_start(out=g3[0:1, 1:2], in_=i_gw).then_inc(S["ld"], 16)
            g.dma_start(out=g3[0:1, 2:3], in_=i_gb).then_inc(S["ld"], 16)
            for t in range(8):
                if t >= 2:
                    g.wait_ge(S["ve"], 3 * (t - 2) + 3)
                g.dma_start(out=ab[:, 4096 * (t % 2):4096 * (t % 2 + 1)],
                            in_=i_a2[t]).then_inc(S[f"la{t % 2}"], 16)
            g.wait_ge(S["se"], 8)
            g.dma_start(out=o_out[7],
                        in_=outb[:, 4096:8192]).then_inc(S["so1"], 16)
            g.wait_ge(S["so0"], 16)
            g.wait_ge(S["so1"], 16)
            g.wait_ge(S["sos0"], 16 * 3)
            g.wait_ge(S["sos1"], 16 * 3)

        @block.sync
        def _(sy):
            for t in range(8):
                if t >= 2:
                    sy.wait_ge(S["ve"], 3 * (t - 2) + 3)
                sy.dma_start(out=hb[:, 4096 * (t % 2):4096 * (t % 2 + 1)],
                             in_=i_h[t]).then_inc(S[f"lh{t % 2}"], 16)
            sy.wait_ge(S["se"], 7)
            sy.dma_start(out=o_out[6],
                         in_=outb[:, 0:4096]).then_inc(S["so0"], 16)

        @block.vector
        def _(v):
            vch = _Chain(v, S["vch"])
            v.memset(onesf[:], 1.0)
            v.memset(dmv[:], 0.0)
            v.memset(g3[0:1, 3:4], 0.0)
            v.memset(cst2[:], 1e-5)
            v.wait_ge(S["ld"], 64)
            vch.step(v.reduce_sum(out=cc[0:1, 0:1], in_=sumsb[:], axis=AX.X))
            vch.step(v.tensor_scalar(out=cc[0:1, 1:2], in0=cc[0:1, 0:1],
                                     scalar1=1.0 / 8192, scalar2=1e-6,
                                     op0=ALU.mult, op1=ALU.add))
            v.reciprocal(out=g3[0:1, 0:1], in_=cc[0:1, 1:2]).then_inc(S["vb"], 1)
            for t in range(8):
                v.wait_ge(S["sb"], 1)
                c0 = colsb[:, 8 * t:8 * t + 1]
                c2 = colsb[:, 8 * t + 2:8 * t + 3]
                c3 = colsb[:, 8 * t + 3:8 * t + 4]
                c4 = colsb[:, 8 * t + 4:8 * t + 5]
                c5 = colsb[:, 8 * t + 5:8 * t + 6]
                c6 = colsb[:, 8 * t + 6:8 * t + 7]
                vch.step(v.scalar_tensor_tensor(out=cc[:, 0:1], in0=c0,
                                       scalar=bc3[:, 0:1], in1=c0,
                                       op0=ALU.mult, op1=ALU.bypass))
                v.wait_ge(S["sb"], 3 * t + 2)
                vch.step(v.tensor_scalar(out=cc[:, 2:3], in0=cc[:, 1:2], scalar1=2.5,
                                scalar2=None, op0=ALU.mult))
                vch.step(v.scalar_tensor_tensor(out=cc[:, 3:4], in0=cc[:, 0:1], scalar=0.5,
                                       in1=cc[:, 2:3], op0=ALU.mult, op1=ALU.add))
                vch.step(v.tensor_scalar(out=cc[:, 4:5], in0=cc[:, 3:4], scalar1=0.0,
                                scalar2=5.0, op0=ALU.max, op1=ALU.min))
                vch.step(v.tensor_scalar(out=cc[:, 5:6], in0=cc[:, 4:5], scalar1=-0.5,
                                scalar2=1.0 / 1.5, op0=ALU.add, op1=ALU.mult))
                v.tensor_scalar(out=cc[:, 6:7], in0=cc[:, 5:6], scalar1=0.0,
                                scalar2=1.0, op0=ALU.max,
                                op1=ALU.min).then_inc(S["ve"], 1)
                v.wait_ge(S["sb"], 3 * t + 3)
                vch.step(v.tensor_scalar(out=cc[:, 8:9], in0=cc[:, 7:8], scalar1=0.05,
                                scalar2=None, op0=ALU.is_ge))
                vch.step(v.tensor_tensor(out=gcol[:, t:t + 1], in0=cc[:, 7:8],
                                in1=cc[:, 8:9], op=ALU.mult))
                vch.step(v.scalar_tensor_tensor(out=cc[:, 9:10], in0=c2,
                                       scalar=gcol[:, t:t + 1], in1=c5,
                                       op0=ALU.mult, op1=ALU.add))
                vch.step(v.tensor_tensor(out=cc[:, 11:12], in0=gcol[:, t:t + 1],
                                in1=gcol[:, t:t + 1], op=ALU.mult))
                vch.step(v.tensor_scalar(out=cc[:, 10:11], in0=cc[:, 9:10],
                                scalar1=1.0 / 4096, scalar2=None, op0=ALU.mult))
                vch.step(v.scalar_tensor_tensor(out=cc[:, 13:14], in0=c3,
                                       scalar=gcol[:, t:t + 1], in1=c3,
                                       op0=ALU.mult, op1=ALU.bypass))
                vch.step(v.scalar_tensor_tensor(out=cc[:, 12:13], in0=c4,
                                       scalar=cc[:, 11:12], in1=c4,
                                       op0=ALU.mult, op1=ALU.bypass))
                vch.step(v.scalar_tensor_tensor(out=cc[:, 14:15], in0=cc[:, 13:14],
                                       scalar=2.0, in1=cc[:, 12:13],
                                       op0=ALU.mult, op1=ALU.add))
                vch.step(v.scalar_tensor_tensor(out=cc[:, 15:16], in0=c6,
                                       scalar=4096.0, in1=cc[:, 14:15],
                                       op0=ALU.mult, op1=ALU.add))
                vch.step(v.tensor_tensor(out=cc[:, 13:14], in0=cc[:, 10:11],
                                in1=cc[:, 10:11], op=ALU.mult))
                vch.step(v.tensor_scalar(out=cc[:, 12:13], in0=cc[:, 15:16],
                                scalar1=1.0 / 4096, scalar2=None, op0=ALU.mult))
                v.tensor_sub(out=cc[:, 14:15], in0=cc[:, 12:13],
                             in1=cc[:, 13:14]).then_inc(S["ve"], 1)
                v.wait_ge(S["sb"], 3 * t + 4)
                vch.step(v.scalar_tensor_tensor(out=ncol[:, t:t + 1], in0=cc[:, 10:11],
                                       scalar=-1.0, in1=rcol[:, t:t + 1],
                                       op0=ALU.mult, op1=ALU.mult))
                v.wait_ge(S[f"lh{t % 2}"], 16 * (t // 2 + 1))
                v.wait_ge(S[f"la{t % 2}"], 16 * (t // 2 + 1))
                if t >= 1:
                    v.wait_ge(S["se"], t)
                v.scalar_tensor_tensor(
                    out=tmpb[:], in0=ab[:, 4096 * (t % 2):4096 * (t % 2 + 1)],
                    scalar=gcol[:, t:t + 1],
                    in1=hb[:, 4096 * (t % 2):4096 * (t % 2 + 1)],
                    op0=ALU.mult, op1=ALU.add).then_inc(S["ve"], 1)

        @block.tensor
        def _(t_):
            t_.wait_ge(S["vb"], 1)
            t_.matmul(out=P[0][:, 0:4], lhsT=onesf[:], rhs=g3[0:1, 0:4],
                      start=True, stop=True).then_inc(S["tb"], 1)

        @block.scalar
        def _(s_):
            sch = _Chain(s_, S["sch"])
            s_.wait_ge(S["tb"], 1)
            s_.copy(out=bc3[:], in_=P[0][:, 0:4]).then_inc(S["sb"], 1)
            for t in range(8):
                s_.wait_ge(S["ld"], 64)
                s_.activation(out=cc[:, 1:2], in_=colsb[:, 8 * t + 1:8 * t + 2],
                              func=AF.Sigmoid).then_inc(S["sb"], 1)
                s_.wait_ge(S["ve"], 3 * t + 1)
                s_.activation(out=cc[:, 7:8], in_=cc[:, 6:7], func=AF.Sigmoid,
                              scale=bc3[:, 1:2], bias=bc3[:, 2:3]
                              ).then_inc(S["sb"], 1)
                s_.wait_ge(S["ve"], 3 * t + 2)
                sch.step(s_.activation(out=cc[:, 15:16], in_=cc[:, 14:15],
                                       func=AF.Ln, bias=cst2[:, 0:1], scale=1.0))
                s_.activation(out=rcol[:, t:t + 1], in_=cc[:, 15:16], func=AF.Exp,
                              scale=-0.5).then_inc(S["sb"], 1)
                s_.wait_ge(S["ve"], 3 * t + 3)
                if t >= 2:
                    s_.wait_ge(S[f"sos{t % 2}"], 16 * ((t - 2) // 2 + 1))
                s_.activation(out=outb[:, 4096 * (t % 2):4096 * (t % 2 + 1)],
                              in_=tmpb[:], func=AF.Identity,
                              scale=rcol[:, t:t + 1], bias=ncol[:, t:t + 1]
                              ).then_inc(S["se"], 1)
                if t <= 5:
                    s_.wait_ge(S["se"], t + 1)
                    s_.dma_start(out=o_out[t],
                                 in_=outb[:, 4096 * (t % 2):4096 * (t % 2 + 1)]
                                 ).then_inc(S[f"sos{t % 2}"], 16)
    return nc


def _l1_in_maps(inputs, asm, consts):
    idt = consts[0]
    x = np.asarray(inputs["L0_attn_data"], dtype=np.float32).reshape(8192, 4096)
    bias = np.asarray(inputs["L0_attn_bias"], dtype=np.float32)
    log_rel = np.asarray(inputs["L0_ffn_log_rel"], dtype=np.float32)
    bu1 = np.asarray(inputs["L0_ffn_bu1"], dtype=np.float32)
    wu2 = np.asarray(inputs["L0_ffn_Wu2"], dtype=np.float32)
    bu2 = np.asarray(inputs["L0_ffn_bu2"], dtype=np.float32)
    logrel_exp = np.ascontiguousarray(
        np.broadcast_to(log_rel[None, :], (128, 1024)))
    shared = {
        "w_l1": asm["w_l1"], "wu1t": asm["wu1t_l1"], "i_vd": asm["vd_l1"],
        "i_wvut": asm["wvut_l1"], "i_wqt": asm["wqt_l1"],
        "i_keyst": np.ascontiguousarray(asm["keysT"]),
        "i_rkt": np.ascontiguousarray(asm["rkT"]),
        "i_wrt": np.ascontiguousarray(asm["wrT"]),
        "i_logrel": logrel_exp,
        "bias_r": np.ascontiguousarray(bias[None, :]),
        "i_bu1": np.ascontiguousarray(bu1.reshape(8, 128).T),
        "i_wu2t": np.ascontiguousarray(wu2.reshape(8, 128).T),
        "i_bu2": bu2.reshape(1, 1),
        "idt": idt,
    }
    maps = []
    for c in range(NC):
        xs = x[T * c:T * (c + 1)]
        m = dict(shared)
        m["x_rows"] = np.ascontiguousarray(xs.reshape(8, 128, 4096))
        m["x_cols"] = np.ascontiguousarray(
            xs.reshape(8, 128, NSL, SLW).transpose(2, 1, 0, 3))
        maps.append(m)
    return maps


def _l2_in_maps(inputs, res1):
    gw = np.asarray(inputs["L0_ffn_gate_w1"], dtype=np.float32).reshape(1, 1)
    gb = np.asarray(inputs["L0_ffn_gate_bias"], dtype=np.float32).reshape(1, 1)
    sums = np.ascontiguousarray(
        np.concatenate([np.asarray(res1[c]["o_lsum"]).reshape(1, 1)
                        for c in range(NC)], axis=1))
    maps = []
    for c in range(NC):
        cols = np.asarray(res1[c]["o_cols"]).reshape(8, 128, 8)
        maps.append({
            "i_h": np.asarray(res1[c]["o_h"]).reshape(8, 128, 4096),
            "i_a2": np.asarray(res1[c]["o_a2"]).reshape(8, 128, 4096),
            "i_cols": np.ascontiguousarray(cols.transpose(1, 0, 2)),
            "i_sums": sums,
            "i_gw": gw, "i_gb": gb,
        })
    return maps


def kernel(**inputs):
    consts = _consts()
    nc0 = build_l0()
    res0 = run_bass_kernel_spmd(nc0, _l0_in_maps(inputs, consts),
                                core_ids=list(range(NC)))
    asm = _l0_assemble(res0.results)
    nc1 = build_l1()
    res1 = run_bass_kernel_spmd(nc1, _l1_in_maps(inputs, asm, consts),
                                core_ids=list(range(NC)))
    nc2 = build_l2()
    res2 = run_bass_kernel_spmd(nc2, _l2_in_maps(inputs, res1.results),
                                core_ids=list(range(NC)))
    out = np.concatenate(
        [np.asarray(res2.results[c]["o_out"]).reshape(T, D) for c in range(NC)],
        axis=0)
    return np.ascontiguousarray(out.reshape(2, 4096, 4096), dtype=np.float32)
